# version 06
# baseline (speedup 1.0000x reference)
import math
import numpy as np
import ml_dtypes
import bass_rust
import concourse.bass as bass
import concourse.mybir as mybir
from concourse.bass_utils import run_bass_kernel_spmd
from contextlib import ExitStack

F32 = mybir.dt.float32
BF16 = mybir.dt.bfloat16
I16 = mybir.dt.int16
U8 = mybir.dt.uint8
AF = mybir.ActivationFunctionType
ALU = mybir.AluOpType
AX = mybir.AxisListType

BC, S, D = 2, 2048, 1024
NH, DH, FF, NE, IFF = 16, 64, 4096, 8, 2816
RS = 2176
NEG = -1.0e30
NST = S // 128

DEBUG_DUMPS = ()
DEBUG_STOP = None
DEBUG_L1_HEADS = NH
DEBUG_L1_MODE = 9
SERIAL_READS = False

SB_GRAN = 512
DR_GRAN = 1 << 18
GRAN_MAP = {"SM": 16, "SM3": 16}
_PSUM_NAMES = {f"ps{i}" for i in range(8)} | {"pb0", "pb1"}


def _keys(ap):
    t = ap.tensor
    name = t.name
    a = ap.ap
    if str(ap.space) == "MemorySpace.PSUM":
        return (name,)
    if str(ap.space) == "MemorySpace.DRAM":
        off = ap.offset
        span = 1
        for st, cnt in a:
            span += abs(st) * (cnt - 1)
        lo = off // DR_GRAN
        hi = (off + span - 1) // DR_GRAN
        return tuple(f"{name}:{i}" for i in range(lo, hi + 1))
    pitch = a[0][0] if len(a) > 0 else 1
    off = ap.offset % pitch if pitch > 0 else ap.offset
    span = 1
    for st, cnt in a[1:]:
        span += abs(st) * (cnt - 1)
    g = GRAN_MAP.get(name, SB_GRAN)
    lo = off // g
    hi = (off + span - 1) // g
    return tuple(f"{name}:{i}" for i in range(lo, hi + 1))


class _Op:
    __slots__ = ("eng", "fn", "dma", "idx", "deps", "signals", "cnt", "tag",
                 "slot")


class Sched:
    def __init__(self):
        self.ops = []
        self.res = {}

    def add(self, eng, fn, ins=(), outs=(), dma=False, tag=""):
        op = _Op()
        op.eng = eng; op.fn = fn; op.dma = dma; op.tag = tag
        op.idx = len(self.ops); op.signals = False; op.cnt = 0
        deps = {}

        def dep(p):
            if p is None or p is op:
                return
            if (not p.dma) and (not op.dma) and p.eng == "pe" and eng == "pe":
                return
            if p.dma:
                k = ("D", p.eng, p.idx)
            else:
                k = (p.eng, False)
            cur = deps.get(k)
            if cur is None or p.idx > cur.idx:
                deps[k] = p

        rkeys = []
        wkeys = []
        for ap_ in ins:
            for k in _keys(ap_):
                if SERIAL_READS or k in _PSUM_NAMES:
                    wkeys.append(k)
                else:
                    rkeys.append(k)
        for ap_ in outs:
            wkeys.extend(_keys(ap_))
        for r in rkeys:
            st = self.res.get(r)
            if st is not None:
                dep(st[0])
        for wk in wkeys:
            st = self.res.get(wk)
            if st is not None:
                dep(st[0])
                for rr in st[1]:
                    dep(rr)
        op.deps = list(deps.values())
        for p in op.deps:
            p.signals = True
        for r in rkeys:
            st = self.res.get(r)
            if st is None:
                st = self.res[r] = [None, []]
            st[1].append(op)
        for wk in wkeys:
            self.res[wk] = [op, []]
        self.ops.append(op)
        return op

    def finalize(self, dma_slots):
        cnt = {}
        for op in self.ops:
            if op.dma:
                op.signals = True
                nq = cnt.get(op.eng, 0)
                cnt[op.eng] = nq + 1
                R = dma_slots[op.eng]
                op.slot = nq % R
                op.cnt = nq // R + 1
            elif op.signals:
                k = (op.eng, False)
                cnt[k] = cnt.get(k, 0) + 1
                op.cnt = cnt[k]

    def emit(self, eng, handle, sems):
        hwm = {}
        for op in self.ops:
            if op.eng != eng:
                continue
            if op.dma and op.cnt > 1:
                k = ("D", op.eng, op.slot)
                val = (op.cnt - 1) * 16
                if hwm.get(k, 0) < val:
                    handle.wait_ge(sems[k], val)
                    hwm[k] = val
            for p in op.deps:
                if p.dma:
                    k = ("D", p.eng, p.slot)
                    val = p.cnt * 16
                else:
                    k = (p.eng, False)
                    val = p.cnt
                if hwm.get(k, 0) < val:
                    handle.wait_ge(sems[k], val)
                    hwm[k] = val
            inst = op.fn(handle)
            if op.signals:
                if op.dma:
                    inst.then_inc(sems[("D", op.eng, op.slot)], 16)
                else:
                    inst.then_inc(sems[(op.eng, False)], 1)


def APx(base_ap, offset, pattern):
    return bass_rust.AP(base_ap.tensor, offset, pattern)


def cdiv(a, b):
    return (a + b - 1) // b


def build_nc():
    nc = bass.Bass("TRN2", target_bir_lowering=False, debug=False)
    dumps = set(DEBUG_DUMPS)

    def din(name, shape, dtype=F32):
        return nc.dram_tensor(name, shape, dtype, kind="ExternalInput").ap()

    w = {}
    w["x"] = din("x", [BC * S, D])
    for item in [
        ("l0a_wq_t", [D, D]), ("l0a_wk_t", [D, D]), ("l0a_wv_t", [D, D]),
        ("l0a_wo_t", [D, D]),
        ("l0f_wqkv", [D, 3 * D]), ("l0f_wo", [D, D]), ("l0f_wr", [D, D]),
        ("l0f_ubt", [DH, NH]), ("l0f_vbt", [DH, NH]),
        ("l0f_ln1g", [D]), ("l0f_ln1b", [D]), ("l0f_ln2g", [D]), ("l0f_ln2b", [D]),
        ("l0f_ln3g", [D]), ("l0f_ln3b", [D]),
        ("l0f_wff1", [D, FF]), ("l0f_bff1", [128, FF // 128]),
        ("l0f_wff2", [FF, D]), ("l0f_bff2", [1, D]),
        ("l0f_wg", [D, NE]),
        ("l0f_we1", [NE, D, FF]), ("l0f_be1", [NE, 128, FF // 128]),
        ("l0f_we2", [NE, FF, D]), ("l0f_be2", [NE, 1, D]),
        ("l0f_wout", [D, D]),
        ("l1a_wq", [D, D]), ("l1a_wk", [D, D]), ("l1a_wv", [D, D]),
        ("l1a_wo", [D, D]), ("l1a_alpha", [1, 1]),
        ("l1f_wgate", [IFF, D]), ("l1f_wup", [IFF, D]), ("l1f_wdown", [D, IFF]),
        ("c_ident", [128, 128]), ("c_rotA", [128, 128]),
        ("c_cos0", [D, S]), ("c_sin0", [D, S]),
        ("c_cos1", [D, S]), ("c_sin1", [D, S]),
        ("c_pet", [D, S]), ("c_hm", [128, 8 * NH]),
        ("c_cmask", [128, 128], U8), ("c_ones", [1, 128]), ("c_onesf", [1, 128]), ("c_ones2", [1, 512]),
    ]:
        nm, shp = item[0], item[1]
        w[nm] = din(nm, shp, item[2] if len(item) > 2 else F32)

    out = nc.dram_tensor("out", [BC * S, D], F32, kind="ExternalOutput").ap()

    sc = {}

    def S_(name, shape, dtype=F32):
        kind = "ExternalOutput" if name in dumps else "Internal"
        sc[name] = nc.dram_tensor(name, shape, dtype, kind=kind).ap()

    S_("xn_tm", [BC * S, D]); S_("xnT", [BC * D, S], BF16)
    S_("l0a_qT", [BC * D, S], BF16); S_("l0a_kT", [BC * D, S], BF16)
    S_("l0a_v", [BC * S, D], BF16)
    S_("s_tm", [BC * S, D])
    S_("l0f_qT", [BC * D, S], BF16); S_("l0f_kT", [BC * D, S], BF16)
    S_("l0f_v", [BC * S, D], BF16); S_("rT", [D, S], BF16)
    S_("bd0", [128 * RS], BF16); S_("bd1", [128 * RS], BF16)
    S_("h1_tm", [BC * S, D]); S_("h1T", [BC * D, S], BF16)
    S_("h2_tm", [BC * S, D]); S_("h2T", [BC * D, S], BF16)
    S_("m_tm", [BC * S, NE]); S_("moe_tm", [BC * S, D])
    S_("x1_tm", [BC * S, D])
    S_("xn1_tm", [BC * S, D]); S_("xn1T", [BC * D, S], BF16)
    S_("l1_qT", [BC * D, S], BF16); S_("l1_kT", [BC * D, S], BF16)
    S_("l1_v", [BC * S, D], BF16)
    S_("l1_qn", [BC * NH, S]); S_("l1_kn", [BC * NH, S])
    S_("l1_qnT", [BC * S, NH])
    S_("wgate_qT", [D, IFF], BF16); S_("wup_qT", [D, IFF], BF16)
    S_("wdown_qT", [IFF, D], BF16)
    S_("hqT", [BC * IFF, S], BF16); S_("xqT", [BC * D, S], BF16)
    S_("ctx0T", [BC * D, S], BF16); S_("ctx1T", [BC * D, S], BF16)

    sch = Sched()

    with ExitStack() as ctx:
        def sb(name, shape, dtype=F32):
            return ctx.enter_context(nc.sbuf_tensor(name, shape, dtype))

        M1 = sb("M1", [128, 16384], BF16)
        M2 = sb("M2", [128, 16384], BF16)
        W = sb("W", [128, 32768], BF16)
        F1 = sb("F1", [128, 2048])
        F2 = sb("F2", [128, 2816])
        F3 = sb("F3", [128, 2816])
        CN = sb("CN", [128, 2048])
        Bq = sb("Bq", [128, 2816], BF16)
        PTb = sb("PTb", [128, 2048], BF16)
        HB = sb("HB", [128, 4096], BF16)
        HB2 = sb("HB2", [128, 2048], BF16)
        IQ = sb("IQ", [128, 1408], I16)
        U8T = sb("U8T", [128, 1408], U8)
        U8b = sb("U8b", [128, 1408], U8)
        LGB = sb("LGB", [128, 2048], BF16)
        SM = sb("SM", [128, 64])
        SM3 = sb("SM3", [128, 64])
        DG = sb("DG", [128, 256], BF16)
        MME = sb("MME", [128, 2 * NE])
        QNS = sb("QNS", [128, 16])
        WGS = sb("WGS", [128, 8 * NE], BF16)
        BR = sb("BR", [1, 1024], BF16)
        ident_bf = sb("ident_bf", [128, 128], BF16)
        ident_f = sb("ident_f", [128, 128], F32)
        rotA = sb("rotA", [128, 128], BF16)
        cmask = sb("cmask", [128, 128], U8)
        negT = sb("negT", [128, 128])
        ones_bf = sb("ones_bf", [1, 128], BF16)
        ones_f = sb("ones_f", [1, 128], F32)
        ones2_bf = sb("ones2_bf", [1, 512], BF16)
        ubt = sb("ubt", [DH, NH])
        vbt = sb("vbt", [DH, NH])
        hm = sb("hm", [128, 8 * NH], BF16)
        bff1_sb = sb("bff1_sb", [128, FF // 128])
        be1_sb = sb("be1_sb", [128, FF // 128])
        alpha_sc = sb("alpha_sc", [128, 1])
        lnsc = sb("lnsc", [1, 1])
        arow = sb("arow", [1, 1])
        lnrow = sb("lnrow", [1, D], BF16)
        zrow = sb("zrow", [128, 1])
        eps1 = sb("eps1", [128, 1])

        PS = [ctx.enter_context(nc.psum_tensor(f"ps{i}", [128, 512], F32))
              for i in range(7)]
        PB = [ctx.enter_context(nc.psum_tensor("pb0", [128, 1024], BF16))
              for i in range(1)]

        sems = {}
        for e in ("pe", "dve", "act", "sp", "g"):
            sems[(e, False)] = ctx.enter_context(nc.semaphore(f"sc_{e}"))
        DMA_SLOTS = {"sp": 36, "g": 36, "act": 12}
        for e, R in DMA_SLOTS.items():
            for i in range(R):
                sems[("D", e, i)] = ctx.enter_context(nc.semaphore(f"sd_{e}{i}"))

        blk = ctx.enter_context(nc.Block())

        def _outer16(ap_):
            for st_, cnt_ in ap_.ap:
                if cnt_ != 1:
                    return cnt_ % 16 == 0
            return False

        def dma(q, out_ap, in_ap, tag=""):
            if not (_outer16(out_ap) and _outer16(in_ap)):
                q = "act"
            elif q == "act":
                q = "g"
            sch.add(q, lambda e, o=out_ap, i=in_ap: e.dma_start(out=o, in_=i),
                    ins=[in_ap], outs=[out_ap], dma=True, tag=tag)

        def dma_acc(out_ap, in_ap, tag=""):
            sch.add("g", lambda e, o=out_ap, i=in_ap:
                    e.dma_start(out=o, in_=i, accum_op=ALU.add),
                    ins=[in_ap, out_ap], outs=[out_ap], dma=True, tag=tag)

        def mm(ps_ap, lhsT, rhs, start, stop, tag=""):
            sch.add("pe", lambda e, o=ps_ap, l=lhsT, r=rhs, s1=start, s2=stop:
                    e.matmul(o, l, r, start=s1, stop=s2),
                    ins=[lhsT, rhs], outs=[ps_ap], tag=tag)

        def trans(ps_ap, in_ap, id_ap, tag=""):
            sch.add("pe", lambda e, o=ps_ap, i=in_ap, d=id_ap: e.transpose(o, i, d),
                    ins=[in_ap, id_ap], outs=[ps_ap], tag=tag)

        def v_ts(out_ap, in_ap, s1, s2, op0, op1=None, accum=None, tag=""):
            ins = [in_ap] + [s for s in (s1, s2) if hasattr(s, "tensor")]
            outs = [out_ap] + ([accum] if accum is not None else [])
            if op1 is None and s2 is None and accum is None:
                sch.add("dve", lambda e, o=out_ap, i=in_ap, a=s1, p0=op0:
                        e.tensor_scalar(o, i, a, None, op0=p0),
                        ins=ins, outs=outs, tag=tag)
            elif accum is None:
                sch.add("dve", lambda e, o=out_ap, i=in_ap, a=s1, b=s2, p0=op0,
                        p1=op1: e.tensor_scalar(o, i, a, b, op0=p0, op1=p1),
                        ins=ins, outs=outs, tag=tag)
            else:
                sch.add("dve", lambda e, o=out_ap, i=in_ap, a=s1, b=s2, p0=op0,
                        p1=op1, ac=accum:
                        e.tensor_scalar(o, i, a, b, op0=p0, op1=p1, accum_out=ac),
                        ins=ins, outs=outs, tag=tag)

        def v_tt(out_ap, a_ap, b_ap, op, tag=""):
            sch.add("dve", lambda e, o=out_ap, a=a_ap, b=b_ap, p=op:
                    e.tensor_tensor(o, a, b, op=p),
                    ins=[a_ap, b_ap], outs=[out_ap], tag=tag)

        def v_stt(out_ap, in0, scal, in1, op0, op1, tag=""):
            ins = [in0, in1] + ([scal] if hasattr(scal, "tensor") else [])
            sch.add("dve", lambda e, o=out_ap, a=in0, s=scal, b=in1, p0=op0, p1=op1:
                    e.scalar_tensor_tensor(o, a, s, b, op0=p0, op1=p1),
                    ins=ins, outs=[out_ap], tag=tag)

        def v_copy(out_ap, in_ap, tag=""):
            sch.add("dve", lambda e, o=out_ap, i=in_ap: e.tensor_copy(o, i),
                    ins=[in_ap], outs=[out_ap], tag=tag)

        def v_red(out_ap, in_ap, op, absv=None, tag=""):
            sch.add("dve", lambda e, o=out_ap, i=in_ap, p=op, av=absv:
                    e.tensor_reduce(o, i, axis=AX.X, op=p, apply_absolute_value=av),
                    ins=[in_ap], outs=[out_ap], tag=tag)

        def v_recip(out_ap, in_ap, tag=""):
            sch.add("dve", lambda e, o=out_ap, i=in_ap: e.reciprocal(o, i),
                    ins=[in_ap], outs=[out_ap], tag=tag)

        def g_ts(out_ap, in_ap, s1, s2, op0, op1=None, tag=""):
            ins = [in_ap] + [x for x in (s1, s2) if hasattr(x, "tensor")]
            if op1 is None and s2 is None:
                sch.add("g", lambda e, o=out_ap, i=in_ap, a=s1, p0=op0:
                        e.tensor_scalar(o, i, a, None, op0=p0),
                        ins=ins, outs=[out_ap], tag=tag)
            else:
                sch.add("g", lambda e, o=out_ap, i=in_ap, a=s1, b=s2, p0=op0,
                        p1=op1: e.tensor_scalar(o, i, a, b, op0=p0, op1=p1),
                        ins=ins, outs=[out_ap], tag=tag)

        def g_tt(out_ap, a_ap, b_ap, op, tag=""):
            sch.add("g", lambda e, o=out_ap, a=a_ap, b=b_ap, p=op:
                    e.tensor_tensor(o, a, b, op=p),
                    ins=[a_ap, b_ap], outs=[out_ap], tag=tag)

        def v_sel(out_ap, mask, t_ap, f_ap, tag=""):
            sch.add("dve", lambda e, o=out_ap, m=mask, a=t_ap, b=f_ap:
                    e.select(o, m, a, b), ins=[mask, t_ap, f_ap], outs=[out_ap],
                    tag=tag)

        def v_memset(ap_, val, tag=""):
            sch.add("dve", lambda e, o=ap_, v=val: e.memset(o, v), ins=[],
                    outs=[ap_], tag=tag)

        def a_copy(out_ap, in_ap, tag=""):
            sch.add("act", lambda e, o=out_ap, i=in_ap: e.activation(
                o, i, AF.Identity), ins=[in_ap], outs=[out_ap], tag=tag)

        def a_act(out_ap, in_ap, func, bias=None, scale=None, accum=None, tag=""):
            ins = [in_ap]
            if bias is not None and hasattr(bias, "tensor"):
                ins.append(bias)
            outs = [out_ap] + ([accum] if accum is not None else [])
            kw = {}
            if bias is not None:
                kw["bias"] = bias
            if scale is not None:
                kw["scale"] = scale

            def f(e, o=out_ap, i=in_ap, fc=func, kw=kw, ac=accum):
                if ac is not None:
                    return e.activation(o, i, fc, accum_out=ac, **kw)
                return e.activation(o, i, fc, **kw)
            sch.add("act", f, ins=ins, outs=outs, tag=tag)

        ALL_OUT = []

        stop_hit = [False]

        def stopped():
            return stop_hit[0]

        def phase_done(name):
            if DEBUG_STOP is not None and name == DEBUG_STOP:
                stop_hit[0] = True

        dma("g", ident_bf[:], w["c_ident"])
        dma("sp", ident_f[:], w["c_ident"])
        dma("g", rotA[:], w["c_rotA"])
        dma("sp", cmask[:], w["c_cmask"])
        dma("g", ones_bf[:], w["c_ones"])
        dma("sp", ones_f[:], w["c_onesf"])
        dma("g", ones2_bf[:], w["c_ones2"])
        dma("sp", ubt[:], w["l0f_ubt"])
        dma("sp", vbt[:], w["l0f_vbt"])
        dma("g", hm[:], w["c_hm"])
        dma("sp", bff1_sb[:], w["l0f_bff1"])
        v_memset(negT[:], NEG)
        v_memset(zrow[:], 0.0)
        v_memset(eps1[:], 1e-5)
        s0 = float(math.sqrt(DH) / math.log(1.0 + DH))
        dma("sp", arow[:], w["l1a_alpha"])
        a_act(lnsc[:], arow[:], AF.Exp, scale=math.log(s0))
        mm(PS[5][:, :1], ones_f[:], lnsc[:], True, True)
        v_copy(alpha_sc[:], PS[5][:, :1])

        def load_lgb(gnm, bnm):
            for j, nm in enumerate((gnm, bnm)):
                dma("g", lnrow[:], w[nm].unsqueeze(0))
                for h in range(2):
                    mm(PS[5][:, :512], ones_bf[:], lnrow[:, h * 512:(h + 1) * 512],
                       True, True)
                    v_copy(LGB[:, j * 1024 + h * 512:j * 1024 + (h + 1) * 512],
                           PS[5][:, :512])

        def ln_tile(src, dst, smbase, affine=False):
            c = smbase
            v_red(SM[:, c:c + 1], src, ALU.add)
            a_act(CN[:, 0:1024], src, AF.Square, accum=SM[:, c + 1:c + 2])
            v_ts(SM[:, c + 2:c + 3], SM[:, c:c + 1], 1.0 / D, None, ALU.mult)
            v_tt(SM[:, c + 3:c + 4], SM[:, c + 2:c + 3], SM[:, c + 2:c + 3],
                 ALU.mult)
            v_ts(SM[:, c + 4:c + 5], SM[:, c + 1:c + 2], 1.0 / D,
                 SM[:, c + 3:c + 4], ALU.mult, ALU.subtract)
            a_act(SM[:, c + 5:c + 6], SM[:, c + 4:c + 5], AF.Ln, bias=eps1[:])
            a_act(SM[:, c + 6:c + 7], SM[:, c + 5:c + 6], AF.Exp, scale=-0.5)
            v_tt(SM[:, c + 7:c + 8], SM[:, c + 2:c + 3], SM[:, c + 6:c + 7],
                 ALU.mult)
            if not affine:
                v_ts(dst, src, SM[:, c + 6:c + 7], SM[:, c + 7:c + 8],
                     ALU.mult, ALU.subtract)
            else:
                v_ts(CN[:, 1024:2048], src, SM[:, c + 6:c + 7],
                     SM[:, c + 7:c + 8], ALU.mult, ALU.subtract)
                v_tt(CN[:, 0:1024], CN[:, 1024:2048], LGB[:, 0:1024], ALU.mult)
                v_tt(dst, CN[:, 0:1024], LGB[:, 1024:2048], ALU.add)

        def t8_store(src_bf, dram_ap, rowbase, col, nch=8):
            for dc in range(nch):
                trans(PB[0][:, :128], src_bf[:, dc * 128:(dc + 1) * 128],
                      ident_bf[:])
                v_copy(PTb[:, dc * 128:(dc + 1) * 128], PB[0][:, :128])
            dst = APx(dram_ap, rowbase * S + col,
                      [[S, 128], [128 * S, nch], [1, 128]])
            srcv = PTb[:, 0:nch * 128].rearrange("p (c n) -> p c n", n=128)
            dma("act", dst, srcv)

        def ph_ln_in(b, src_ap, dst_tm, dst_T):
            for st in range(NST):
                r0 = b * S + st * 128
                p2 = st % 2
                xin = F2[:, p2 * 1024:(p2 + 1) * 1024]
                xo = F3[:, p2 * 1024:(p2 + 1) * 1024]
                dma("sp", xin, src_ap[r0:r0 + 128, :])
                ln_tile(xin, xo, p2 * 16, affine=False)
                dma("g", dst_tm[r0:r0 + 128, :], xo)
                a_act(Bq[:, 0:1024], xo, AF.Identity)
                t8_store(Bq[:, 0:1024], dst_T, b * D, st * 128)

        for b in range(BC):
            ph_ln_in(b, w["x"], sc["xn_tm"], sc["xnT"])
        phase_done("ln0")

        def quant_fp4(src, dst_bf, wd, alt=False):
            nb = wd // 16
            bmraw = CN[:, 1408:1408 + nb]
            bmc = CN[:, 1536:1536 + nb]
            rec = CN[:, 1792:1792 + nb]
            src3 = src.rearrange("p (b k) -> p b k", k=16)

            def bc16(ap2d):
                a = ap2d.ap
                return APx(ap2d, ap2d.offset, [[a[0][0], 128], [a[1][0], nb],
                                               [0, 16]])
            if alt:
                T = F1[:, 0:wd]
                SG = F3[:, 1408:1408 + wd]
            else:
                T = F2[:, 0:wd]
                SG = F1[:, 0:wd]
            SQ = CN[:, 0:wd]
            v_red(bmraw, src3, ALU.max, absv=True)
            v_ts(bmc, bmraw, 1.0 / 6.0, 1e-12, ALU.mult, ALU.max)
            v_recip(rec, bmc)
            t3 = T.rearrange("p (b k) -> p b k", k=16)
            v_tt(t3, src3, bc16(rec), ALU.mult)
            a_act(SQ, T, AF.Square)
            a_act(SG, T, AF.Sign)
            v_ts(IQ[:, 0:wd], T, 2.0, None, ALU.mult)
            v_ts(src, IQ[:, 0:wd], 0.5, None, ALU.mult)
            g_ts(U8T[:, 0:wd], SQ, 5.0625, None, ALU.is_lt)
            g_ts(U8b[:, 0:wd], SQ, 20.25, None, ALU.is_lt)
            v_ts(IQ[:, 0:wd], T, 1.0, None, ALU.mult)
            g_ts(T, SQ, 25.0, None, ALU.is_gt)
            v_stt(SQ, T, 2.0, SG, ALU.mult, ALU.mult)
            g_ts(T, SG, 4.0, None, ALU.mult)
            g_tt(SG, SQ, T, ALU.add)
            v_copy(T, IQ[:, 0:wd])
            v_sel(SQ, U8b[:, 0:wd], T, SG)
            v_sel(T, U8T[:, 0:wd], src, SQ)
            d3 = dst_bf.rearrange("p (b k) -> p b k", k=16)
            v_tt(d3, t3, bc16(bmc), ALU.mult)

        def emit_wq_tile(src_dram, n_rows, kdim, dst_dram, it):
            nh = cdiv(kdim, 1408)
            for hh in range(nh):
                c0 = hh * 1408
                cw = min(1408, kdim - c0)
                dma("sp", F3[:, c0:c0 + cw],
                    src_dram[it * 128:(it + 1) * 128, c0:c0 + cw])
                quant_fp4(F3[:, c0:c0 + cw], Bq[:, c0:c0 + cw], cw,
                          alt=(kdim == D))
            nch = kdim // 128
            for g2 in range(cdiv(nch, 8)):
                n8 = min(8, nch - g2 * 8)
                for dc in range(n8):
                    di = g2 * 8 + dc
                    trans(PB[0][:, :128], Bq[:, di * 128:(di + 1) * 128],
                          ident_bf[:])
                    v_copy(PTb[:, dc * 128:(dc + 1) * 128], PB[0][:, :128])
                dst = APx(dst_dram, (g2 * 8 * 128) * n_rows + it * 128,
                          [[n_rows, 128], [128 * n_rows, n8], [1, 128]])
                dma("act", dst,
                    PTb[:, 0:n8 * 128].rearrange("p (c n) -> p c n", n=128))

        WQ_TASKS = []
        for args_ in ((w["l1f_wgate"], IFF, D, sc["wgate_qT"]),
                      (w["l1f_wup"], IFF, D, sc["wup_qT"]),
                      (w["l1f_wdown"], D, IFF, sc["wdown_qT"])):
            for it in range(args_[1] // 128):
                WQ_TASKS.append(args_ + (it,))
        WQ_TASKS.reverse()
        phase_done("wquant")

        def load_xnT_M1(b, src):
            for dc in range(8):
                dma("sp", M1[:, dc * 2048:(dc + 1) * 2048],
                    src[b * D + dc * 128:b * D + dc * 128 + 128, :])

        def load_w_bf(arena, base, src_dram, rows, cols):
            for dc in range(rows // 128):
                dma("g", arena[:, base + dc * cols:base + (dc + 1) * cols],
                    src_dram[dc * 128:(dc + 1) * 128, :])

        TRIGC = 24576
        TRIGS = 26624

        def proj_rope(b, wbase, cosd, sind, dstd, qnpsum=None):
            for ec in range(8):
                dma("g", W[:, TRIGC:TRIGC + 2048], cosd[ec * 128:(ec + 1) * 128, :])
                dma("g", W[:, TRIGS:TRIGS + 2048], sind[ec * 128:(ec + 1) * 128, :])
                for s4 in range(4):
                    pp = s4 % 2
                    for dc in range(8):
                        mm(PS[pp][:, :512],
                           W[:, wbase + dc * 1024 + ec * 128:
                             wbase + dc * 1024 + ec * 128 + 128],
                           M1[:, dc * 2048 + s4 * 512:dc * 2048 + (s4 + 1) * 512],
                           dc == 0, dc == 7)
                    v_copy(Bq[:, s4 * 512:(s4 + 1) * 512], PS[pp][:, :512])
                    mm(PS[pp][:, :512], rotA[:], Bq[:, s4 * 512:(s4 + 1) * 512],
                       True, True)
                    v_tt(F2[:, s4 * 512:(s4 + 1) * 512], PS[pp][:, :512],
                         W[:, TRIGS + s4 * 512:TRIGS + (s4 + 1) * 512], ALU.mult)
                    v_tt(F3[:, s4 * 512:(s4 + 1) * 512],
                         Bq[:, s4 * 512:(s4 + 1) * 512],
                         W[:, TRIGC + s4 * 512:TRIGC + (s4 + 1) * 512], ALU.mult)
                    v_tt(PTb[:, s4 * 512:(s4 + 1) * 512],
                         F2[:, s4 * 512:(s4 + 1) * 512],
                         F3[:, s4 * 512:(s4 + 1) * 512], ALU.add)
                    if qnpsum is not None:
                        a_act(HB[:, s4 * 512:(s4 + 1) * 512],
                              PTb[:, s4 * 512:(s4 + 1) * 512], AF.Square)
                        mm(PS[2 + s4][0:16, :512], hm[:, ec * 16:(ec + 1) * 16],
                           HB[:, s4 * 512:(s4 + 1) * 512], ec == 0, ec == 7)
                dma("act", dstd[b * D + ec * 128:b * D + ec * 128 + 128, :], PTb[:])

        def proj_v(b, rhs_fn, dstd):
            for st in range(NST):
                p2 = st % 2
                for nh2 in range(2):
                    pp = nh2
                    for dc in range(8):
                        mm(PS[pp][:, :512],
                           M1[:, dc * 2048 + st * 128:dc * 2048 + st * 128 + 128],
                           rhs_fn(dc, nh2), dc == 0, dc == 7)
                    v_copy(HB[:, p2 * 1024 + nh2 * 512:p2 * 1024 + (nh2 + 1) * 512],
                           PS[pp][:, :512])
                dma("act", dstd[b * S + st * 128:b * S + st * 128 + 128, :],
                    HB[:, p2 * 1024:(p2 + 1) * 1024])

        def ph_l0a_proj(b):
            load_xnT_M1(b, sc["xnT"])
            load_w_bf(W, 0, w["l0a_wq_t"], D, D)
            load_w_bf(W, 8192, w["l0a_wk_t"], D, D)
            load_w_bf(W, 16384, w["l0a_wv_t"], D, D)
            proj_rope(b, 0, w["c_cos0"], w["c_sin0"], sc["l0a_qT"])
            proj_rope(b, 8192, w["c_cos0"], w["c_sin0"], sc["l0a_kT"])
            proj_v(b, lambda dc, nh2: W[:, 16384 + dc * 1024 + nh2 * 512:
                                        16384 + dc * 1024 + nh2 * 512 + 512],
                   sc["l0a_v"])

        def ph_l0a_attn(b):
            for dc in range(8):
                dma("sp", M1[:, dc * 2048:(dc + 1) * 2048],
                    sc["l0a_qT"][b * D + dc * 128:b * D + dc * 128 + 128, :])
                dma("sp", M2[:, dc * 2048:(dc + 1) * 2048],
                    sc["l0a_kT"][b * D + dc * 128:b * D + dc * 128 + 128, :])
            for st in range(NST):
                dma("sp", W[:, st * 1024:(st + 1) * 1024],
                    sc["l0a_v"][b * S + st * 128:b * S + st * 128 + 128, :])
            load_w_bf(W, 16384, w["l0a_wo_t"], D, D)

            def stage1(qt, par):
                sm = 32 * par
                for kc4 in range(4):
                    pp = kc4 % 2
                    for dc in range(8):
                        mm(PS[pp][:, :512],
                           M1[:, dc * 2048 + qt * 128:dc * 2048 + qt * 128 + 128],
                           M2[:, dc * 2048 + kc4 * 512:dc * 2048 + (kc4 + 1) * 512],
                           dc == 0, dc == 7)
                    sbuf_sc = F1 if par == 0 else F2
                    v_ts(sbuf_sc[:, kc4 * 512:(kc4 + 1) * 512],
                         PS[pp][:, :512], 1.0, -3.0e38, ALU.mult, ALU.max,
                         accum=SM[:, sm + kc4:sm + kc4 + 1])
                v_red(SM[:, sm + 4:sm + 5], SM[:, sm:sm + 4], ALU.max)
                v_ts(SM[:, sm + 5:sm + 6], SM[:, sm + 4:sm + 5], -1.0 / 32.0,
                     None, ALU.mult)

            def stage2(qt, par):
                sm = 32 * par
                prb0 = 24576 + par * 2048
                ptb0 = 28672 + par * 2048
                sbuf_sc = F1 if par == 0 else F2
                a_act(APx(W[:], prb0, [[32768, 128], [1, 2048]]),
                      sbuf_sc[:, 0:2048], AF.Exp,
                      bias=SM[:, sm + 5:sm + 6], scale=1.0 / 32.0,
                      accum=SM[:, sm + 6:sm + 7])
                v_recip(SM[:, sm + 7:sm + 8], SM[:, sm + 6:sm + 7])
                v_ts(DG[:, par * 128:(par + 1) * 128], ident_bf[:],
                     SM[:, sm + 7:sm + 8], None, ALU.mult)
                for kc in range(16):
                    pp = 4 + kc % 2
                    mm(PS[pp][:, :128],
                       APx(W[:], prb0 + kc * 128, [[32768, 128], [1, 128]]),
                       DG[:, par * 128:(par + 1) * 128], True, True)
                    a_copy(APx(W[:], ptb0 + kc * 128, [[32768, 128], [1, 128]]),
                           PS[pp][:, :128])
                for dc in range(8):
                    pp = 4 + dc % 2
                    for kc in range(16):
                        mm(PS[pp][:, :128],
                           W[:, kc * 1024 + dc * 128:kc * 1024 + dc * 128 + 128],
                           APx(W[:], ptb0 + kc * 128, [[32768, 128], [1, 128]]),
                           kc == 0, kc == 15)
                    v_copy(HB[:, par * 2048 + dc * 128:par * 2048 + (dc + 1) * 128],
                           PS[pp][:, :128])
                for nh2 in range(2):
                    pp = 2 + nh2
                    for dc in range(8):
                        mm(PS[pp][:, :512],
                           HB[:, par * 2048 + dc * 128:par * 2048 + (dc + 1) * 128],
                           W[:, 16384 + dc * 1024 + nh2 * 512:
                             16384 + dc * 1024 + nh2 * 512 + 512],
                           dc == 0, dc == 7)
                    v_copy(CN[:, par * 1024 + nh2 * 512:par * 1024 + (nh2 + 1) * 512],
                           PS[pp][:, :512])
                dma("act", sc["s_tm"][b * S + qt * 128:b * S + qt * 128 + 128, :],
                    CN[:, par * 1024:par * 1024 + 1024])

            for qt in range(NST):
                stage1(qt, qt % 2)
                if qt >= 1:
                    stage2(qt - 1, (qt - 1) % 2)
            stage2(NST - 1, (NST - 1) % 2)

        if not stopped():
            for b in range(BC):
                ph_l0a_proj(b)
        phase_done("l0a_proj")
        if not stopped():
            for b in range(BC):
                ph_l0a_attn(b)
        phase_done("l0a")

        def ph_rT():
            for dc in range(8):
                dma("g", M2[:, dc * 2048:(dc + 1) * 2048],
                    w["c_pet"][dc * 128:(dc + 1) * 128, :])
                dma("g", W[:, 24576 + dc * 1024:24576 + (dc + 1) * 1024],
                    w["l0f_wr"][dc * 128:(dc + 1) * 128, :])
            for ec in range(8):
                for s4 in range(4):
                    pp = s4 % 2
                    for dc in range(8):
                        mm(PS[pp][:, :512],
                           W[:, 24576 + dc * 1024 + ec * 128:
                             24576 + dc * 1024 + ec * 128 + 128],
                           M2[:, dc * 2048 + s4 * 512:dc * 2048 + (s4 + 1) * 512],
                           dc == 0, dc == 7)
                    v_copy(PTb[:, s4 * 512:(s4 + 1) * 512], PS[pp][:, :512])
                dma("act", sc["rT"][ec * 128:(ec + 1) * 128, :], PTb[:])

        def ph_l0f_qkv(b):
            load_xnT_M1(b, sc["xnT"])
            for dc in range(8):
                dma("g", W[:, dc * 3072:(dc + 1) * 3072],
                    w["l0f_wqkv"][dc * 128:(dc + 1) * 128, :])
            for cbase, dstd in ((0, sc["l0f_qT"]), (D, sc["l0f_kT"])):
                for ec in range(8):
                    for s4 in range(4):
                        pp = s4 % 2
                        for dc in range(8):
                            mm(PS[pp][:, :512],
                               W[:, dc * 3072 + cbase + ec * 128:
                                 dc * 3072 + cbase + ec * 128 + 128],
                               M1[:, dc * 2048 + s4 * 512:
                                  dc * 2048 + (s4 + 1) * 512],
                               dc == 0, dc == 7)
                        v_copy(PTb[:, s4 * 512:(s4 + 1) * 512], PS[pp][:, :512])
                    dma("act", dstd[b * D + ec * 128:b * D + ec * 128 + 128, :],
                        PTb[:])
            proj_v(b, lambda dc, nh2: W[:, dc * 3072 + 2 * D + nh2 * 512:
                                        dc * 3072 + 2 * D + nh2 * 512 + 512],
                   sc["l0f_v"])

        if not stopped():
            ph_rT()
            for b in range(BC):
                ph_l0f_qkv(b)
        phase_done("l0f_qkv")

        def ph_l0f_attn(b):
            for h in range(NH):
                dc = h // 2
                rb = (h % 2) * 64
                qu = APx(M2[:], 0, [[16384, 64], [1, 2048]])
                qv = APx(M2[:], 2048, [[16384, 64], [1, 2048]])
                kh = APx(M2[:], 4096, [[16384, 64], [1, 2048]])
                rh = APx(M2[:], 6144, [[16384, 64], [1, 2048]])
                vh = M2[:, 8192:9216]
                qu2 = APx(M2[:], 13312, [[16384, 64], [1, 2048]])
                dma("sp", qu, sc["l0f_qT"][b * D + h * 64:b * D + h * 64 + 64, :])
                dma("sp", kh, sc["l0f_kT"][b * D + h * 64:b * D + h * 64 + 64, :])
                dma("sp", rh, sc["rT"][h * 64:h * 64 + 64, :])
                vsrc = APx(sc["l0f_v"], (b * S) * D + h * 64,
                           [[D, 128], [128 * D, 16], [1, 64]])
                dma("sp", vh.rearrange("p (c n) -> p c n", n=64), vsrc)
                v_ts(qv, qu, vbt[:, h:h + 1], None, ALU.add)
                v_ts(qu2, qu, ubt[:, h:h + 1], None, ALU.add)

                def stage1a(qt, par):
                    kw = (qt + 1) * 128
                    nk4 = cdiv(kw, 512)
                    c0 = 2048 - kw
                    bdbuf = sc["bd0"] if (h * NST + qt) % 2 == 0 else sc["bd1"]
                    bst = 12288 + par * 2048
                    bsh = 8192 + par * 2048
                    for i4 in range(nk4):
                        cw = min(512, kw - i4 * 512)
                        cs = c0 + i4 * 512
                        pp = 2 + i4 % 2
                        mm(PS[pp][:, :cw],
                           APx(M2[:], 2048 + qt * 128, [[16384, 64], [1, 128]]),
                           APx(M2[:], 6144 + cs, [[16384, 64], [1, cw]]),
                           True, True)
                        a_copy(APx(W[:], bst + cs - c0, [[32768, 128], [1, cw]]),
                               PS[pp][:, :cw])
                    dma("act", APx(bdbuf, c0, [[RS, 128], [1, kw]]),
                        APx(W[:], bst, [[32768, 128], [1, kw]]))
                    dma("sp", APx(W[:], bsh, [[32768, 128], [1, kw]]),
                        APx(bdbuf, 2047 - qt * 128, [[RS - 1, 128], [1, kw]]))

                def stage1b(qt, par):
                    sm = 32 * par
                    sbuf_sc = F1 if par == 0 else F2
                    kw = (qt + 1) * 128
                    nk4 = cdiv(kw, 512)
                    bsh = 8192 + par * 2048
                    for i4 in range(nk4):
                        cw = min(512, kw - i4 * 512)
                        pp = i4 % 2
                        mm(PS[pp][:, :cw],
                           APx(M2[:], 13312 + qt * 128, [[16384, 64], [1, 128]]),
                           APx(M2[:], 4096 + i4 * 512, [[16384, 64], [1, cw]]),
                           True, True)
                        v_tt(sbuf_sc[:, i4 * 512:i4 * 512 + cw], PS[pp][:, :cw],
                             APx(W[:], bsh + i4 * 512, [[32768, 128], [1, cw]]),
                             ALU.add)
                    v_copy(F2[:, 2560:2688], sbuf_sc[:, kw - 128:kw])
                    v_sel(sbuf_sc[:, kw - 128:kw], cmask[:], F2[:, 2560:2688],
                          negT[:])
                    v_red(SM[:, sm + 0:sm + 1], sbuf_sc[:, 0:kw], ALU.max)
                    v_ts(SM[:, sm + 1:sm + 2], SM[:, sm + 0:sm + 1], -0.125,
                         None, ALU.mult)

                def stage2(qt, par):
                    sm = 32 * par
                    sbuf_sc = F1 if par == 0 else F2
                    kw = (qt + 1) * 128
                    prb0 = par * 2048
                    ptb0 = 4096 + par * 2048
                    a_act(APx(W[:], prb0, [[32768, 128], [1, kw]]),
                          sbuf_sc[:, 0:kw], AF.Exp, bias=SM[:, sm + 1:sm + 2],
                          scale=0.125, accum=SM[:, sm + 2:sm + 3])
                    v_recip(SM[:, sm + 3:sm + 4], SM[:, sm + 2:sm + 3])
                    v_ts(DG[:, par * 128:(par + 1) * 128], ident_bf[:],
                         SM[:, sm + 3:sm + 4], None, ALU.mult)
                    for kc in range(qt + 1):
                        pp = 4 + kc % 2
                        mm(PS[pp][:, :128],
                           APx(W[:], prb0 + kc * 128, [[32768, 128], [1, 128]]),
                           DG[:, par * 128:(par + 1) * 128], True, True)
                        a_copy(APx(W[:], ptb0 + kc * 128,
                                   [[32768, 128], [1, 128]]), PS[pp][:, :128])
                    for kc in range(qt + 1):
                        mm(PS[6][0:64, :128], vh[:, kc * 64:(kc + 1) * 64],
                           APx(W[:], ptb0 + kc * 128, [[32768, 128], [1, 128]]),
                           kc == 0, kc == qt)
                    v_copy(M1[rb:rb + 64,
                              dc * 2048 + qt * 128:dc * 2048 + qt * 128 + 128],
                           PS[6][0:64, :128])

                stage1a(0, 0)
                for qt in range(NST):
                    if qt + 1 < NST:
                        stage1a(qt + 1, (qt + 1) % 2)
                    stage1b(qt, qt % 2)
                    if qt >= 1:
                        stage2(qt - 1, (qt - 1) % 2)
                stage2(NST - 1, (NST - 1) % 2)

        def ph_l0f_h1(b):
            if "ctx0T" in dumps:
                for dc in range(8):
                    dma("g", sc["ctx0T"][b * D + dc * 128:b * D + dc * 128 + 128, :],
                        M1[:, dc * 2048:(dc + 1) * 2048])
            load_w_bf(W, 0, w["l0f_wo"], D, D)
            load_lgb("l0f_ln1g", "l0f_ln1b")
            for st in range(NST):
                r0 = b * S + st * 128
                for nh2 in range(2):
                    pp = nh2
                    for dc in range(8):
                        mm(PS[pp][:, :512],
                           M1[:, dc * 2048 + st * 128:dc * 2048 + st * 128 + 128],
                           W[:, dc * 1024 + nh2 * 512:dc * 1024 + nh2 * 512 + 512],
                           dc == 0, dc == 7)
                dma("sp", F2[:, 0:1024], sc["xn_tm"][r0:r0 + 128, :])
                for nh2 in range(2):
                    v_tt(F3[:, nh2 * 512:(nh2 + 1) * 512], PS[nh2][:, :512],
                         F2[:, nh2 * 512:(nh2 + 1) * 512], ALU.add)
                ln_tile(F3[:, 0:1024], F3[:, 1024:2048], 0, affine=True)
                dma("g", sc["h1_tm"][r0:r0 + 128, :], F3[:, 1024:2048])
                a_act(Bq[:, 0:1024], F3[:, 1024:2048], AF.Identity)
                t8_store(Bq[:, 0:1024], sc["h1T"], b * D, st * 128)

        def ph_l0f_ffn(b):
            for dc in range(8):
                dma("g", W[:, dc * 4096:(dc + 1) * 4096],
                    w["l0f_wff1"][dc * 128:(dc + 1) * 128, :])
            for fc in range(32):
                arena = M1 if fc < 16 else M2
                base = (fc % 16) * 1024
                dma("g", arena[:, base:base + 1024],
                    w["l0f_wff2"][fc * 128:(fc + 1) * 128, :])
            dma("g", BR[:], w["l0f_bff2"])
            load_lgb("l0f_ln2g", "l0f_ln2b")
            for st in range(NST):
                r0 = b * S + st * 128
                hsrc = APx(sc["h1T"], (b * D) * S + st * 128,
                           [[S, 128], [128 * S, 8], [1, 128]])
                dma("sp", APx(HB2[:], 0, [[2048, 128], [128, 8], [1, 128]]), hsrc)
                for fc in range(32):
                    pp = fc % 2
                    for dc in range(8):
                        mm(PS[pp][:, :128],
                           W[:, dc * 4096 + fc * 128:dc * 4096 + fc * 128 + 128],
                           HB2[:, dc * 128:(dc + 1) * 128], dc == 0, dc == 7)
                    a_act(HB[:, fc * 128:(fc + 1) * 128], PS[pp][:, :128],
                          AF.Relu, bias=bff1_sb[:, fc:fc + 1])
                for nh2 in range(2):
                    pp = 2 + nh2
                    mm(PS[pp][:, :512], ones_bf[:],
                       BR[:, nh2 * 512:(nh2 + 1) * 512], True, False)
                    for fc in range(32):
                        arena = M1 if fc < 16 else M2
                        base = (fc % 16) * 1024
                        mm(PS[pp][:, :512], HB[:, fc * 128:(fc + 1) * 128],
                           arena[:, base + nh2 * 512:base + nh2 * 512 + 512],
                           False, fc == 31)
                dma("sp", F2[:, 0:1024], sc["h1_tm"][r0:r0 + 128, :])
                for nh2 in range(2):
                    v_tt(F3[:, nh2 * 512:(nh2 + 1) * 512], PS[2 + nh2][:, :512],
                         F2[:, nh2 * 512:(nh2 + 1) * 512], ALU.add)
                ln_tile(F3[:, 0:1024], F3[:, 1024:2048], 0, affine=True)
                dma("g", sc["h2_tm"][r0:r0 + 128, :], F3[:, 1024:2048])
                a_act(Bq[:, 0:1024], F3[:, 1024:2048], AF.Identity)
                t8_store(Bq[:, 0:1024], sc["h2T"], b * D, st * 128)

        def ph_l0f_gate(b):
            for dc in range(8):
                dma("g", WGS[:, dc * 8:(dc + 1) * 8],
                    w["l0f_wg"][dc * 128:(dc + 1) * 128, :])
            for st in range(NST):
                r0 = b * S + st * 128
                hsrc = APx(sc["h2T"], (b * D) * S + st * 128,
                           [[S, 128], [128 * S, 8], [1, 128]])
                dma("sp", APx(HB2[:], 0, [[2048, 128], [128, 8], [1, 128]]), hsrc)
                for dc in range(8):
                    mm(PS[0][:, :8], HB2[:, dc * 128:(dc + 1) * 128],
                       WGS[:, dc * 8:(dc + 1) * 8], dc == 0, dc == 7)
                v_copy(SM3[:, 0:8], PS[0][:, :8])
                v_red(SM3[:, 8:9], SM3[:, 0:8], ALU.max)
                v_ts(SM3[:, 9:10], SM3[:, 8:9], -1.0, None, ALU.mult)
                a_act(SM3[:, 16:24], SM3[:, 0:8], AF.Exp, bias=SM3[:, 9:10],
                      accum=SM3[:, 10:11])
                v_recip(SM3[:, 11:12], SM3[:, 10:11])
                v_ts(SM3[:, 24:32], SM3[:, 16:24], SM3[:, 11:12], None, ALU.mult)
                v_red(SM3[:, 12:13], SM3[:, 24:32], ALU.max)
                v_ts(SM3[:, 32:40], SM3[:, 24:32], SM3[:, 12:13], None, ALU.is_ge)
                v_copy(SM3[:, 13:14], zrow[:])
                for ee in range(NE):
                    ra = 13 + (ee % 2)
                    rb2 = 13 + ((ee + 1) % 2)
                    v_ts(SM3[:, 15:16], SM3[:, ra:ra + 1], -1.0, 1.0, ALU.mult,
                         ALU.add)
                    v_tt(SM3[:, 40 + ee:41 + ee], SM3[:, 32 + ee:33 + ee],
                         SM3[:, 15:16], ALU.mult)
                    v_tt(SM3[:, rb2:rb2 + 1], SM3[:, ra:ra + 1],
                         SM3[:, 40 + ee:41 + ee], ALU.add)
                v_ts(MME[:, 0:8], SM3[:, 40:48], SM3[:, 12:13], None, ALU.mult)
                dma("g", sc["m_tm"][r0:r0 + 128, :], MME[:, 0:8])

        if not stopped():
            for b in range(BC):
                ph_l0f_attn(b)
                ph_l0f_h1(b)
        phase_done("l0f_h1")
        if not stopped():
            for b in range(BC):
                ph_l0f_ffn(b)
                ph_l0f_gate(b)
        phase_done("l0f_h2")

        def ph_moe():
            for e_ in range(NE):
                for dc in range(8):
                    dma("g", W[:, dc * 4096:(dc + 1) * 4096],
                        w["l0f_we1"][e_, dc * 128:(dc + 1) * 128, :])
                for fc in range(32):
                    arena = M1 if fc < 16 else M2
                    base = (fc % 16) * 1024
                    dma("g", arena[:, base:base + 1024],
                        w["l0f_we2"][e_, fc * 128:(fc + 1) * 128, :])
                dma("sp", be1_sb[:], w["l0f_be1"][e_])
                dma("g", BR[:], w["l0f_be2"][e_])
                for b in range(BC):
                    for st in range(NST):
                        r0 = b * S + st * 128
                        p2 = st % 2
                        hb2 = HB2[:, p2 * 1024:(p2 + 1) * 1024]
                        mme = MME[:, p2 * 8:(p2 + 1) * 8]
                        hsrc = APx(sc["h2T"], (b * D) * S + st * 128,
                                   [[S, 128], [128 * S, 8], [1, 128]])
                        dma("sp", APx(HB2[:], p2 * 1024,
                                      [[2048, 128], [128, 8], [1, 128]]), hsrc)
                        dma("sp", mme, sc["m_tm"][r0:r0 + 128, :])
                        for fc in range(32):
                            pp = fc % 2
                            for dc in range(8):
                                mm(PS[pp][:, :128],
                                   W[:, dc * 4096 + fc * 128:
                                     dc * 4096 + fc * 128 + 128],
                                   hb2[:, dc * 128:(dc + 1) * 128],
                                   dc == 0, dc == 7)
                            a_act(HB[:, fc * 128:(fc + 1) * 128], PS[pp][:, :128],
                                  AF.Relu, bias=be1_sb[:, fc:fc + 1])
                        for nh2 in range(2):
                            mm(PS[2 + nh2][:, :512], ones_bf[:],
                               BR[:, nh2 * 512:(nh2 + 1) * 512], True, False)
                        for fc in range(32):
                            arena = M1 if fc < 16 else M2
                            base = (fc % 16) * 1024
                            for nh2 in range(2):
                                mm(PS[2 + nh2][:, :512],
                                   HB[:, fc * 128:(fc + 1) * 128],
                                   arena[:, base + nh2 * 512:
                                         base + nh2 * 512 + 512],
                                   False, fc == 31)
                        for nh2 in range(2):
                            v_ts(F2[:, p2 * 1024 + nh2 * 512:
                                    p2 * 1024 + (nh2 + 1) * 512],
                                 PS[2 + nh2][:, :512], mme[:, e_:e_ + 1], None,
                                 ALU.mult)
                        if e_ == 0:
                            dma("g", sc["moe_tm"][r0:r0 + 128, :],
                                F2[:, p2 * 1024:(p2 + 1) * 1024])
                        else:
                            dma_acc(sc["moe_tm"][r0:r0 + 128, :],
                                    F2[:, p2 * 1024:(p2 + 1) * 1024])
                        if st % 4 == 3 and WQ_TASKS:
                            emit_wq_tile(*WQ_TASKS.pop())

        if not stopped():
            ph_moe()
            while WQ_TASKS:
                emit_wq_tile(*WQ_TASKS.pop())
        phase_done("moe")

        def ph_l0_final(b):
            load_w_bf(W, 0, w["l0f_wout"], D, D)
            load_lgb("l0f_ln3g", "l0f_ln3b")
            for st in range(NST):
                r0 = b * S + st * 128
                dma("sp", F2[:, 0:1024], sc["h2_tm"][r0:r0 + 128, :])
                dma("sp", F1[:, 0:1024], sc["moe_tm"][r0:r0 + 128, :])
                v_tt(F3[:, 0:1024], F2[:, 0:1024], F1[:, 0:1024], ALU.add)
                ln_tile(F3[:, 0:1024], F3[:, 1024:2048], 0, affine=True)
                a_act(Bq[:, 0:1024], F3[:, 1024:2048], AF.Identity)
                for dc in range(8):
                    trans(PB[0][:, :128], Bq[:, dc * 128:(dc + 1) * 128],
                          ident_bf[:])
                    v_copy(HB[:, dc * 128:(dc + 1) * 128], PB[0][:, :128])
                for nh2 in range(2):
                    pp = nh2
                    for dc in range(8):
                        mm(PS[pp][:, :512], HB[:, dc * 128:(dc + 1) * 128],
                           W[:, dc * 1024 + nh2 * 512:dc * 1024 + nh2 * 512 + 512],
                           dc == 0, dc == 7)
                    a_act(F2[:, 1024 + nh2 * 512:1024 + (nh2 + 1) * 512],
                          PS[pp][:, :512], AF.Silu)
                dma("sp", F1[:, 1024:2048], sc["s_tm"][r0:r0 + 128, :])
                dma("sp", CN[:, 0:1024], w["x"][r0:r0 + 128, :])
                v_tt(F2[:, 0:1024], F1[:, 1024:2048], F2[:, 1024:2048], ALU.mult)
                v_tt(F3[:, 0:1024], CN[:, 0:1024], F2[:, 0:1024], ALU.add)
                dma("g", sc["x1_tm"][r0:r0 + 128, :], F3[:, 0:1024])

        if not stopped():
            for b in range(BC):
                ph_l0_final(b)
        phase_done("l0")

        if not stopped():
            for b in range(BC):
                ph_ln_in(b, sc["x1_tm"], sc["xn1_tm"], sc["xn1T"])
        phase_done("ln1")

        def ph_l1_proj(b):
            load_xnT_M1(b, sc["xn1T"])
            load_w_bf(W, 0, w["l1a_wq"], D, D)
            load_w_bf(W, 8192, w["l1a_wk"], D, D)
            load_w_bf(W, 16384, w["l1a_wv"], D, D)
            for wbase, dstd, dstn in ((0, sc["l1_qT"], sc["l1_qn"]),
                                      (8192, sc["l1_kT"], sc["l1_kn"])):
                proj_rope(b, wbase, w["c_cos1"], w["c_sin1"], dstd, qnpsum=True)
                for s4 in range(4):
                    v_copy(F1[0:16, s4 * 512:(s4 + 1) * 512],
                           PS[2 + s4][0:16, :512])
                dma("g", dstn[b * NH:(b + 1) * NH, :], F1[0:16, 0:2048])
                if wbase == 0:
                    for qt in range(NST):
                        pp = qt % 2
                        trans(PS[pp][:, :16], F1[0:16, qt * 128:(qt + 1) * 128],
                              ident_f[0:16, 0:16])
                        v_copy(QNS[:], PS[pp][:, :16])
                        dma("g", sc["l1_qnT"][b * S + qt * 128:
                                              b * S + qt * 128 + 128, :], QNS[:])
            proj_v(b, lambda dc, nh2: W[:, 16384 + dc * 1024 + nh2 * 512:
                                        16384 + dc * 1024 + nh2 * 512 + 512],
                   sc["l1_v"])

        def ph_l1_attn(b):
            for h in range(DEBUG_L1_HEADS):
                dc = h // 2
                rb = (h % 2) * 64
                qh = APx(M2[:], 0, [[16384, 64], [1, 2048]])
                kh = APx(M2[:], 2048, [[16384, 64], [1, 2048]])
                vh = M2[:, 4096:5120]
                q2h = APx(M2[:], 5120, [[16384, 64], [1, 2048]])
                knr = APx(M2[:], 7168, [[16384, 1], [1, 2048]])
                qnr = APx(M2[:], 9216, [[16384, 1], [1, 2048]])
                dma("sp", qh, sc["l1_qT"][b * D + h * 64:b * D + h * 64 + 64, :])
                dma("sp", kh, sc["l1_kT"][b * D + h * 64:b * D + h * 64 + 64, :])
                vsrc = APx(sc["l1_v"], (b * S) * D + h * 64,
                           [[D, 128], [128 * D, 16], [1, 64]])
                dma("sp", vh.rearrange("p (c n) -> p c n", n=64), vsrc)
                v_ts(q2h, qh, -2.0, None, ALU.mult)
                dma("sp", F1[0:1, 0:2048],
                    sc["l1_kn"][b * NH + h:b * NH + h + 1, :])
                dma("sp", CN[0:1, 0:2048],
                    sc["l1_qn"][b * NH + h:b * NH + h + 1, :])
                v_ts(knr, F1[0:1, 0:2048], 1e-5, None, ALU.add)
                v_copy(qnr, CN[0:1, 0:2048])

                def stage1(qt, par):
                    sm = 32 * par
                    sbuf_sc = F1 if par == 0 else F3
                    kw = (qt + 1) * 128
                    nk4 = cdiv(kw, 512)
                    for i4 in range(nk4):
                        cw = min(512, kw - i4 * 512)
                        pp = i4 % 2
                        mm(PS[pp][:, :cw],
                           APx(M2[:], 0 + qt * 128, [[16384, 64], [1, 128]]),
                           APx(M2[:], 2048 + i4 * 512, [[16384, 64], [1, cw]]),
                           True, True)
                        mm(PS[2 + pp][:, :cw],
                           APx(M2[:], 5120 + qt * 128, [[16384, 64], [1, 128]]),
                           APx(M2[:], 2048 + i4 * 512, [[16384, 64], [1, cw]]),
                           True, False)
                        mm(PS[2 + pp][:, :cw], ones_bf[:],
                           APx(M2[:], 7168 + i4 * 512, [[16384, 1], [1, cw]]),
                           False, False)
                        mm(PS[2 + pp][:, :cw],
                           APx(M2[:], 9216 + qt * 128, [[16384, 1], [1, 128]]),
                           ones2_bf[:, 0:cw], False, True)
                        a_act(F2[:, i4 * 512:i4 * 512 + cw], PS[pp][:, :cw],
                              AF.Square)
                        if i4 % 3 == 2:
                            v_copy(F3[:, 2304:2304 + cw], PS[2 + pp][:, :cw])
                            v_recip(CN[:, i4 * 512:i4 * 512 + cw],
                                    F3[:, 2304:2304 + cw])
                        else:
                            a_act(F3[:, 2304:2304 + cw], PS[2 + pp][:, :cw],
                                  AF.Ln)
                            a_act(CN[:, i4 * 512:i4 * 512 + cw],
                                  F3[:, 2304:2304 + cw], AF.Exp, scale=-1.0)
                        v_stt(sbuf_sc[:, i4 * 512:i4 * 512 + cw],
                              F2[:, i4 * 512:i4 * 512 + cw], alpha_sc[:],
                              CN[:, i4 * 512:i4 * 512 + cw], ALU.mult,
                              ALU.mult)
                    v_copy(F2[:, 2560:2688], sbuf_sc[:, kw - 128:kw])
                    v_sel(sbuf_sc[:, kw - 128:kw], cmask[:], F2[:, 2560:2688],
                          negT[:])
                    v_red(SM[:, sm + 0:sm + 1], sbuf_sc[:, 0:kw], ALU.max)
                    v_ts(SM[:, sm + 1:sm + 2], SM[:, sm + 0:sm + 1], -1.0,
                         None, ALU.mult)

                def stage2(qt, par):
                    sm = 32 * par
                    sbuf_sc = F1 if par == 0 else F3
                    kw = (qt + 1) * 128
                    prb0 = par * 2048
                    ptb0 = 4096 + par * 2048
                    a_act(APx(W[:], prb0, [[32768, 128], [1, kw]]),
                          sbuf_sc[:, 0:kw], AF.Exp, bias=SM[:, sm + 1:sm + 2],
                          accum=SM[:, sm + 2:sm + 3])
                    v_recip(SM[:, sm + 3:sm + 4], SM[:, sm + 2:sm + 3])
                    v_ts(DG[:, par * 128:(par + 1) * 128], ident_bf[:],
                         SM[:, sm + 3:sm + 4], None, ALU.mult)
                    for kc in range(qt + 1):
                        pp = 4 + kc % 2
                        mm(PS[pp][:, :128],
                           APx(W[:], prb0 + kc * 128, [[32768, 128], [1, 128]]),
                           DG[:, par * 128:(par + 1) * 128], True, True)
                        dstp = APx(W[:], ptb0 + kc * 128,
                                   [[32768, 128], [1, 128]])
                        if kc % 2 == 0:
                            v_copy(dstp, PS[pp][:, :128])
                        else:
                            a_copy(dstp, PS[pp][:, :128])
                    for kc in range(qt + 1):
                        mm(PS[6][0:64, :128], vh[:, kc * 64:(kc + 1) * 64],
                           APx(W[:], ptb0 + kc * 128, [[32768, 128], [1, 128]]),
                           kc == 0, kc == qt)
                    v_copy(M1[rb:rb + 64,
                              dc * 2048 + qt * 128:dc * 2048 + qt * 128 + 128],
                           PS[6][0:64, :128])

                for qt in range(NST):
                    stage1(qt, qt % 2)
                    if qt >= 1:
                        stage2(qt - 1, (qt - 1) % 2)
                stage2(NST - 1, (NST - 1) % 2)

        def ph_l1_out(b):
            if "ctx1T" in dumps:
                for dc in range(8):
                    dma("g", sc["ctx1T"][b * D + dc * 128:b * D + dc * 128 + 128, :],
                        M1[:, dc * 2048:(dc + 1) * 2048])
            load_w_bf(W, 0, w["l1a_wo"], D, D)
            for st in range(NST):
                r0 = b * S + st * 128
                for nh2 in range(2):
                    pp = nh2
                    for dc in range(8):
                        mm(PS[pp][:, :512],
                           M1[:, dc * 2048 + st * 128:dc * 2048 + st * 128 + 128],
                           W[:, dc * 1024 + nh2 * 512:dc * 1024 + nh2 * 512 + 512],
                           dc == 0, dc == 7)
                    v_copy(CN[:, nh2 * 512:(nh2 + 1) * 512], PS[pp][:, :512])
                dma("act", sc["s_tm"][r0:r0 + 128, :], CN[:, 0:1024])

        if not stopped():
            for b in range(BC):
                ph_l1_proj(b)
        phase_done("l1_proj")
        if not stopped():
            for b in range(BC):
                ph_l1_attn(b)
                ph_l1_out(b)
        phase_done("l1a")

        def ph_l1f_x(b):
            for st in range(NST):
                r0 = b * S + st * 128
                dma("sp", F3[:, 1408:2432], sc["xn1_tm"][r0:r0 + 128, :])
                quant_fp4(F3[:, 1408:2432], Bq[:, 0:1024], 1024)
                t8_store(Bq[:, 0:1024], sc["xqT"], b * D, st * 128)

        def ph_l1f_h(b):
            for dcc in range(8):
                dma("sp", W[:, dcc * 2816:(dcc + 1) * 2816],
                    sc["wgate_qT"][dcc * 128:(dcc + 1) * 128, :])
                if dcc < 5:
                    dma("sp", M1[:, dcc * 2816:(dcc + 1) * 2816],
                        sc["wup_qT"][dcc * 128:(dcc + 1) * 128, :])
                else:
                    dma("sp", M2[:, (dcc - 5) * 2816:(dcc - 4) * 2816],
                        sc["wup_qT"][dcc * 128:(dcc + 1) * 128, :])

            def wup_ap(dcc, c0, cw):
                if dcc < 5:
                    return M1[:, dcc * 2816 + c0:dcc * 2816 + c0 + cw]
                return M2[:, (dcc - 5) * 2816 + c0:(dcc - 5) * 2816 + c0 + cw]

            for st in range(NST):
                r0 = b * S + st * 128
                hsrc = APx(sc["xqT"], (b * D) * S + st * 128,
                           [[S, 128], [128 * S, 8], [1, 128]])
                dma("sp", APx(HB2[:], 0, [[2048, 128], [128, 8], [1, 128]]), hsrc)
                for ncc in range(6):
                    cw = min(512, IFF - ncc * 512)
                    for dcc in range(8):
                        mm(PS[0][:, :cw], HB2[:, dcc * 128:(dcc + 1) * 128],
                           W[:, dcc * 2816 + ncc * 512:
                             dcc * 2816 + ncc * 512 + cw], dcc == 0, dcc == 7)
                    for dcc in range(8):
                        mm(PS[1][:, :cw], HB2[:, dcc * 128:(dcc + 1) * 128],
                           wup_ap(dcc, ncc * 512, cw), dcc == 0, dcc == 7)
                    a_act(CN[:, 0:cw], PS[0][:, :cw], AF.Silu)
                    v_tt(F3[:, ncc * 512:ncc * 512 + cw], CN[:, 0:cw],
                         PS[1][:, :cw], ALU.mult)
                quant_fp4(F3[:, 0:1408], Bq[:, 0:1408], 1408)
                quant_fp4(F3[:, 1408:2816], Bq[:, 1408:2816], 1408)
                for g2 in range(3):
                    n8 = min(8, 22 - g2 * 8)
                    for dc2 in range(n8):
                        di = g2 * 8 + dc2
                        trans(PB[0][:, :128], Bq[:, di * 128:(di + 1) * 128],
                              ident_bf[:])
                        v_copy(PTb[:, dc2 * 128:(dc2 + 1) * 128], PB[0][:, :128])
                    dsth = APx(sc["hqT"], (b * IFF + g2 * 8 * 128) * S + st * 128,
                               [[S, 128], [128 * S, n8], [1, 128]])
                    dma("act", dsth,
                        PTb[:, 0:n8 * 128].rearrange("p (c n) -> p c n", n=128))

        def ph_l1f_down(b):
            for dcc in range(22):
                dma("sp", W[:, dcc * 1024:(dcc + 1) * 1024],
                    sc["wdown_qT"][dcc * 128:(dcc + 1) * 128, :])
            for st in range(NST):
                r0 = b * S + st * 128
                hsrc = APx(sc["hqT"], (b * IFF) * S + st * 128,
                           [[S, 128], [128 * S, 22], [1, 128]])
                dma("sp", HB[:, 0:22 * 128].rearrange("p (c n) -> p c n", n=128),
                    hsrc)
                for nh2 in range(2):
                    pp = nh2
                    for dcc in range(22):
                        mm(PS[pp][:, :512], HB[:, dcc * 128:(dcc + 1) * 128],
                           W[:, dcc * 1024 + nh2 * 512:
                             dcc * 1024 + nh2 * 512 + 512],
                           dcc == 0, dcc == 21)
                    a_act(F2[:, 1024 + nh2 * 512:1024 + (nh2 + 1) * 512],
                          PS[pp][:, :512], AF.Silu)
                dma("sp", F1[:, 0:1024], sc["s_tm"][r0:r0 + 128, :])
                dma("sp", F3[:, 0:1024], sc["x1_tm"][r0:r0 + 128, :])
                v_tt(F2[:, 0:1024], F1[:, 0:1024], F2[:, 1024:2048], ALU.mult)
                v_tt(F3[:, 1024:2048], F3[:, 0:1024], F2[:, 0:1024], ALU.add)
                o = out[r0:r0 + 128, :]
                dma("g", o, F3[:, 1024:2048])
                ALL_OUT.append(o)

        if not stopped():
            for b in range(BC):
                ph_l1f_x(b)
        phase_done("l1f_x")
        if not stopped():
            for b in range(BC):
                ph_l1f_h(b)
                ph_l1f_down(b)
        phase_done("l1f")

        sch.add("g", lambda e: e.memset(zrow[:], 0.0), ins=ALL_OUT,
                outs=[zrow[:]], tag="fin")

        sch.finalize(DMA_SLOTS)

        @blk.tensor
        def _(t):
            sch.emit("pe", t, sems)

        @blk.vector
        def _(v):
            sch.emit("dve", v, sems)

        @blk.scalar
        def _(a):
            sch.emit("act", a, sems)

        @blk.sync
        def _(s):
            sch.emit("sp", s, sems)

        @blk.gpsimd
        def _(g):
            sch.emit("g", g, sems)

    return nc


def _host_consts():
    c = {}
    c["c_ident"] = np.eye(128, dtype=np.float32)
    A = np.zeros((128, 128), dtype=np.float32)
    for i in range(64):
        A[2 * i, 2 * i + 1] = -1.0
        A[2 * i + 1, 2 * i] = 1.0
    c["c_rotA"] = np.ascontiguousarray(A.T)
    inv0 = 1.0 / (10000.0 ** (np.arange(0, D, 2, dtype=np.float64) / D))
    t = np.arange(S, dtype=np.float64)
    ang0 = np.outer(inv0, t)
    c["c_cos0"] = np.repeat(np.cos(ang0), 2, axis=0).astype(np.float32)
    c["c_sin0"] = np.repeat(np.sin(ang0), 2, axis=0).astype(np.float32)
    inv1 = 1.0 / (10000.0 ** (np.arange(0, DH, 2, dtype=np.float64) / DH))
    ang1 = np.outer(inv1, t)
    c["c_cos1"] = np.tile(np.repeat(np.cos(ang1), 2, axis=0), (NH, 1)).astype(
        np.float32)
    c["c_sin1"] = np.tile(np.repeat(np.sin(ang1), 2, axis=0), (NH, 1)).astype(
        np.float32)
    pos = (S - 1.0) - np.arange(S, dtype=np.float64)
    angp = np.outer(inv0, pos)
    c["c_pet"] = np.concatenate([np.sin(angp), np.cos(angp)], axis=0).astype(
        np.float32)
    hmv = np.zeros((128, 8 * NH), dtype=np.float32)
    for dc in range(8):
        hmv[0:64, dc * NH + 2 * dc] = 1.0
        hmv[64:128, dc * NH + 2 * dc + 1] = 1.0
    c["c_hm"] = hmv
    c["c_cmask"] = np.tril(np.ones((128, 128), dtype=np.uint8))
    c["c_ones"] = np.ones((1, 128), dtype=np.float32)
    c["c_onesf"] = np.ones((1, 128), dtype=np.float32)
    c["c_ones2"] = np.ones((1, 512), dtype=np.float32)
    return c


def _host_weights(inp):
    f32 = np.float32
    a = lambda x: np.ascontiguousarray(np.asarray(x, dtype=f32))
    w = {}
    w["l0a_wq_t"] = a(np.asarray(inp["L0_attn_wq"]).T)
    w["l0a_wk_t"] = a(np.asarray(inp["L0_attn_wk"]).T)
    w["l0a_wv_t"] = a(np.asarray(inp["L0_attn_wv"]).T)
    w["l0a_wo_t"] = a(np.asarray(inp["L0_attn_wo"]).T)
    w["l0f_wqkv"] = a(inp["L0_ffn_Wqkv"])
    w["l0f_wo"] = a(inp["L0_ffn_Wo"])
    w["l0f_wr"] = a(inp["L0_ffn_Wr"])
    w["l0f_ubt"] = a(np.asarray(inp["L0_ffn_u_bias"]).T)
    w["l0f_vbt"] = a(np.asarray(inp["L0_ffn_v_bias"]).T)
    for nm in ("ln1", "ln2", "ln3"):
        w[f"l0f_{nm}g"] = a(inp[f"L0_ffn_{nm}_g"])
        w[f"l0f_{nm}b"] = a(inp[f"L0_ffn_{nm}_b"])
    w["l0f_wff1"] = a(inp["L0_ffn_Wff1"])
    w["l0f_bff1"] = a(np.asarray(inp["L0_ffn_bff1"]).reshape(FF // 128, 128).T)
    w["l0f_wff2"] = a(inp["L0_ffn_Wff2"])
    w["l0f_bff2"] = a(np.asarray(inp["L0_ffn_bff2"]).reshape(1, D))
    w["l0f_wg"] = a(inp["L0_ffn_Wg"])
    w["l0f_we1"] = a(inp["L0_ffn_We1"])
    w["l0f_be1"] = a(np.asarray(inp["L0_ffn_be1"]).reshape(NE, FF // 128, 128)
                     .transpose(0, 2, 1))
    w["l0f_we2"] = a(inp["L0_ffn_We2"])
    w["l0f_be2"] = a(np.asarray(inp["L0_ffn_be2"]).reshape(NE, 1, D))
    w["l0f_wout"] = a(inp["L0_ffn_Wout"])
    w["l1a_wq"] = a(inp["L1_attn_Wq"])
    w["l1a_wk"] = a(inp["L1_attn_Wk"])
    w["l1a_wv"] = a(inp["L1_attn_Wv"])
    w["l1a_wo"] = a(inp["L1_attn_Wo"])
    w["l1a_alpha"] = a(np.asarray(inp["L1_attn_alpha"]).reshape(1, 1))
    w["l1f_wgate"] = a(inp["L1_ffn_w_gate"])
    w["l1f_wup"] = a(inp["L1_ffn_w_up"])
    w["l1f_wdown"] = a(inp["L1_ffn_w_down"])
    return w


def kernel(**inputs):
    nc = build_nc()
    consts = _host_consts()
    weights = _host_weights(inputs)
    x = np.ascontiguousarray(np.asarray(inputs["L0_attn_x"], dtype=np.float32))
    in_maps = []
    for c in range(8):
        m = {"x": x[c * BC:(c + 1) * BC].reshape(BC * S, D)}
        m.update(weights)
        m.update(consts)
        in_maps.append(m)
    res = run_bass_kernel_spmd(nc, in_maps, core_ids=list(range(8)))
    outs = [np.asarray(r["out"]).reshape(BC, S, D) for r in res.results]
    return np.concatenate(outs, axis=0)


# version 07
# speedup vs baseline: 1.2828x; 1.2828x over previous
import math
import numpy as np
import ml_dtypes
import bass_rust
import concourse.bass as bass
import concourse.mybir as mybir
from concourse.bass_utils import run_bass_kernel_spmd
from contextlib import ExitStack

F32 = mybir.dt.float32
BF16 = mybir.dt.bfloat16
I16 = mybir.dt.int16
U8 = mybir.dt.uint8
AF = mybir.ActivationFunctionType
ALU = mybir.AluOpType
AX = mybir.AxisListType

BC, S, D = 2, 2048, 1024
NH, DH, FF, NE, IFF = 16, 64, 4096, 8, 2816
RS = 2176
NEG = -1.0e30
NST = S // 128

DEBUG_DUMPS = ()
DEBUG_STOP = None
DEBUG_L1_HEADS = NH
DEBUG_L1_MODE = 9
SERIAL_READS = False

SB_GRAN = 512
DR_GRAN = 1 << 18
GRAN_MAP = {"SM": 16, "SM3": 16}
_PSUM_NAMES = {f"ps{i}" for i in range(8)} | {"pb0", "pb1"}


def _keys(ap):
    t = ap.tensor
    name = t.name
    a = ap.ap
    if str(ap.space) == "MemorySpace.PSUM":
        return (name,)
    if str(ap.space) == "MemorySpace.DRAM":
        off = ap.offset
        span = 1
        for st, cnt in a:
            span += abs(st) * (cnt - 1)
        lo = off // DR_GRAN
        hi = (off + span - 1) // DR_GRAN
        return tuple(f"{name}:{i}" for i in range(lo, hi + 1))
    pitch = a[0][0] if len(a) > 0 else 1
    off = ap.offset % pitch if pitch > 0 else ap.offset
    span = 1
    for st, cnt in a[1:]:
        span += abs(st) * (cnt - 1)
    g = GRAN_MAP.get(name, SB_GRAN)
    lo = off // g
    hi = (off + span - 1) // g
    return tuple(f"{name}:{i}" for i in range(lo, hi + 1))


class _Op:
    __slots__ = ("eng", "fn", "dma", "idx", "deps", "signals", "cnt", "tag",
                 "slot")


class Sched:
    def __init__(self):
        self.ops = []
        self.res = {}

    def add(self, eng, fn, ins=(), outs=(), dma=False, tag=""):
        op = _Op()
        op.eng = eng; op.fn = fn; op.dma = dma; op.tag = tag
        op.idx = len(self.ops); op.signals = False; op.cnt = 0
        deps = {}

        def dep(p):
            if p is None or p is op:
                return
            if (not p.dma) and (not op.dma) and p.eng == "pe" and eng == "pe":
                return
            if p.dma:
                k = ("D", p.eng, p.idx)
            else:
                k = (p.eng, False)
            cur = deps.get(k)
            if cur is None or p.idx > cur.idx:
                deps[k] = p

        rkeys = []
        wkeys = []
        for ap_ in ins:
            for k in _keys(ap_):
                if SERIAL_READS or k in _PSUM_NAMES:
                    wkeys.append(k)
                else:
                    rkeys.append(k)
        for ap_ in outs:
            wkeys.extend(_keys(ap_))
        for r in rkeys:
            st = self.res.get(r)
            if st is not None:
                dep(st[0])
        for wk in wkeys:
            st = self.res.get(wk)
            if st is not None:
                dep(st[0])
                for rr in st[1]:
                    dep(rr)
        op.deps = list(deps.values())
        for p in op.deps:
            p.signals = True
        for r in rkeys:
            st = self.res.get(r)
            if st is None:
                st = self.res[r] = [None, []]
            st[1].append(op)
        for wk in wkeys:
            self.res[wk] = [op, []]
        self.ops.append(op)
        return op

    def finalize(self, dma_slots):
        cnt = {}
        for op in self.ops:
            if op.dma:
                op.signals = True
                nq = cnt.get(op.eng, 0)
                cnt[op.eng] = nq + 1
                R = dma_slots[op.eng]
                op.slot = nq % R
                op.cnt = nq // R + 1
            elif op.signals:
                k = (op.eng, False)
                cnt[k] = cnt.get(k, 0) + 1
                op.cnt = cnt[k]

    def emit(self, eng, handle, sems):
        hwm = {}
        for op in self.ops:
            if op.eng != eng:
                continue
            if op.dma and op.cnt > 1:
                k = ("D", op.eng, op.slot)
                val = (op.cnt - 1) * 16
                if hwm.get(k, 0) < val:
                    handle.wait_ge(sems[k], val)
                    hwm[k] = val
            for p in op.deps:
                if p.dma:
                    k = ("D", p.eng, p.slot)
                    val = p.cnt * 16
                else:
                    k = (p.eng, False)
                    val = p.cnt
                if hwm.get(k, 0) < val:
                    handle.wait_ge(sems[k], val)
                    hwm[k] = val
            inst = op.fn(handle)
            if op.signals:
                if op.dma:
                    inst.then_inc(sems[("D", op.eng, op.slot)], 16)
                else:
                    inst.then_inc(sems[(op.eng, False)], 1)


def APx(base_ap, offset, pattern):
    return bass_rust.AP(base_ap.tensor, offset, pattern)


def cdiv(a, b):
    return (a + b - 1) // b


def build_nc():
    nc = bass.Bass("TRN2", target_bir_lowering=False, debug=False)
    dumps = set(DEBUG_DUMPS)

    def din(name, shape, dtype=F32):
        return nc.dram_tensor(name, shape, dtype, kind="ExternalInput").ap()

    w = {}
    w["x"] = din("x", [BC * S, D])
    for item in [
        ("l0a_wq_t", [D, D]), ("l0a_wk_t", [D, D]), ("l0a_wv_t", [D, D]),
        ("l0a_wo_t", [D, D]),
        ("l0f_wqkv", [D, 3 * D]), ("l0f_wo", [D, D]), ("l0f_wr", [D, D]),
        ("l0f_ubt", [DH, NH]), ("l0f_vbt", [DH, NH]),
        ("l0f_ln1g", [D]), ("l0f_ln1b", [D]), ("l0f_ln2g", [D]), ("l0f_ln2b", [D]),
        ("l0f_ln3g", [D]), ("l0f_ln3b", [D]),
        ("l0f_wff1", [D, FF]), ("l0f_bff1", [128, FF // 128]),
        ("l0f_wff2", [FF, D]), ("l0f_bff2", [1, D]),
        ("l0f_wg", [D, NE]),
        ("l0f_we1", [NE, D, FF]), ("l0f_be1", [NE, 128, FF // 128]),
        ("l0f_we2", [NE, FF, D]), ("l0f_be2", [NE, 1, D]),
        ("l0f_wout", [D, D]),
        ("l1a_wq", [D, D]), ("l1a_wk", [D, D]), ("l1a_wv", [D, D]),
        ("l1a_wo", [D, D]), ("l1a_alpha", [1, 1]),
        ("l1f_wgate", [IFF, D]), ("l1f_wup", [IFF, D]), ("l1f_wdown", [D, IFF]),
        ("c_ident", [128, 128]), ("c_rotA", [128, 128]),
        ("c_cos0", [D, S]), ("c_sin0", [D, S]),
        ("c_cos1", [D, S]), ("c_sin1", [D, S]),
        ("c_pet", [D, S]), ("c_hm", [128, 8 * NH]),
        ("c_cmask", [128, 128], U8), ("c_ones", [1, 128]), ("c_onesf", [1, 128]), ("c_ones2", [1, 512]),
    ]:
        nm, shp = item[0], item[1]
        w[nm] = din(nm, shp, item[2] if len(item) > 2 else F32)

    out = nc.dram_tensor("out", [BC * S, D], F32, kind="ExternalOutput").ap()

    sc = {}

    def S_(name, shape, dtype=F32):
        kind = "ExternalOutput" if name in dumps else "Internal"
        sc[name] = nc.dram_tensor(name, shape, dtype, kind=kind).ap()

    S_("xn_tm", [BC * S, D]); S_("xnT", [BC * D, S], BF16)
    S_("l0a_qT", [BC * D, S], BF16); S_("l0a_kT", [BC * D, S], BF16)
    S_("l0a_v", [BC * S, D], BF16)
    S_("s_tm", [BC * S, D])
    S_("l0f_qT", [BC * D, S], BF16); S_("l0f_kT", [BC * D, S], BF16)
    S_("l0f_v", [BC * S, D], BF16); S_("rT", [D, S], BF16)
    S_("bd0", [128 * RS], BF16); S_("bd1", [128 * RS], BF16)
    S_("h1_tm", [BC * S, D]); S_("h1T", [BC * D, S], BF16)
    S_("h2_tm", [BC * S, D]); S_("h2T", [BC * D, S], BF16)
    S_("m_tm", [BC * S, NE]); S_("moe_tm", [BC * S, D])
    S_("x1_tm", [BC * S, D])
    S_("xn1_tm", [BC * S, D]); S_("xn1T", [BC * D, S], BF16)
    S_("l1_qT", [BC * D, S], BF16); S_("l1_kT", [BC * D, S], BF16)
    S_("l1_v", [BC * S, D], BF16)
    S_("l1_qn", [BC * NH, S]); S_("l1_kn", [BC * NH, S])
    S_("l1_qnT", [BC * S, NH])
    S_("wgate_qT", [D, IFF], BF16); S_("wup_qT", [D, IFF], BF16)
    S_("wdown_qT", [IFF, D], BF16)
    S_("hqT", [BC * IFF, S], BF16); S_("xqT", [BC * D, S], BF16)
    S_("ctx0T", [BC * D, S], BF16); S_("ctx1T", [BC * D, S], BF16)

    sch = Sched()

    with ExitStack() as ctx:
        def sb(name, shape, dtype=F32):
            return ctx.enter_context(nc.sbuf_tensor(name, shape, dtype))

        M1 = sb("M1", [128, 16384], BF16)
        M2 = sb("M2", [128, 16384], BF16)
        W = sb("W", [128, 32768], BF16)
        F1 = sb("F1", [128, 2048])
        F2 = sb("F2", [128, 2816])
        F3 = sb("F3", [128, 2816])
        CN = sb("CN", [128, 2048])
        Bq = sb("Bq", [128, 2816], BF16)
        PTb = sb("PTb", [128, 2048], BF16)
        HB = sb("HB", [128, 4096], BF16)
        HB2 = sb("HB2", [128, 2048], BF16)
        IQ = sb("IQ", [128, 1408], I16)
        U8T = sb("U8T", [128, 1408], U8)
        U8b = sb("U8b", [128, 1408], U8)
        LGB = sb("LGB", [128, 2048], BF16)
        SM = sb("SM", [128, 64])
        SM3 = sb("SM3", [128, 64])
        DG = sb("DG", [128, 256], BF16)
        MME = sb("MME", [128, 2 * NE])
        QNS = sb("QNS", [128, 16])
        WGS = sb("WGS", [128, 8 * NE], BF16)
        BR = sb("BR", [1, 1024], BF16)
        ident_bf = sb("ident_bf", [128, 128], BF16)
        ident_f = sb("ident_f", [128, 128], F32)
        rotA = sb("rotA", [128, 128], BF16)
        cmask = sb("cmask", [128, 128], U8)
        negT = sb("negT", [128, 128])
        ones_bf = sb("ones_bf", [1, 128], BF16)
        ones_f = sb("ones_f", [1, 128], F32)
        ones2_bf = sb("ones2_bf", [1, 512], BF16)
        ubt = sb("ubt", [DH, NH])
        vbt = sb("vbt", [DH, NH])
        hm = sb("hm", [128, 8 * NH], BF16)
        bff1_sb = sb("bff1_sb", [128, FF // 128])
        be1_sb = sb("be1_sb", [128, FF // 128])
        alpha_sc = sb("alpha_sc", [128, 1])
        lnsc = sb("lnsc", [1, 1])
        arow = sb("arow", [1, 1])
        lnrow = sb("lnrow", [1, D], BF16)
        zrow = sb("zrow", [128, 1])
        eps1 = sb("eps1", [128, 1])

        PS = [ctx.enter_context(nc.psum_tensor(f"ps{i}", [128, 512], F32))
              for i in range(7)]
        PB = [ctx.enter_context(nc.psum_tensor("pb0", [128, 1024], BF16))
              for i in range(1)]

        sems = {}
        for e in ("pe", "dve", "act", "sp", "g"):
            sems[(e, False)] = ctx.enter_context(nc.semaphore(f"sc_{e}"))
        DMA_SLOTS = {"sp": 36, "g": 36, "act": 12}
        for e, R in DMA_SLOTS.items():
            for i in range(R):
                sems[("D", e, i)] = ctx.enter_context(nc.semaphore(f"sd_{e}{i}"))

        blk = ctx.enter_context(nc.Block())

        def _outer16(ap_):
            for st_, cnt_ in ap_.ap:
                if cnt_ != 1:
                    return cnt_ % 16 == 0
            return False

        def dma(q, out_ap, in_ap, tag=""):
            if not (_outer16(out_ap) and _outer16(in_ap)):
                q = "act"
            elif q == "act":
                q = "g"
            sch.add(q, lambda e, o=out_ap, i=in_ap: e.dma_start(out=o, in_=i),
                    ins=[in_ap], outs=[out_ap], dma=True, tag=tag)

        def dma_acc(out_ap, in_ap, tag=""):
            sch.add("g", lambda e, o=out_ap, i=in_ap:
                    e.dma_start(out=o, in_=i, accum_op=ALU.add),
                    ins=[in_ap, out_ap], outs=[out_ap], dma=True, tag=tag)

        def mm(ps_ap, lhsT, rhs, start, stop, tag=""):
            sch.add("pe", lambda e, o=ps_ap, l=lhsT, r=rhs, s1=start, s2=stop:
                    e.matmul(o, l, r, start=s1, stop=s2),
                    ins=[lhsT, rhs], outs=[ps_ap], tag=tag)

        def trans(ps_ap, in_ap, id_ap, tag=""):
            sch.add("pe", lambda e, o=ps_ap, i=in_ap, d=id_ap: e.transpose(o, i, d),
                    ins=[in_ap, id_ap], outs=[ps_ap], tag=tag)

        def v_ts(out_ap, in_ap, s1, s2, op0, op1=None, accum=None, tag=""):
            ins = [in_ap] + [s for s in (s1, s2) if hasattr(s, "tensor")]
            outs = [out_ap] + ([accum] if accum is not None else [])
            if op1 is None and s2 is None and accum is None:
                sch.add("dve", lambda e, o=out_ap, i=in_ap, a=s1, p0=op0:
                        e.tensor_scalar(o, i, a, None, op0=p0),
                        ins=ins, outs=outs, tag=tag)
            elif accum is None:
                sch.add("dve", lambda e, o=out_ap, i=in_ap, a=s1, b=s2, p0=op0,
                        p1=op1: e.tensor_scalar(o, i, a, b, op0=p0, op1=p1),
                        ins=ins, outs=outs, tag=tag)
            else:
                sch.add("dve", lambda e, o=out_ap, i=in_ap, a=s1, b=s2, p0=op0,
                        p1=op1, ac=accum:
                        e.tensor_scalar(o, i, a, b, op0=p0, op1=p1, accum_out=ac),
                        ins=ins, outs=outs, tag=tag)

        def v_tt(out_ap, a_ap, b_ap, op, tag=""):
            sch.add("dve", lambda e, o=out_ap, a=a_ap, b=b_ap, p=op:
                    e.tensor_tensor(o, a, b, op=p),
                    ins=[a_ap, b_ap], outs=[out_ap], tag=tag)

        def v_stt(out_ap, in0, scal, in1, op0, op1, tag=""):
            ins = [in0, in1] + ([scal] if hasattr(scal, "tensor") else [])
            sch.add("dve", lambda e, o=out_ap, a=in0, s=scal, b=in1, p0=op0, p1=op1:
                    e.scalar_tensor_tensor(o, a, s, b, op0=p0, op1=p1),
                    ins=ins, outs=[out_ap], tag=tag)

        def v_copy(out_ap, in_ap, tag=""):
            sch.add("dve", lambda e, o=out_ap, i=in_ap: e.tensor_copy(o, i),
                    ins=[in_ap], outs=[out_ap], tag=tag)

        def v_red(out_ap, in_ap, op, absv=None, tag=""):
            sch.add("dve", lambda e, o=out_ap, i=in_ap, p=op, av=absv:
                    e.tensor_reduce(o, i, axis=AX.X, op=p, apply_absolute_value=av),
                    ins=[in_ap], outs=[out_ap], tag=tag)

        def v_recip(out_ap, in_ap, tag=""):
            sch.add("dve", lambda e, o=out_ap, i=in_ap: e.reciprocal(o, i),
                    ins=[in_ap], outs=[out_ap], tag=tag)

        def g_ts(out_ap, in_ap, s1, s2, op0, op1=None, tag=""):
            ins = [in_ap] + [x for x in (s1, s2) if hasattr(x, "tensor")]
            if op1 is None and s2 is None:
                sch.add("g", lambda e, o=out_ap, i=in_ap, a=s1, p0=op0:
                        e.tensor_scalar(o, i, a, None, op0=p0),
                        ins=ins, outs=[out_ap], tag=tag)
            else:
                sch.add("g", lambda e, o=out_ap, i=in_ap, a=s1, b=s2, p0=op0,
                        p1=op1: e.tensor_scalar(o, i, a, b, op0=p0, op1=p1),
                        ins=ins, outs=[out_ap], tag=tag)

        def g_tt(out_ap, a_ap, b_ap, op, tag=""):
            sch.add("g", lambda e, o=out_ap, a=a_ap, b=b_ap, p=op:
                    e.tensor_tensor(o, a, b, op=p),
                    ins=[a_ap, b_ap], outs=[out_ap], tag=tag)

        def v_sel(out_ap, mask, t_ap, f_ap, tag=""):
            sch.add("dve", lambda e, o=out_ap, m=mask, a=t_ap, b=f_ap:
                    e.select(o, m, a, b), ins=[mask, t_ap, f_ap], outs=[out_ap],
                    tag=tag)

        def v_memset(ap_, val, tag=""):
            sch.add("dve", lambda e, o=ap_, v=val: e.memset(o, v), ins=[],
                    outs=[ap_], tag=tag)

        def a_copy(out_ap, in_ap, tag=""):
            sch.add("act", lambda e, o=out_ap, i=in_ap: e.activation(
                o, i, AF.Identity), ins=[in_ap], outs=[out_ap], tag=tag)

        def a_act(out_ap, in_ap, func, bias=None, scale=None, accum=None, tag=""):
            ins = [in_ap]
            if bias is not None and hasattr(bias, "tensor"):
                ins.append(bias)
            outs = [out_ap] + ([accum] if accum is not None else [])
            kw = {}
            if bias is not None:
                kw["bias"] = bias
            if scale is not None:
                kw["scale"] = scale

            def f(e, o=out_ap, i=in_ap, fc=func, kw=kw, ac=accum):
                if ac is not None:
                    return e.activation(o, i, fc, accum_out=ac, **kw)
                return e.activation(o, i, fc, **kw)
            sch.add("act", f, ins=ins, outs=outs, tag=tag)

        ALL_OUT = []

        stop_hit = [False]

        def stopped():
            return stop_hit[0]

        def phase_done(name):
            if DEBUG_STOP is not None and name == DEBUG_STOP:
                stop_hit[0] = True

        dma("g", ident_bf[:], w["c_ident"])
        dma("sp", ident_f[:], w["c_ident"])
        dma("g", rotA[:], w["c_rotA"])
        dma("sp", cmask[:], w["c_cmask"])
        dma("g", ones_bf[:], w["c_ones"])
        dma("sp", ones_f[:], w["c_onesf"])
        dma("g", ones2_bf[:], w["c_ones2"])
        dma("sp", ubt[:], w["l0f_ubt"])
        dma("sp", vbt[:], w["l0f_vbt"])
        dma("g", hm[:], w["c_hm"])
        dma("sp", bff1_sb[:], w["l0f_bff1"])
        v_memset(negT[:], NEG)
        v_memset(zrow[:], 0.0)
        v_memset(eps1[:], 1e-5)
        s0 = float(math.sqrt(DH) / math.log(1.0 + DH))
        dma("sp", arow[:], w["l1a_alpha"])
        a_act(lnsc[:], arow[:], AF.Exp, scale=math.log(s0))
        mm(PS[5][:, :1], ones_f[:], lnsc[:], True, True)
        v_copy(alpha_sc[:], PS[5][:, :1])

        def load_lgb(gnm, bnm):
            for j, nm in enumerate((gnm, bnm)):
                dma("g", lnrow[:], w[nm].unsqueeze(0))
                for h in range(2):
                    mm(PS[5][:, :512], ones_bf[:], lnrow[:, h * 512:(h + 1) * 512],
                       True, True)
                    v_copy(LGB[:, j * 1024 + h * 512:j * 1024 + (h + 1) * 512],
                           PS[5][:, :512])

        def ln_tile(src, dst, smbase, affine=False):
            c = smbase
            v_red(SM[:, c:c + 1], src, ALU.add)
            a_act(CN[:, 0:1024], src, AF.Square, accum=SM[:, c + 1:c + 2])
            v_ts(SM[:, c + 2:c + 3], SM[:, c:c + 1], 1.0 / D, None, ALU.mult)
            v_tt(SM[:, c + 3:c + 4], SM[:, c + 2:c + 3], SM[:, c + 2:c + 3],
                 ALU.mult)
            v_ts(SM[:, c + 4:c + 5], SM[:, c + 1:c + 2], 1.0 / D,
                 SM[:, c + 3:c + 4], ALU.mult, ALU.subtract)
            a_act(SM[:, c + 5:c + 6], SM[:, c + 4:c + 5], AF.Ln, bias=eps1[:])
            a_act(SM[:, c + 6:c + 7], SM[:, c + 5:c + 6], AF.Exp, scale=-0.5)
            v_tt(SM[:, c + 7:c + 8], SM[:, c + 2:c + 3], SM[:, c + 6:c + 7],
                 ALU.mult)
            if not affine:
                v_ts(dst, src, SM[:, c + 6:c + 7], SM[:, c + 7:c + 8],
                     ALU.mult, ALU.subtract)
            else:
                v_ts(CN[:, 1024:2048], src, SM[:, c + 6:c + 7],
                     SM[:, c + 7:c + 8], ALU.mult, ALU.subtract)
                v_tt(CN[:, 0:1024], CN[:, 1024:2048], LGB[:, 0:1024], ALU.mult)
                v_tt(dst, CN[:, 0:1024], LGB[:, 1024:2048], ALU.add)

        def t8_store(src_bf, dram_ap, rowbase, col, nch=8):
            for dc in range(nch):
                trans(PB[0][:, :128], src_bf[:, dc * 128:(dc + 1) * 128],
                      ident_bf[:])
                v_copy(PTb[:, dc * 128:(dc + 1) * 128], PB[0][:, :128])
            dst = APx(dram_ap, rowbase * S + col,
                      [[S, 128], [128 * S, nch], [1, 128]])
            srcv = PTb[:, 0:nch * 128].rearrange("p (c n) -> p c n", n=128)
            dma("act", dst, srcv)

        def ph_ln_in(b, src_ap, dst_tm, dst_T):
            for st in range(NST):
                r0 = b * S + st * 128
                p2 = st % 2
                xin = F2[:, p2 * 1024:(p2 + 1) * 1024]
                xo = F3[:, p2 * 1024:(p2 + 1) * 1024]
                dma("sp", xin, src_ap[r0:r0 + 128, :])
                ln_tile(xin, xo, p2 * 16, affine=False)
                dma("g", dst_tm[r0:r0 + 128, :], xo)
                a_act(Bq[:, 0:1024], xo, AF.Identity)
                t8_store(Bq[:, 0:1024], dst_T, b * D, st * 128)

        for b in range(BC):
            ph_ln_in(b, w["x"], sc["xn_tm"], sc["xnT"])
        phase_done("ln0")

        def quant_fp4(src, dst_bf, wd, alt=False):
            nb = wd // 16
            bmraw = CN[:, 1408:1408 + nb]
            bmc = CN[:, 1536:1536 + nb]
            rec = CN[:, 1792:1792 + nb]
            src3 = src.rearrange("p (b k) -> p b k", k=16)

            def bc16(ap2d):
                a = ap2d.ap
                return APx(ap2d, ap2d.offset, [[a[0][0], 128], [a[1][0], nb],
                                               [0, 16]])
            if alt:
                T = F1[:, 0:wd]
                SG = F3[:, 1408:1408 + wd]
            else:
                T = F2[:, 0:wd]
                SG = F1[:, 0:wd]
            SQ = CN[:, 0:wd]
            v_red(bmraw, src3, ALU.max, absv=True)
            v_ts(bmc, bmraw, 1.0 / 6.0, 1e-12, ALU.mult, ALU.max)
            v_recip(rec, bmc)
            t3 = T.rearrange("p (b k) -> p b k", k=16)
            v_tt(t3, src3, bc16(rec), ALU.mult)
            a_act(SQ, T, AF.Square)
            a_act(SG, T, AF.Sign)
            v_ts(IQ[:, 0:wd], T, 2.0, None, ALU.mult)
            v_ts(src, IQ[:, 0:wd], 0.5, None, ALU.mult)
            v_ts(U8T[:, 0:wd], SQ, 5.0625, None, ALU.is_lt)
            v_ts(U8b[:, 0:wd], SQ, 20.25, None, ALU.is_lt)
            v_ts(IQ[:, 0:wd], T, 1.0, None, ALU.mult)
            v_ts(T, SQ, 25.0, None, ALU.is_gt)
            v_stt(SQ, T, 2.0, SG, ALU.mult, ALU.mult)
            v_ts(T, SG, 4.0, None, ALU.mult)
            v_tt(SG, SQ, T, ALU.add)
            v_copy(T, IQ[:, 0:wd])
            v_sel(SQ, U8b[:, 0:wd], T, SG)
            v_sel(T, U8T[:, 0:wd], src, SQ)
            d3 = dst_bf.rearrange("p (b k) -> p b k", k=16)
            v_tt(d3, t3, bc16(bmc), ALU.mult)

        def emit_wq_tile(src_dram, n_rows, kdim, dst_dram, it):
            nh = cdiv(kdim, 1408)
            for hh in range(nh):
                c0 = hh * 1408
                cw = min(1408, kdim - c0)
                dma("sp", F3[:, c0:c0 + cw],
                    src_dram[it * 128:(it + 1) * 128, c0:c0 + cw])
                quant_fp4(F3[:, c0:c0 + cw], Bq[:, c0:c0 + cw], cw,
                          alt=(kdim == D))
            nch = kdim // 128
            for g2 in range(cdiv(nch, 8)):
                n8 = min(8, nch - g2 * 8)
                for dc in range(n8):
                    di = g2 * 8 + dc
                    trans(PB[0][:, :128], Bq[:, di * 128:(di + 1) * 128],
                          ident_bf[:])
                    v_copy(PTb[:, dc * 128:(dc + 1) * 128], PB[0][:, :128])
                dst = APx(dst_dram, (g2 * 8 * 128) * n_rows + it * 128,
                          [[n_rows, 128], [128 * n_rows, n8], [1, 128]])
                dma("act", dst,
                    PTb[:, 0:n8 * 128].rearrange("p (c n) -> p c n", n=128))

        WQ_TASKS = []
        for args_ in ((w["l1f_wgate"], IFF, D, sc["wgate_qT"]),
                      (w["l1f_wup"], IFF, D, sc["wup_qT"]),
                      (w["l1f_wdown"], D, IFF, sc["wdown_qT"])):
            for it in range(args_[1] // 128):
                WQ_TASKS.append(args_ + (it,))
        WQ_TASKS.reverse()
        phase_done("wquant")

        def load_xnT_M1(b, src):
            for dc in range(8):
                dma("sp", M1[:, dc * 2048:(dc + 1) * 2048],
                    src[b * D + dc * 128:b * D + dc * 128 + 128, :])

        def load_w_bf(arena, base, src_dram, rows, cols):
            for dc in range(rows // 128):
                dma("g", arena[:, base + dc * cols:base + (dc + 1) * cols],
                    src_dram[dc * 128:(dc + 1) * 128, :])

        TRIGC = 24576
        TRIGS = 26624

        def proj_rope(b, wbase, cosd, sind, dstd, qnpsum=None):
            for ec in range(8):
                dma("g", W[:, TRIGC:TRIGC + 2048], cosd[ec * 128:(ec + 1) * 128, :])
                dma("g", W[:, TRIGS:TRIGS + 2048], sind[ec * 128:(ec + 1) * 128, :])
                for s4 in range(4):
                    pp = s4 % 2
                    for dc in range(8):
                        mm(PS[pp][:, :512],
                           W[:, wbase + dc * 1024 + ec * 128:
                             wbase + dc * 1024 + ec * 128 + 128],
                           M1[:, dc * 2048 + s4 * 512:dc * 2048 + (s4 + 1) * 512],
                           dc == 0, dc == 7)
                    v_copy(Bq[:, s4 * 512:(s4 + 1) * 512], PS[pp][:, :512])
                    mm(PS[pp][:, :512], rotA[:], Bq[:, s4 * 512:(s4 + 1) * 512],
                       True, True)
                    v_tt(F2[:, s4 * 512:(s4 + 1) * 512], PS[pp][:, :512],
                         W[:, TRIGS + s4 * 512:TRIGS + (s4 + 1) * 512], ALU.mult)
                    v_tt(F3[:, s4 * 512:(s4 + 1) * 512],
                         Bq[:, s4 * 512:(s4 + 1) * 512],
                         W[:, TRIGC + s4 * 512:TRIGC + (s4 + 1) * 512], ALU.mult)
                    v_tt(PTb[:, s4 * 512:(s4 + 1) * 512],
                         F2[:, s4 * 512:(s4 + 1) * 512],
                         F3[:, s4 * 512:(s4 + 1) * 512], ALU.add)
                    if qnpsum is not None:
                        a_act(HB[:, s4 * 512:(s4 + 1) * 512],
                              PTb[:, s4 * 512:(s4 + 1) * 512], AF.Square)
                        mm(PS[2 + s4][0:16, :512], hm[:, ec * 16:(ec + 1) * 16],
                           HB[:, s4 * 512:(s4 + 1) * 512], ec == 0, ec == 7)
                dma("act", dstd[b * D + ec * 128:b * D + ec * 128 + 128, :], PTb[:])

        def proj_v(b, rhs_fn, dstd):
            for st in range(NST):
                p2 = st % 2
                for nh2 in range(2):
                    pp = nh2
                    for dc in range(8):
                        mm(PS[pp][:, :512],
                           M1[:, dc * 2048 + st * 128:dc * 2048 + st * 128 + 128],
                           rhs_fn(dc, nh2), dc == 0, dc == 7)
                    v_copy(HB[:, p2 * 1024 + nh2 * 512:p2 * 1024 + (nh2 + 1) * 512],
                           PS[pp][:, :512])
                dma("act", dstd[b * S + st * 128:b * S + st * 128 + 128, :],
                    HB[:, p2 * 1024:(p2 + 1) * 1024])

        def ph_l0a_proj(b):
            load_xnT_M1(b, sc["xnT"])
            load_w_bf(W, 0, w["l0a_wq_t"], D, D)
            load_w_bf(W, 8192, w["l0a_wk_t"], D, D)
            load_w_bf(W, 16384, w["l0a_wv_t"], D, D)
            proj_rope(b, 0, w["c_cos0"], w["c_sin0"], sc["l0a_qT"])
            proj_rope(b, 8192, w["c_cos0"], w["c_sin0"], sc["l0a_kT"])
            proj_v(b, lambda dc, nh2: W[:, 16384 + dc * 1024 + nh2 * 512:
                                        16384 + dc * 1024 + nh2 * 512 + 512],
                   sc["l0a_v"])

        def ph_l0a_attn(b):
            for dc in range(8):
                dma("sp", M1[:, dc * 2048:(dc + 1) * 2048],
                    sc["l0a_qT"][b * D + dc * 128:b * D + dc * 128 + 128, :])
                dma("sp", M2[:, dc * 2048:(dc + 1) * 2048],
                    sc["l0a_kT"][b * D + dc * 128:b * D + dc * 128 + 128, :])
            for st in range(NST):
                dma("sp", W[:, st * 1024:(st + 1) * 1024],
                    sc["l0a_v"][b * S + st * 128:b * S + st * 128 + 128, :])
            load_w_bf(W, 16384, w["l0a_wo_t"], D, D)

            def stage1(qt, par):
                sm = 32 * par
                for kc4 in range(4):
                    pp = kc4 % 2
                    for dc in range(8):
                        mm(PS[pp][:, :512],
                           M1[:, dc * 2048 + qt * 128:dc * 2048 + qt * 128 + 128],
                           M2[:, dc * 2048 + kc4 * 512:dc * 2048 + (kc4 + 1) * 512],
                           dc == 0, dc == 7)
                    sbuf_sc = F1 if par == 0 else F2
                    v_ts(sbuf_sc[:, kc4 * 512:(kc4 + 1) * 512],
                         PS[pp][:, :512], 1.0, -3.0e38, ALU.mult, ALU.max,
                         accum=SM[:, sm + kc4:sm + kc4 + 1])
                v_red(SM[:, sm + 4:sm + 5], SM[:, sm:sm + 4], ALU.max)
                v_ts(SM[:, sm + 5:sm + 6], SM[:, sm + 4:sm + 5], -1.0 / 32.0,
                     None, ALU.mult)

            def stage2(qt, par):
                sm = 32 * par
                prb0 = 24576 + par * 2048
                ptb0 = 28672 + par * 2048
                sbuf_sc = F1 if par == 0 else F2
                a_act(APx(W[:], prb0, [[32768, 128], [1, 2048]]),
                      sbuf_sc[:, 0:2048], AF.Exp,
                      bias=SM[:, sm + 5:sm + 6], scale=1.0 / 32.0,
                      accum=SM[:, sm + 6:sm + 7])
                v_recip(SM[:, sm + 7:sm + 8], SM[:, sm + 6:sm + 7])
                v_ts(DG[:, par * 128:(par + 1) * 128], ident_bf[:],
                     SM[:, sm + 7:sm + 8], None, ALU.mult)
                for kc in range(16):
                    pp = 4 + kc % 2
                    mm(PS[pp][:, :128],
                       APx(W[:], prb0 + kc * 128, [[32768, 128], [1, 128]]),
                       DG[:, par * 128:(par + 1) * 128], True, True)
                    a_copy(APx(W[:], ptb0 + kc * 128, [[32768, 128], [1, 128]]),
                           PS[pp][:, :128])
                for dc in range(8):
                    pp = 4 + dc % 2
                    for kc in range(16):
                        mm(PS[pp][:, :128],
                           W[:, kc * 1024 + dc * 128:kc * 1024 + dc * 128 + 128],
                           APx(W[:], ptb0 + kc * 128, [[32768, 128], [1, 128]]),
                           kc == 0, kc == 15)
                    v_copy(HB[:, par * 2048 + dc * 128:par * 2048 + (dc + 1) * 128],
                           PS[pp][:, :128])
                for nh2 in range(2):
                    pp = 2 + nh2
                    for dc in range(8):
                        mm(PS[pp][:, :512],
                           HB[:, par * 2048 + dc * 128:par * 2048 + (dc + 1) * 128],
                           W[:, 16384 + dc * 1024 + nh2 * 512:
                             16384 + dc * 1024 + nh2 * 512 + 512],
                           dc == 0, dc == 7)
                    v_copy(CN[:, par * 1024 + nh2 * 512:par * 1024 + (nh2 + 1) * 512],
                           PS[pp][:, :512])
                dma("act", sc["s_tm"][b * S + qt * 128:b * S + qt * 128 + 128, :],
                    CN[:, par * 1024:par * 1024 + 1024])

            for qt in range(NST):
                stage1(qt, qt % 2)
                if qt >= 1:
                    stage2(qt - 1, (qt - 1) % 2)
            stage2(NST - 1, (NST - 1) % 2)

        if not stopped():
            for b in range(BC):
                ph_l0a_proj(b)
        phase_done("l0a_proj")
        if not stopped():
            for b in range(BC):
                ph_l0a_attn(b)
        phase_done("l0a")

        def ph_rT():
            for dc in range(8):
                dma("g", M2[:, dc * 2048:(dc + 1) * 2048],
                    w["c_pet"][dc * 128:(dc + 1) * 128, :])
                dma("g", W[:, 24576 + dc * 1024:24576 + (dc + 1) * 1024],
                    w["l0f_wr"][dc * 128:(dc + 1) * 128, :])
            for ec in range(8):
                for s4 in range(4):
                    pp = s4 % 2
                    for dc in range(8):
                        mm(PS[pp][:, :512],
                           W[:, 24576 + dc * 1024 + ec * 128:
                             24576 + dc * 1024 + ec * 128 + 128],
                           M2[:, dc * 2048 + s4 * 512:dc * 2048 + (s4 + 1) * 512],
                           dc == 0, dc == 7)
                    v_copy(PTb[:, s4 * 512:(s4 + 1) * 512], PS[pp][:, :512])
                dma("act", sc["rT"][ec * 128:(ec + 1) * 128, :], PTb[:])

        def ph_l0f_qkv(b):
            load_xnT_M1(b, sc["xnT"])
            for dc in range(8):
                dma("g", W[:, dc * 3072:(dc + 1) * 3072],
                    w["l0f_wqkv"][dc * 128:(dc + 1) * 128, :])
            for cbase, dstd in ((0, sc["l0f_qT"]), (D, sc["l0f_kT"])):
                for ec in range(8):
                    for s4 in range(4):
                        pp = s4 % 2
                        for dc in range(8):
                            mm(PS[pp][:, :512],
                               W[:, dc * 3072 + cbase + ec * 128:
                                 dc * 3072 + cbase + ec * 128 + 128],
                               M1[:, dc * 2048 + s4 * 512:
                                  dc * 2048 + (s4 + 1) * 512],
                               dc == 0, dc == 7)
                        v_copy(PTb[:, s4 * 512:(s4 + 1) * 512], PS[pp][:, :512])
                    dma("act", dstd[b * D + ec * 128:b * D + ec * 128 + 128, :],
                        PTb[:])
            proj_v(b, lambda dc, nh2: W[:, dc * 3072 + 2 * D + nh2 * 512:
                                        dc * 3072 + 2 * D + nh2 * 512 + 512],
                   sc["l0f_v"])

        if not stopped():
            ph_rT()
            for b in range(BC):
                ph_l0f_qkv(b)
        phase_done("l0f_qkv")

        def ph_l0f_attn(b):
            for h in range(NH):
                dc = h // 2
                rb = (h % 2) * 64
                qu = APx(M2[:], 0, [[16384, 64], [1, 2048]])
                qv = APx(M2[:], 2048, [[16384, 64], [1, 2048]])
                kh = APx(M2[:], 4096, [[16384, 64], [1, 2048]])
                rh = APx(M2[:], 6144, [[16384, 64], [1, 2048]])
                vh = M2[:, 8192:9216]
                qu2 = APx(M2[:], 13312, [[16384, 64], [1, 2048]])
                dma("sp", qu, sc["l0f_qT"][b * D + h * 64:b * D + h * 64 + 64, :])
                dma("sp", kh, sc["l0f_kT"][b * D + h * 64:b * D + h * 64 + 64, :])
                dma("sp", rh, sc["rT"][h * 64:h * 64 + 64, :])
                vsrc = APx(sc["l0f_v"], (b * S) * D + h * 64,
                           [[D, 128], [128 * D, 16], [1, 64]])
                dma("sp", vh.rearrange("p (c n) -> p c n", n=64), vsrc)
                v_ts(qv, qu, vbt[:, h:h + 1], None, ALU.add)
                v_ts(qu2, qu, ubt[:, h:h + 1], None, ALU.add)

                def stage1a(qt, par):
                    kw = (qt + 1) * 128
                    nk4 = cdiv(kw, 512)
                    c0 = 2048 - kw
                    bdbuf = sc["bd0"] if (h * NST + qt) % 2 == 0 else sc["bd1"]
                    bst = 12288 + par * 2048
                    bsh = 8192 + par * 2048
                    for i4 in range(nk4):
                        cw = min(512, kw - i4 * 512)
                        cs = c0 + i4 * 512
                        pp = 2 + i4 % 2
                        mm(PS[pp][:, :cw],
                           APx(M2[:], 2048 + qt * 128, [[16384, 64], [1, 128]]),
                           APx(M2[:], 6144 + cs, [[16384, 64], [1, cw]]),
                           True, True)
                        a_copy(APx(W[:], bst + cs - c0, [[32768, 128], [1, cw]]),
                               PS[pp][:, :cw])
                    dma("act", APx(bdbuf, c0, [[RS, 128], [1, kw]]),
                        APx(W[:], bst, [[32768, 128], [1, kw]]))
                    dma("sp", APx(W[:], bsh, [[32768, 128], [1, kw]]),
                        APx(bdbuf, 2047 - qt * 128, [[RS - 1, 128], [1, kw]]))

                def stage1b(qt, par):
                    sm = 32 * par
                    sbuf_sc = F1 if par == 0 else F2
                    kw = (qt + 1) * 128
                    nk4 = cdiv(kw, 512)
                    bsh = 8192 + par * 2048
                    for i4 in range(nk4):
                        cw = min(512, kw - i4 * 512)
                        pp = i4 % 2
                        mm(PS[pp][:, :cw],
                           APx(M2[:], 13312 + qt * 128, [[16384, 64], [1, 128]]),
                           APx(M2[:], 4096 + i4 * 512, [[16384, 64], [1, cw]]),
                           True, True)
                        v_tt(sbuf_sc[:, i4 * 512:i4 * 512 + cw], PS[pp][:, :cw],
                             APx(W[:], bsh + i4 * 512, [[32768, 128], [1, cw]]),
                             ALU.add)
                    v_copy(F2[:, 2560:2688], sbuf_sc[:, kw - 128:kw])
                    v_sel(sbuf_sc[:, kw - 128:kw], cmask[:], F2[:, 2560:2688],
                          negT[:])
                    v_red(SM[:, sm + 0:sm + 1], sbuf_sc[:, 0:kw], ALU.max)
                    v_ts(SM[:, sm + 1:sm + 2], SM[:, sm + 0:sm + 1], -0.125,
                         None, ALU.mult)

                def stage2(qt, par):
                    sm = 32 * par
                    sbuf_sc = F1 if par == 0 else F2
                    kw = (qt + 1) * 128
                    prb0 = par * 2048
                    ptb0 = 4096 + par * 2048
                    a_act(APx(W[:], prb0, [[32768, 128], [1, kw]]),
                          sbuf_sc[:, 0:kw], AF.Exp, bias=SM[:, sm + 1:sm + 2],
                          scale=0.125, accum=SM[:, sm + 2:sm + 3])
                    v_recip(SM[:, sm + 3:sm + 4], SM[:, sm + 2:sm + 3])
                    v_ts(DG[:, par * 128:(par + 1) * 128], ident_bf[:],
                         SM[:, sm + 3:sm + 4], None, ALU.mult)
                    for kc in range(qt + 1):
                        pp = 4 + kc % 2
                        mm(PS[pp][:, :128],
                           APx(W[:], prb0 + kc * 128, [[32768, 128], [1, 128]]),
                           DG[:, par * 128:(par + 1) * 128], True, True)
                        a_copy(APx(W[:], ptb0 + kc * 128,
                                   [[32768, 128], [1, 128]]), PS[pp][:, :128])
                    for kc in range(qt + 1):
                        mm(PS[6][0:64, :128], vh[:, kc * 64:(kc + 1) * 64],
                           APx(W[:], ptb0 + kc * 128, [[32768, 128], [1, 128]]),
                           kc == 0, kc == qt)
                    v_copy(M1[rb:rb + 64,
                              dc * 2048 + qt * 128:dc * 2048 + qt * 128 + 128],
                           PS[6][0:64, :128])

                stage1a(0, 0)
                for qt in range(NST):
                    if qt + 1 < NST:
                        stage1a(qt + 1, (qt + 1) % 2)
                    stage1b(qt, qt % 2)
                    if qt >= 1:
                        stage2(qt - 1, (qt - 1) % 2)
                stage2(NST - 1, (NST - 1) % 2)

        def ph_l0f_h1(b):
            if "ctx0T" in dumps:
                for dc in range(8):
                    dma("g", sc["ctx0T"][b * D + dc * 128:b * D + dc * 128 + 128, :],
                        M1[:, dc * 2048:(dc + 1) * 2048])
            load_w_bf(W, 0, w["l0f_wo"], D, D)
            load_lgb("l0f_ln1g", "l0f_ln1b")
            for st in range(NST):
                r0 = b * S + st * 128
                for nh2 in range(2):
                    pp = nh2
                    for dc in range(8):
                        mm(PS[pp][:, :512],
                           M1[:, dc * 2048 + st * 128:dc * 2048 + st * 128 + 128],
                           W[:, dc * 1024 + nh2 * 512:dc * 1024 + nh2 * 512 + 512],
                           dc == 0, dc == 7)
                dma("sp", F2[:, 0:1024], sc["xn_tm"][r0:r0 + 128, :])
                for nh2 in range(2):
                    v_tt(F3[:, nh2 * 512:(nh2 + 1) * 512], PS[nh2][:, :512],
                         F2[:, nh2 * 512:(nh2 + 1) * 512], ALU.add)
                ln_tile(F3[:, 0:1024], F3[:, 1024:2048], 0, affine=True)
                dma("g", sc["h1_tm"][r0:r0 + 128, :], F3[:, 1024:2048])
                a_act(Bq[:, 0:1024], F3[:, 1024:2048], AF.Identity)
                t8_store(Bq[:, 0:1024], sc["h1T"], b * D, st * 128)

        def ph_l0f_ffn(b):
            for dc in range(8):
                dma("g", W[:, dc * 4096:(dc + 1) * 4096],
                    w["l0f_wff1"][dc * 128:(dc + 1) * 128, :])
            for fc in range(32):
                arena = M1 if fc < 16 else M2
                base = (fc % 16) * 1024
                dma("g", arena[:, base:base + 1024],
                    w["l0f_wff2"][fc * 128:(fc + 1) * 128, :])
            dma("g", BR[:], w["l0f_bff2"])
            load_lgb("l0f_ln2g", "l0f_ln2b")
            for st in range(NST):
                r0 = b * S + st * 128
                hsrc = APx(sc["h1T"], (b * D) * S + st * 128,
                           [[S, 128], [128 * S, 8], [1, 128]])
                dma("sp", APx(HB2[:], 0, [[2048, 128], [128, 8], [1, 128]]), hsrc)
                for fc in range(32):
                    pp = fc % 2
                    for dc in range(8):
                        mm(PS[pp][:, :128],
                           W[:, dc * 4096 + fc * 128:dc * 4096 + fc * 128 + 128],
                           HB2[:, dc * 128:(dc + 1) * 128], dc == 0, dc == 7)
                    a_act(HB[:, fc * 128:(fc + 1) * 128], PS[pp][:, :128],
                          AF.Relu, bias=bff1_sb[:, fc:fc + 1])
                for nh2 in range(2):
                    pp = 2 + nh2
                    mm(PS[pp][:, :512], ones_bf[:],
                       BR[:, nh2 * 512:(nh2 + 1) * 512], True, False)
                    for fc in range(32):
                        arena = M1 if fc < 16 else M2
                        base = (fc % 16) * 1024
                        mm(PS[pp][:, :512], HB[:, fc * 128:(fc + 1) * 128],
                           arena[:, base + nh2 * 512:base + nh2 * 512 + 512],
                           False, fc == 31)
                dma("sp", F2[:, 0:1024], sc["h1_tm"][r0:r0 + 128, :])
                for nh2 in range(2):
                    v_tt(F3[:, nh2 * 512:(nh2 + 1) * 512], PS[2 + nh2][:, :512],
                         F2[:, nh2 * 512:(nh2 + 1) * 512], ALU.add)
                ln_tile(F3[:, 0:1024], F3[:, 1024:2048], 0, affine=True)
                dma("g", sc["h2_tm"][r0:r0 + 128, :], F3[:, 1024:2048])
                a_act(Bq[:, 0:1024], F3[:, 1024:2048], AF.Identity)
                t8_store(Bq[:, 0:1024], sc["h2T"], b * D, st * 128)

        def ph_l0f_gate(b):
            for dc in range(8):
                dma("g", WGS[:, dc * 8:(dc + 1) * 8],
                    w["l0f_wg"][dc * 128:(dc + 1) * 128, :])
            for st in range(NST):
                r0 = b * S + st * 128
                hsrc = APx(sc["h2T"], (b * D) * S + st * 128,
                           [[S, 128], [128 * S, 8], [1, 128]])
                dma("sp", APx(HB2[:], 0, [[2048, 128], [128, 8], [1, 128]]), hsrc)
                for dc in range(8):
                    mm(PS[0][:, :8], HB2[:, dc * 128:(dc + 1) * 128],
                       WGS[:, dc * 8:(dc + 1) * 8], dc == 0, dc == 7)
                v_copy(SM3[:, 0:8], PS[0][:, :8])
                v_red(SM3[:, 8:9], SM3[:, 0:8], ALU.max)
                v_ts(SM3[:, 9:10], SM3[:, 8:9], -1.0, None, ALU.mult)
                a_act(SM3[:, 16:24], SM3[:, 0:8], AF.Exp, bias=SM3[:, 9:10],
                      accum=SM3[:, 10:11])
                v_recip(SM3[:, 11:12], SM3[:, 10:11])
                v_ts(SM3[:, 24:32], SM3[:, 16:24], SM3[:, 11:12], None, ALU.mult)
                v_red(SM3[:, 12:13], SM3[:, 24:32], ALU.max)
                v_ts(SM3[:, 32:40], SM3[:, 24:32], SM3[:, 12:13], None, ALU.is_ge)
                v_copy(SM3[:, 13:14], zrow[:])
                for ee in range(NE):
                    ra = 13 + (ee % 2)
                    rb2 = 13 + ((ee + 1) % 2)
                    v_ts(SM3[:, 15:16], SM3[:, ra:ra + 1], -1.0, 1.0, ALU.mult,
                         ALU.add)
                    v_tt(SM3[:, 40 + ee:41 + ee], SM3[:, 32 + ee:33 + ee],
                         SM3[:, 15:16], ALU.mult)
                    v_tt(SM3[:, rb2:rb2 + 1], SM3[:, ra:ra + 1],
                         SM3[:, 40 + ee:41 + ee], ALU.add)
                v_ts(MME[:, 0:8], SM3[:, 40:48], SM3[:, 12:13], None, ALU.mult)
                dma("g", sc["m_tm"][r0:r0 + 128, :], MME[:, 0:8])

        if not stopped():
            for b in range(BC):
                ph_l0f_attn(b)
                ph_l0f_h1(b)
        phase_done("l0f_h1")
        if not stopped():
            for b in range(BC):
                ph_l0f_ffn(b)
                ph_l0f_gate(b)
        phase_done("l0f_h2")

        def ph_moe():
            for e_ in range(NE):
                for dc in range(8):
                    dma("g", W[:, dc * 4096:(dc + 1) * 4096],
                        w["l0f_we1"][e_, dc * 128:(dc + 1) * 128, :])
                for fc in range(32):
                    arena = M1 if fc < 16 else M2
                    base = (fc % 16) * 1024
                    dma("g", arena[:, base:base + 1024],
                        w["l0f_we2"][e_, fc * 128:(fc + 1) * 128, :])
                dma("sp", be1_sb[:], w["l0f_be1"][e_])
                dma("g", BR[:], w["l0f_be2"][e_])
                for b in range(BC):
                    for st in range(NST):
                        r0 = b * S + st * 128
                        p2 = st % 2
                        hb2 = HB2[:, p2 * 1024:(p2 + 1) * 1024]
                        mme = MME[:, p2 * 8:(p2 + 1) * 8]
                        hsrc = APx(sc["h2T"], (b * D) * S + st * 128,
                                   [[S, 128], [128 * S, 8], [1, 128]])
                        dma("sp", APx(HB2[:], p2 * 1024,
                                      [[2048, 128], [128, 8], [1, 128]]), hsrc)
                        dma("sp", mme, sc["m_tm"][r0:r0 + 128, :])
                        for fc in range(32):
                            pp = fc % 2
                            for dc in range(8):
                                mm(PS[pp][:, :128],
                                   W[:, dc * 4096 + fc * 128:
                                     dc * 4096 + fc * 128 + 128],
                                   hb2[:, dc * 128:(dc + 1) * 128],
                                   dc == 0, dc == 7)
                            a_act(HB[:, fc * 128:(fc + 1) * 128], PS[pp][:, :128],
                                  AF.Relu, bias=be1_sb[:, fc:fc + 1])
                        for nh2 in range(2):
                            mm(PS[2 + nh2][:, :512], ones_bf[:],
                               BR[:, nh2 * 512:(nh2 + 1) * 512], True, False)
                        for fc in range(32):
                            arena = M1 if fc < 16 else M2
                            base = (fc % 16) * 1024
                            for nh2 in range(2):
                                mm(PS[2 + nh2][:, :512],
                                   HB[:, fc * 128:(fc + 1) * 128],
                                   arena[:, base + nh2 * 512:
                                         base + nh2 * 512 + 512],
                                   False, fc == 31)
                        for nh2 in range(2):
                            v_ts(F2[:, p2 * 1024 + nh2 * 512:
                                    p2 * 1024 + (nh2 + 1) * 512],
                                 PS[2 + nh2][:, :512], mme[:, e_:e_ + 1], None,
                                 ALU.mult)
                        if e_ == 0:
                            dma("g", sc["moe_tm"][r0:r0 + 128, :],
                                F2[:, p2 * 1024:(p2 + 1) * 1024])
                        else:
                            dma_acc(sc["moe_tm"][r0:r0 + 128, :],
                                    F2[:, p2 * 1024:(p2 + 1) * 1024])
                        if st % 4 == 3 and WQ_TASKS:
                            emit_wq_tile(*WQ_TASKS.pop())

        if not stopped():
            ph_moe()
            while WQ_TASKS:
                emit_wq_tile(*WQ_TASKS.pop())
        phase_done("moe")

        def ph_l0_final(b):
            load_w_bf(W, 0, w["l0f_wout"], D, D)
            load_lgb("l0f_ln3g", "l0f_ln3b")
            for st in range(NST):
                r0 = b * S + st * 128
                dma("sp", F2[:, 0:1024], sc["h2_tm"][r0:r0 + 128, :])
                dma("sp", F1[:, 0:1024], sc["moe_tm"][r0:r0 + 128, :])
                v_tt(F3[:, 0:1024], F2[:, 0:1024], F1[:, 0:1024], ALU.add)
                ln_tile(F3[:, 0:1024], F3[:, 1024:2048], 0, affine=True)
                a_act(Bq[:, 0:1024], F3[:, 1024:2048], AF.Identity)
                for dc in range(8):
                    trans(PB[0][:, :128], Bq[:, dc * 128:(dc + 1) * 128],
                          ident_bf[:])
                    v_copy(HB[:, dc * 128:(dc + 1) * 128], PB[0][:, :128])
                for nh2 in range(2):
                    pp = nh2
                    for dc in range(8):
                        mm(PS[pp][:, :512], HB[:, dc * 128:(dc + 1) * 128],
                           W[:, dc * 1024 + nh2 * 512:dc * 1024 + nh2 * 512 + 512],
                           dc == 0, dc == 7)
                    a_act(F2[:, 1024 + nh2 * 512:1024 + (nh2 + 1) * 512],
                          PS[pp][:, :512], AF.Silu)
                dma("sp", F1[:, 1024:2048], sc["s_tm"][r0:r0 + 128, :])
                dma("sp", CN[:, 0:1024], w["x"][r0:r0 + 128, :])
                v_tt(F2[:, 0:1024], F1[:, 1024:2048], F2[:, 1024:2048], ALU.mult)
                v_tt(F3[:, 0:1024], CN[:, 0:1024], F2[:, 0:1024], ALU.add)
                dma("g", sc["x1_tm"][r0:r0 + 128, :], F3[:, 0:1024])

        if not stopped():
            for b in range(BC):
                ph_l0_final(b)
        phase_done("l0")

        if not stopped():
            for b in range(BC):
                ph_ln_in(b, sc["x1_tm"], sc["xn1_tm"], sc["xn1T"])
        phase_done("ln1")

        def ph_l1_proj(b):
            load_xnT_M1(b, sc["xn1T"])
            load_w_bf(W, 0, w["l1a_wq"], D, D)
            load_w_bf(W, 8192, w["l1a_wk"], D, D)
            load_w_bf(W, 16384, w["l1a_wv"], D, D)
            for wbase, dstd, dstn in ((0, sc["l1_qT"], sc["l1_qn"]),
                                      (8192, sc["l1_kT"], sc["l1_kn"])):
                proj_rope(b, wbase, w["c_cos1"], w["c_sin1"], dstd, qnpsum=True)
                for s4 in range(4):
                    v_copy(F1[0:16, s4 * 512:(s4 + 1) * 512],
                           PS[2 + s4][0:16, :512])
                dma("g", dstn[b * NH:(b + 1) * NH, :], F1[0:16, 0:2048])
                if wbase == 0:
                    for qt in range(NST):
                        pp = qt % 2
                        trans(PS[pp][:, :16], F1[0:16, qt * 128:(qt + 1) * 128],
                              ident_f[0:16, 0:16])
                        v_copy(QNS[:], PS[pp][:, :16])
                        dma("g", sc["l1_qnT"][b * S + qt * 128:
                                              b * S + qt * 128 + 128, :], QNS[:])
            proj_v(b, lambda dc, nh2: W[:, 16384 + dc * 1024 + nh2 * 512:
                                        16384 + dc * 1024 + nh2 * 512 + 512],
                   sc["l1_v"])

        def ph_l1_attn(b):
            for h in range(DEBUG_L1_HEADS):
                dc = h // 2
                rb = (h % 2) * 64
                qh = APx(M2[:], 0, [[16384, 64], [1, 2048]])
                kh = APx(M2[:], 2048, [[16384, 64], [1, 2048]])
                vh = M2[:, 4096:5120]
                q2h = APx(M2[:], 5120, [[16384, 64], [1, 2048]])
                knr = APx(M2[:], 7168, [[16384, 1], [1, 2048]])
                qnr = APx(M2[:], 9216, [[16384, 1], [1, 2048]])
                dma("sp", qh, sc["l1_qT"][b * D + h * 64:b * D + h * 64 + 64, :])
                dma("sp", kh, sc["l1_kT"][b * D + h * 64:b * D + h * 64 + 64, :])
                vsrc = APx(sc["l1_v"], (b * S) * D + h * 64,
                           [[D, 128], [128 * D, 16], [1, 64]])
                dma("sp", vh.rearrange("p (c n) -> p c n", n=64), vsrc)
                v_ts(q2h, qh, -2.0, None, ALU.mult)
                dma("sp", F1[0:1, 0:2048],
                    sc["l1_kn"][b * NH + h:b * NH + h + 1, :])
                dma("sp", CN[0:1, 0:2048],
                    sc["l1_qn"][b * NH + h:b * NH + h + 1, :])
                v_ts(knr, F1[0:1, 0:2048], 1e-5, None, ALU.add)
                v_copy(qnr, CN[0:1, 0:2048])

                def stage1(qt, par):
                    sm = 32 * par
                    sbuf_sc = F1 if par == 0 else F3
                    kw = (qt + 1) * 128
                    nk4 = cdiv(kw, 512)
                    for i4 in range(nk4):
                        cw = min(512, kw - i4 * 512)
                        pp = i4 % 2
                        mm(PS[pp][:, :cw],
                           APx(M2[:], 0 + qt * 128, [[16384, 64], [1, 128]]),
                           APx(M2[:], 2048 + i4 * 512, [[16384, 64], [1, cw]]),
                           True, True)
                        mm(PS[2 + pp][:, :cw],
                           APx(M2[:], 5120 + qt * 128, [[16384, 64], [1, 128]]),
                           APx(M2[:], 2048 + i4 * 512, [[16384, 64], [1, cw]]),
                           True, False)
                        mm(PS[2 + pp][:, :cw], ones_bf[:],
                           APx(M2[:], 7168 + i4 * 512, [[16384, 1], [1, cw]]),
                           False, False)
                        mm(PS[2 + pp][:, :cw],
                           APx(M2[:], 9216 + qt * 128, [[16384, 1], [1, 128]]),
                           ones2_bf[:, 0:cw], False, True)
                        a_act(F2[:, i4 * 512:i4 * 512 + cw], PS[pp][:, :cw],
                              AF.Square)
                        if i4 % 3 == 2:
                            v_copy(F3[:, 2304:2304 + cw], PS[2 + pp][:, :cw])
                            v_recip(CN[:, i4 * 512:i4 * 512 + cw],
                                    F3[:, 2304:2304 + cw])
                        else:
                            a_act(F3[:, 2304:2304 + cw], PS[2 + pp][:, :cw],
                                  AF.Ln)
                            a_act(CN[:, i4 * 512:i4 * 512 + cw],
                                  F3[:, 2304:2304 + cw], AF.Exp, scale=-1.0)
                        v_stt(sbuf_sc[:, i4 * 512:i4 * 512 + cw],
                              F2[:, i4 * 512:i4 * 512 + cw], alpha_sc[:],
                              CN[:, i4 * 512:i4 * 512 + cw], ALU.mult,
                              ALU.mult)
                    v_copy(F2[:, 2560:2688], sbuf_sc[:, kw - 128:kw])
                    v_sel(sbuf_sc[:, kw - 128:kw], cmask[:], F2[:, 2560:2688],
                          negT[:])
                    v_red(SM[:, sm + 0:sm + 1], sbuf_sc[:, 0:kw], ALU.max)
                    v_ts(SM[:, sm + 1:sm + 2], SM[:, sm + 0:sm + 1], -1.0,
                         None, ALU.mult)

                def stage2(qt, par):
                    sm = 32 * par
                    sbuf_sc = F1 if par == 0 else F3
                    kw = (qt + 1) * 128
                    prb0 = par * 2048
                    ptb0 = 4096 + par * 2048
                    a_act(APx(W[:], prb0, [[32768, 128], [1, kw]]),
                          sbuf_sc[:, 0:kw], AF.Exp, bias=SM[:, sm + 1:sm + 2],
                          accum=SM[:, sm + 2:sm + 3])
                    v_recip(SM[:, sm + 3:sm + 4], SM[:, sm + 2:sm + 3])
                    v_ts(DG[:, par * 128:(par + 1) * 128], ident_bf[:],
                         SM[:, sm + 3:sm + 4], None, ALU.mult)
                    for kc in range(qt + 1):
                        pp = 4 + kc % 2
                        mm(PS[pp][:, :128],
                           APx(W[:], prb0 + kc * 128, [[32768, 128], [1, 128]]),
                           DG[:, par * 128:(par + 1) * 128], True, True)
                        dstp = APx(W[:], ptb0 + kc * 128,
                                   [[32768, 128], [1, 128]])
                        if kc % 2 == 0:
                            v_copy(dstp, PS[pp][:, :128])
                        else:
                            a_copy(dstp, PS[pp][:, :128])
                    for kc in range(qt + 1):
                        mm(PS[6][0:64, :128], vh[:, kc * 64:(kc + 1) * 64],
                           APx(W[:], ptb0 + kc * 128, [[32768, 128], [1, 128]]),
                           kc == 0, kc == qt)
                    v_copy(M1[rb:rb + 64,
                              dc * 2048 + qt * 128:dc * 2048 + qt * 128 + 128],
                           PS[6][0:64, :128])

                for qt in range(NST):
                    stage1(qt, qt % 2)
                    if qt >= 1:
                        stage2(qt - 1, (qt - 1) % 2)
                stage2(NST - 1, (NST - 1) % 2)

        def ph_l1_out(b):
            if "ctx1T" in dumps:
                for dc in range(8):
                    dma("g", sc["ctx1T"][b * D + dc * 128:b * D + dc * 128 + 128, :],
                        M1[:, dc * 2048:(dc + 1) * 2048])
            load_w_bf(W, 0, w["l1a_wo"], D, D)
            for st in range(NST):
                r0 = b * S + st * 128
                for nh2 in range(2):
                    pp = nh2
                    for dc in range(8):
                        mm(PS[pp][:, :512],
                           M1[:, dc * 2048 + st * 128:dc * 2048 + st * 128 + 128],
                           W[:, dc * 1024 + nh2 * 512:dc * 1024 + nh2 * 512 + 512],
                           dc == 0, dc == 7)
                    v_copy(CN[:, nh2 * 512:(nh2 + 1) * 512], PS[pp][:, :512])
                dma("act", sc["s_tm"][r0:r0 + 128, :], CN[:, 0:1024])

        if not stopped():
            for b in range(BC):
                ph_l1_proj(b)
        phase_done("l1_proj")
        if not stopped():
            for b in range(BC):
                ph_l1_attn(b)
                ph_l1_out(b)
        phase_done("l1a")

        def ph_l1f_x(b):
            for st in range(NST):
                r0 = b * S + st * 128
                dma("sp", F3[:, 1408:2432], sc["xn1_tm"][r0:r0 + 128, :])
                quant_fp4(F3[:, 1408:2432], Bq[:, 0:1024], 1024)
                t8_store(Bq[:, 0:1024], sc["xqT"], b * D, st * 128)

        def ph_l1f_h(b):
            for dcc in range(8):
                dma("sp", W[:, dcc * 2816:(dcc + 1) * 2816],
                    sc["wgate_qT"][dcc * 128:(dcc + 1) * 128, :])
                if dcc < 5:
                    dma("sp", M1[:, dcc * 2816:(dcc + 1) * 2816],
                        sc["wup_qT"][dcc * 128:(dcc + 1) * 128, :])
                else:
                    dma("sp", M2[:, (dcc - 5) * 2816:(dcc - 4) * 2816],
                        sc["wup_qT"][dcc * 128:(dcc + 1) * 128, :])

            def wup_ap(dcc, c0, cw):
                if dcc < 5:
                    return M1[:, dcc * 2816 + c0:dcc * 2816 + c0 + cw]
                return M2[:, (dcc - 5) * 2816 + c0:(dcc - 5) * 2816 + c0 + cw]

            for st in range(NST):
                r0 = b * S + st * 128
                hsrc = APx(sc["xqT"], (b * D) * S + st * 128,
                           [[S, 128], [128 * S, 8], [1, 128]])
                dma("sp", APx(HB2[:], 0, [[2048, 128], [128, 8], [1, 128]]), hsrc)
                for ncc in range(6):
                    cw = min(512, IFF - ncc * 512)
                    for dcc in range(8):
                        mm(PS[0][:, :cw], HB2[:, dcc * 128:(dcc + 1) * 128],
                           W[:, dcc * 2816 + ncc * 512:
                             dcc * 2816 + ncc * 512 + cw], dcc == 0, dcc == 7)
                    for dcc in range(8):
                        mm(PS[1][:, :cw], HB2[:, dcc * 128:(dcc + 1) * 128],
                           wup_ap(dcc, ncc * 512, cw), dcc == 0, dcc == 7)
                    a_act(CN[:, 0:cw], PS[0][:, :cw], AF.Silu)
                    v_tt(F3[:, ncc * 512:ncc * 512 + cw], CN[:, 0:cw],
                         PS[1][:, :cw], ALU.mult)
                quant_fp4(F3[:, 0:1408], Bq[:, 0:1408], 1408)
                quant_fp4(F3[:, 1408:2816], Bq[:, 1408:2816], 1408)
                for g2 in range(3):
                    n8 = min(8, 22 - g2 * 8)
                    for dc2 in range(n8):
                        di = g2 * 8 + dc2
                        trans(PB[0][:, :128], Bq[:, di * 128:(di + 1) * 128],
                              ident_bf[:])
                        v_copy(PTb[:, dc2 * 128:(dc2 + 1) * 128], PB[0][:, :128])
                    dsth = APx(sc["hqT"], (b * IFF + g2 * 8 * 128) * S + st * 128,
                               [[S, 128], [128 * S, n8], [1, 128]])
                    dma("act", dsth,
                        PTb[:, 0:n8 * 128].rearrange("p (c n) -> p c n", n=128))

        def ph_l1f_down(b):
            for dcc in range(22):
                dma("sp", W[:, dcc * 1024:(dcc + 1) * 1024],
                    sc["wdown_qT"][dcc * 128:(dcc + 1) * 128, :])
            for st in range(NST):
                r0 = b * S + st * 128
                hsrc = APx(sc["hqT"], (b * IFF) * S + st * 128,
                           [[S, 128], [128 * S, 22], [1, 128]])
                dma("sp", HB[:, 0:22 * 128].rearrange("p (c n) -> p c n", n=128),
                    hsrc)
                for nh2 in range(2):
                    pp = nh2
                    for dcc in range(22):
                        mm(PS[pp][:, :512], HB[:, dcc * 128:(dcc + 1) * 128],
                           W[:, dcc * 1024 + nh2 * 512:
                             dcc * 1024 + nh2 * 512 + 512],
                           dcc == 0, dcc == 21)
                    a_act(F2[:, 1024 + nh2 * 512:1024 + (nh2 + 1) * 512],
                          PS[pp][:, :512], AF.Silu)
                dma("sp", F1[:, 0:1024], sc["s_tm"][r0:r0 + 128, :])
                dma("sp", F3[:, 0:1024], sc["x1_tm"][r0:r0 + 128, :])
                v_tt(F2[:, 0:1024], F1[:, 0:1024], F2[:, 1024:2048], ALU.mult)
                v_tt(F3[:, 1024:2048], F3[:, 0:1024], F2[:, 0:1024], ALU.add)
                o = out[r0:r0 + 128, :]
                dma("g", o, F3[:, 1024:2048])
                ALL_OUT.append(o)

        if not stopped():
            for b in range(BC):
                ph_l1f_x(b)
        phase_done("l1f_x")
        if not stopped():
            for b in range(BC):
                ph_l1f_h(b)
                ph_l1f_down(b)
        phase_done("l1f")

        sch.add("g", lambda e: e.memset(zrow[:], 0.0), ins=ALL_OUT,
                outs=[zrow[:]], tag="fin")

        sch.finalize(DMA_SLOTS)

        @blk.tensor
        def _(t):
            sch.emit("pe", t, sems)

        @blk.vector
        def _(v):
            sch.emit("dve", v, sems)

        @blk.scalar
        def _(a):
            sch.emit("act", a, sems)

        @blk.sync
        def _(s):
            sch.emit("sp", s, sems)

        @blk.gpsimd
        def _(g):
            sch.emit("g", g, sems)

    return nc


def _host_consts():
    c = {}
    c["c_ident"] = np.eye(128, dtype=np.float32)
    A = np.zeros((128, 128), dtype=np.float32)
    for i in range(64):
        A[2 * i, 2 * i + 1] = -1.0
        A[2 * i + 1, 2 * i] = 1.0
    c["c_rotA"] = np.ascontiguousarray(A.T)
    inv0 = 1.0 / (10000.0 ** (np.arange(0, D, 2, dtype=np.float64) / D))
    t = np.arange(S, dtype=np.float64)
    ang0 = np.outer(inv0, t)
    c["c_cos0"] = np.repeat(np.cos(ang0), 2, axis=0).astype(np.float32)
    c["c_sin0"] = np.repeat(np.sin(ang0), 2, axis=0).astype(np.float32)
    inv1 = 1.0 / (10000.0 ** (np.arange(0, DH, 2, dtype=np.float64) / DH))
    ang1 = np.outer(inv1, t)
    c["c_cos1"] = np.tile(np.repeat(np.cos(ang1), 2, axis=0), (NH, 1)).astype(
        np.float32)
    c["c_sin1"] = np.tile(np.repeat(np.sin(ang1), 2, axis=0), (NH, 1)).astype(
        np.float32)
    pos = (S - 1.0) - np.arange(S, dtype=np.float64)
    angp = np.outer(inv0, pos)
    c["c_pet"] = np.concatenate([np.sin(angp), np.cos(angp)], axis=0).astype(
        np.float32)
    hmv = np.zeros((128, 8 * NH), dtype=np.float32)
    for dc in range(8):
        hmv[0:64, dc * NH + 2 * dc] = 1.0
        hmv[64:128, dc * NH + 2 * dc + 1] = 1.0
    c["c_hm"] = hmv
    c["c_cmask"] = np.tril(np.ones((128, 128), dtype=np.uint8))
    c["c_ones"] = np.ones((1, 128), dtype=np.float32)
    c["c_onesf"] = np.ones((1, 128), dtype=np.float32)
    c["c_ones2"] = np.ones((1, 512), dtype=np.float32)
    return c


def _host_weights(inp):
    f32 = np.float32
    a = lambda x: np.ascontiguousarray(np.asarray(x, dtype=f32))
    w = {}
    w["l0a_wq_t"] = a(np.asarray(inp["L0_attn_wq"]).T)
    w["l0a_wk_t"] = a(np.asarray(inp["L0_attn_wk"]).T)
    w["l0a_wv_t"] = a(np.asarray(inp["L0_attn_wv"]).T)
    w["l0a_wo_t"] = a(np.asarray(inp["L0_attn_wo"]).T)
    w["l0f_wqkv"] = a(inp["L0_ffn_Wqkv"])
    w["l0f_wo"] = a(inp["L0_ffn_Wo"])
    w["l0f_wr"] = a(inp["L0_ffn_Wr"])
    w["l0f_ubt"] = a(np.asarray(inp["L0_ffn_u_bias"]).T)
    w["l0f_vbt"] = a(np.asarray(inp["L0_ffn_v_bias"]).T)
    for nm in ("ln1", "ln2", "ln3"):
        w[f"l0f_{nm}g"] = a(inp[f"L0_ffn_{nm}_g"])
        w[f"l0f_{nm}b"] = a(inp[f"L0_ffn_{nm}_b"])
    w["l0f_wff1"] = a(inp["L0_ffn_Wff1"])
    w["l0f_bff1"] = a(np.asarray(inp["L0_ffn_bff1"]).reshape(FF // 128, 128).T)
    w["l0f_wff2"] = a(inp["L0_ffn_Wff2"])
    w["l0f_bff2"] = a(np.asarray(inp["L0_ffn_bff2"]).reshape(1, D))
    w["l0f_wg"] = a(inp["L0_ffn_Wg"])
    w["l0f_we1"] = a(inp["L0_ffn_We1"])
    w["l0f_be1"] = a(np.asarray(inp["L0_ffn_be1"]).reshape(NE, FF // 128, 128)
                     .transpose(0, 2, 1))
    w["l0f_we2"] = a(inp["L0_ffn_We2"])
    w["l0f_be2"] = a(np.asarray(inp["L0_ffn_be2"]).reshape(NE, 1, D))
    w["l0f_wout"] = a(inp["L0_ffn_Wout"])
    w["l1a_wq"] = a(inp["L1_attn_Wq"])
    w["l1a_wk"] = a(inp["L1_attn_Wk"])
    w["l1a_wv"] = a(inp["L1_attn_Wv"])
    w["l1a_wo"] = a(inp["L1_attn_Wo"])
    w["l1a_alpha"] = a(np.asarray(inp["L1_attn_alpha"]).reshape(1, 1))
    w["l1f_wgate"] = a(inp["L1_ffn_w_gate"])
    w["l1f_wup"] = a(inp["L1_ffn_w_up"])
    w["l1f_wdown"] = a(inp["L1_ffn_w_down"])
    return w


def kernel(**inputs):
    nc = build_nc()
    consts = _host_consts()
    weights = _host_weights(inputs)
    x = np.ascontiguousarray(np.asarray(inputs["L0_attn_x"], dtype=np.float32))
    in_maps = []
    for c in range(8):
        m = {"x": x[c * BC:(c + 1) * BC].reshape(BC * S, D)}
        m.update(weights)
        m.update(consts)
        in_maps.append(m)
    res = run_bass_kernel_spmd(nc, in_maps, core_ids=list(range(8)))
    outs = [np.asarray(r["out"]).reshape(BC, S, D) for r in res.results]
    return np.concatenate(outs, axis=0)


# version 08
# speedup vs baseline: 1.2829x; 1.0001x over previous
import math
import numpy as np
import ml_dtypes
import bass_rust
import concourse.bass as bass
import concourse.mybir as mybir
from concourse.bass_utils import run_bass_kernel_spmd
from contextlib import ExitStack

F32 = mybir.dt.float32
BF16 = mybir.dt.bfloat16
I16 = mybir.dt.int16
U8 = mybir.dt.uint8
AF = mybir.ActivationFunctionType
ALU = mybir.AluOpType
AX = mybir.AxisListType

BC, S, D = 2, 2048, 1024
NH, DH, FF, NE, IFF = 16, 64, 4096, 8, 2816
RS = 2176
NEG = -1.0e30
NST = S // 128

DEBUG_DUMPS = ()
DEBUG_STOP = None
DEBUG_L1_HEADS = NH
DEBUG_L1_MODE = 9
SERIAL_READS = False

SB_GRAN = 512
DR_GRAN = 1 << 18
GRAN_MAP = {"SM": 16, "SM3": 16}
_PSUM_NAMES = {f"ps{i}" for i in range(8)} | {"pb0", "pb1"}


def _keys(ap):
    t = ap.tensor
    name = t.name
    a = ap.ap
    if str(ap.space) == "MemorySpace.PSUM":
        return (name,)
    if str(ap.space) == "MemorySpace.DRAM":
        off = ap.offset
        span = 1
        for st, cnt in a:
            span += abs(st) * (cnt - 1)
        lo = off // DR_GRAN
        hi = (off + span - 1) // DR_GRAN
        return tuple(f"{name}:{i}" for i in range(lo, hi + 1))
    pitch = a[0][0] if len(a) > 0 else 1
    off = ap.offset % pitch if pitch > 0 else ap.offset
    span = 1
    for st, cnt in a[1:]:
        span += abs(st) * (cnt - 1)
    g = GRAN_MAP.get(name, SB_GRAN)
    lo = off // g
    hi = (off + span - 1) // g
    return tuple(f"{name}:{i}" for i in range(lo, hi + 1))


class _Op:
    __slots__ = ("eng", "fn", "dma", "idx", "deps", "signals", "cnt", "tag",
                 "slot")


class Sched:
    def __init__(self):
        self.ops = []
        self.res = {}

    def add(self, eng, fn, ins=(), outs=(), dma=False, tag=""):
        op = _Op()
        op.eng = eng; op.fn = fn; op.dma = dma; op.tag = tag
        op.idx = len(self.ops); op.signals = False; op.cnt = 0
        deps = {}

        def dep(p):
            if p is None or p is op:
                return
            if (not p.dma) and (not op.dma) and p.eng == "pe" and eng == "pe":
                return
            if p.dma:
                k = ("D", p.eng, p.idx)
            else:
                k = (p.eng, False)
            cur = deps.get(k)
            if cur is None or p.idx > cur.idx:
                deps[k] = p

        rkeys = []
        wkeys = []
        for ap_ in ins:
            for k in _keys(ap_):
                if SERIAL_READS or k in _PSUM_NAMES:
                    wkeys.append(k)
                else:
                    rkeys.append(k)
        for ap_ in outs:
            wkeys.extend(_keys(ap_))
        for r in rkeys:
            st = self.res.get(r)
            if st is not None:
                dep(st[0])
        for wk in wkeys:
            st = self.res.get(wk)
            if st is not None:
                dep(st[0])
                for rr in st[1]:
                    dep(rr)
        op.deps = list(deps.values())
        for p in op.deps:
            p.signals = True
        for r in rkeys:
            st = self.res.get(r)
            if st is None:
                st = self.res[r] = [None, []]
            st[1].append(op)
        for wk in wkeys:
            self.res[wk] = [op, []]
        self.ops.append(op)
        return op

    def finalize(self, dma_slots):
        cnt = {}
        for op in self.ops:
            if op.dma:
                op.signals = True
                nq = cnt.get(op.eng, 0)
                cnt[op.eng] = nq + 1
                R = dma_slots[op.eng]
                op.slot = nq % R
                op.cnt = nq // R + 1
            elif op.signals:
                k = (op.eng, False)
                cnt[k] = cnt.get(k, 0) + 1
                op.cnt = cnt[k]

    def emit(self, eng, handle, sems):
        hwm = {}
        for op in self.ops:
            if op.eng != eng:
                continue
            if op.dma and op.cnt > 1:
                k = ("D", op.eng, op.slot)
                val = (op.cnt - 1) * 16
                if hwm.get(k, 0) < val:
                    handle.wait_ge(sems[k], val)
                    hwm[k] = val
            for p in op.deps:
                if p.dma:
                    k = ("D", p.eng, p.slot)
                    val = p.cnt * 16
                else:
                    k = (p.eng, False)
                    val = p.cnt
                if hwm.get(k, 0) < val:
                    handle.wait_ge(sems[k], val)
                    hwm[k] = val
            inst = op.fn(handle)
            if op.signals:
                if op.dma:
                    inst.then_inc(sems[("D", op.eng, op.slot)], 16)
                else:
                    inst.then_inc(sems[(op.eng, False)], 1)


def APx(base_ap, offset, pattern):
    return bass_rust.AP(base_ap.tensor, offset, pattern)


def cdiv(a, b):
    return (a + b - 1) // b


def build_nc():
    nc = bass.Bass("TRN2", target_bir_lowering=False, debug=False)
    dumps = set(DEBUG_DUMPS)

    def din(name, shape, dtype=F32):
        return nc.dram_tensor(name, shape, dtype, kind="ExternalInput").ap()

    w = {}
    w["x"] = din("x", [BC * S, D])
    for item in [
        ("l0a_wq_t", [D, D]), ("l0a_wk_t", [D, D]), ("l0a_wv_t", [D, D]),
        ("l0a_wo_t", [D, D]),
        ("l0f_wqkv", [D, 3 * D]), ("l0f_wo", [D, D]), ("l0f_wr", [D, D]),
        ("l0f_ubt", [DH, NH]), ("l0f_vbt", [DH, NH]),
        ("l0f_ln1g", [D]), ("l0f_ln1b", [D]), ("l0f_ln2g", [D]), ("l0f_ln2b", [D]),
        ("l0f_ln3g", [D]), ("l0f_ln3b", [D]),
        ("l0f_wff1", [D, FF]), ("l0f_bff1", [128, FF // 128]),
        ("l0f_wff2", [FF, D]), ("l0f_bff2", [1, D]),
        ("l0f_wg", [D, NE]),
        ("l0f_we1", [NE, D, FF]), ("l0f_be1", [NE, 128, FF // 128]),
        ("l0f_we2", [NE, FF, D]), ("l0f_be2", [NE, 1, D]),
        ("l0f_wout", [D, D]),
        ("l1a_wq", [D, D]), ("l1a_wk", [D, D]), ("l1a_wv", [D, D]),
        ("l1a_wo", [D, D]), ("l1a_alpha", [1, 1]),
        ("l1f_wgate", [IFF, D]), ("l1f_wup", [IFF, D]), ("l1f_wdown", [D, IFF]),
        ("c_ident", [128, 128]), ("c_rotA", [128, 128]),
        ("c_cos0", [D, S]), ("c_sin0", [D, S]),
        ("c_cos1", [D, S]), ("c_sin1", [D, S]),
        ("c_pet", [D, S]), ("c_hm", [128, 8 * NH]),
        ("c_cmask", [128, 128], U8), ("c_ones", [1, 128]), ("c_onesf", [1, 128]), ("c_ones2", [1, 512]),
    ]:
        nm, shp = item[0], item[1]
        w[nm] = din(nm, shp, item[2] if len(item) > 2 else F32)

    out = nc.dram_tensor("out", [BC * S, D], F32, kind="ExternalOutput").ap()

    sc = {}

    def S_(name, shape, dtype=F32):
        kind = "ExternalOutput" if name in dumps else "Internal"
        sc[name] = nc.dram_tensor(name, shape, dtype, kind=kind).ap()

    S_("xn_tm", [BC * S, D]); S_("xnT", [BC * D, S], BF16)
    S_("l0a_qT", [BC * D, S], BF16); S_("l0a_kT", [BC * D, S], BF16)
    S_("l0a_v", [BC * S, D], BF16)
    S_("s_tm", [BC * S, D])
    S_("l0f_qT", [BC * D, S], BF16); S_("l0f_kT", [BC * D, S], BF16)
    S_("l0f_v", [BC * S, D], BF16); S_("rT", [D, S], BF16)
    S_("bd0", [128 * RS], BF16); S_("bd1", [128 * RS], BF16)
    S_("h1_tm", [BC * S, D]); S_("h1T", [BC * D, S], BF16)
    S_("h2_tm", [BC * S, D]); S_("h2T", [BC * D, S], BF16)
    S_("m_tm", [BC * S, NE]); S_("moe_tm", [BC * S, D])
    S_("x1_tm", [BC * S, D])
    S_("xn1_tm", [BC * S, D]); S_("xn1T", [BC * D, S], BF16)
    S_("l1_qT", [BC * D, S], BF16); S_("l1_kT", [BC * D, S], BF16)
    S_("l1_v", [BC * S, D], BF16)
    S_("l1_qn", [BC * NH, S]); S_("l1_kn", [BC * NH, S])
    S_("l1_qnT", [BC * S, NH])
    S_("wgate_qT", [D, IFF], BF16); S_("wup_qT", [D, IFF], BF16)
    S_("wdown_qT", [IFF, D], BF16)
    S_("hqT", [BC * IFF, S], BF16); S_("xqT", [BC * D, S], BF16)
    S_("ctx0T", [BC * D, S], BF16); S_("ctx1T", [BC * D, S], BF16)

    sch = Sched()

    with ExitStack() as ctx:
        def sb(name, shape, dtype=F32):
            return ctx.enter_context(nc.sbuf_tensor(name, shape, dtype))

        M1 = sb("M1", [128, 16384], BF16)
        M2 = sb("M2", [128, 16384], BF16)
        W = sb("W", [128, 32768], BF16)
        F1 = sb("F1", [128, 2048])
        F2 = sb("F2", [128, 2816])
        F3 = sb("F3", [128, 2816])
        CN = sb("CN", [128, 2048])
        Bq = sb("Bq", [128, 2816], BF16)
        PTb = sb("PTb", [128, 2048], BF16)
        HB = sb("HB", [128, 4096], BF16)
        HB2 = sb("HB2", [128, 2048], BF16)
        IQ = sb("IQ", [128, 1408], I16)
        U8T = sb("U8T", [128, 1408], U8)
        U8b = sb("U8b", [128, 1408], U8)
        LGB = sb("LGB", [128, 2048], BF16)
        SM = sb("SM", [128, 64])
        SM3 = sb("SM3", [128, 64])
        DG = sb("DG", [128, 384], BF16)
        MME = sb("MME", [128, 2 * NE])
        QNS = sb("QNS", [128, 16])
        WGS = sb("WGS", [128, 8 * NE], BF16)
        BR = sb("BR", [1, 1024], BF16)
        ident_bf = sb("ident_bf", [128, 128], BF16)
        ident_f = sb("ident_f", [128, 128], F32)
        rotA = sb("rotA", [128, 128], BF16)
        cmask = sb("cmask", [128, 128], U8)
        negT = sb("negT", [128, 128])
        ones_bf = sb("ones_bf", [1, 128], BF16)
        ones_f = sb("ones_f", [1, 128], F32)
        ones2_bf = sb("ones2_bf", [1, 512], BF16)
        ubt = sb("ubt", [DH, NH])
        vbt = sb("vbt", [DH, NH])
        hm = sb("hm", [128, 8 * NH], BF16)
        bff1_sb = sb("bff1_sb", [128, FF // 128])
        be1_sb = sb("be1_sb", [128, FF // 128])
        alpha_sc = sb("alpha_sc", [128, 1])
        lnsc = sb("lnsc", [1, 1])
        arow = sb("arow", [1, 1])
        lnrow = sb("lnrow", [1, D], BF16)
        zrow = sb("zrow", [128, 1])
        eps1 = sb("eps1", [128, 1])

        PS = [ctx.enter_context(nc.psum_tensor(f"ps{i}", [128, 512], F32))
              for i in range(7)]
        PB = [ctx.enter_context(nc.psum_tensor("pb0", [128, 1024], BF16))
              for i in range(1)]

        sems = {}
        for e in ("pe", "dve", "act", "sp", "g"):
            sems[(e, False)] = ctx.enter_context(nc.semaphore(f"sc_{e}"))
        DMA_SLOTS = {"sp": 36, "g": 36, "act": 12}
        for e, R in DMA_SLOTS.items():
            for i in range(R):
                sems[("D", e, i)] = ctx.enter_context(nc.semaphore(f"sd_{e}{i}"))

        blk = ctx.enter_context(nc.Block())

        def _outer16(ap_):
            for st_, cnt_ in ap_.ap:
                if cnt_ != 1:
                    return cnt_ % 16 == 0
            return False

        def dma(q, out_ap, in_ap, tag=""):
            if not (_outer16(out_ap) and _outer16(in_ap)):
                q = "act"
            elif q == "act":
                q = "g"
            sch.add(q, lambda e, o=out_ap, i=in_ap: e.dma_start(out=o, in_=i),
                    ins=[in_ap], outs=[out_ap], dma=True, tag=tag)

        def dma_acc(out_ap, in_ap, tag=""):
            sch.add("g", lambda e, o=out_ap, i=in_ap:
                    e.dma_start(out=o, in_=i, accum_op=ALU.add),
                    ins=[in_ap, out_ap], outs=[out_ap], dma=True, tag=tag)

        def mm(ps_ap, lhsT, rhs, start, stop, tag=""):
            sch.add("pe", lambda e, o=ps_ap, l=lhsT, r=rhs, s1=start, s2=stop:
                    e.matmul(o, l, r, start=s1, stop=s2),
                    ins=[lhsT, rhs], outs=[ps_ap], tag=tag)

        def trans(ps_ap, in_ap, id_ap, tag=""):
            sch.add("pe", lambda e, o=ps_ap, i=in_ap, d=id_ap: e.transpose(o, i, d),
                    ins=[in_ap, id_ap], outs=[ps_ap], tag=tag)

        def v_ts(out_ap, in_ap, s1, s2, op0, op1=None, accum=None, tag=""):
            ins = [in_ap] + [s for s in (s1, s2) if hasattr(s, "tensor")]
            outs = [out_ap] + ([accum] if accum is not None else [])
            if op1 is None and s2 is None and accum is None:
                sch.add("dve", lambda e, o=out_ap, i=in_ap, a=s1, p0=op0:
                        e.tensor_scalar(o, i, a, None, op0=p0),
                        ins=ins, outs=outs, tag=tag)
            elif accum is None:
                sch.add("dve", lambda e, o=out_ap, i=in_ap, a=s1, b=s2, p0=op0,
                        p1=op1: e.tensor_scalar(o, i, a, b, op0=p0, op1=p1),
                        ins=ins, outs=outs, tag=tag)
            else:
                sch.add("dve", lambda e, o=out_ap, i=in_ap, a=s1, b=s2, p0=op0,
                        p1=op1, ac=accum:
                        e.tensor_scalar(o, i, a, b, op0=p0, op1=p1, accum_out=ac),
                        ins=ins, outs=outs, tag=tag)

        def v_tt(out_ap, a_ap, b_ap, op, tag=""):
            sch.add("dve", lambda e, o=out_ap, a=a_ap, b=b_ap, p=op:
                    e.tensor_tensor(o, a, b, op=p),
                    ins=[a_ap, b_ap], outs=[out_ap], tag=tag)

        def v_stt(out_ap, in0, scal, in1, op0, op1, tag=""):
            ins = [in0, in1] + ([scal] if hasattr(scal, "tensor") else [])
            sch.add("dve", lambda e, o=out_ap, a=in0, s=scal, b=in1, p0=op0, p1=op1:
                    e.scalar_tensor_tensor(o, a, s, b, op0=p0, op1=p1),
                    ins=ins, outs=[out_ap], tag=tag)

        def v_copy(out_ap, in_ap, tag=""):
            sch.add("dve", lambda e, o=out_ap, i=in_ap: e.tensor_copy(o, i),
                    ins=[in_ap], outs=[out_ap], tag=tag)

        def v_red(out_ap, in_ap, op, absv=None, tag=""):
            sch.add("dve", lambda e, o=out_ap, i=in_ap, p=op, av=absv:
                    e.tensor_reduce(o, i, axis=AX.X, op=p, apply_absolute_value=av),
                    ins=[in_ap], outs=[out_ap], tag=tag)

        def v_recip(out_ap, in_ap, tag=""):
            sch.add("dve", lambda e, o=out_ap, i=in_ap: e.reciprocal(o, i),
                    ins=[in_ap], outs=[out_ap], tag=tag)

        def g_ts(out_ap, in_ap, s1, s2, op0, op1=None, tag=""):
            ins = [in_ap] + [x for x in (s1, s2) if hasattr(x, "tensor")]
            if op1 is None and s2 is None:
                sch.add("g", lambda e, o=out_ap, i=in_ap, a=s1, p0=op0:
                        e.tensor_scalar(o, i, a, None, op0=p0),
                        ins=ins, outs=[out_ap], tag=tag)
            else:
                sch.add("g", lambda e, o=out_ap, i=in_ap, a=s1, b=s2, p0=op0,
                        p1=op1: e.tensor_scalar(o, i, a, b, op0=p0, op1=p1),
                        ins=ins, outs=[out_ap], tag=tag)

        def g_tt(out_ap, a_ap, b_ap, op, tag=""):
            sch.add("g", lambda e, o=out_ap, a=a_ap, b=b_ap, p=op:
                    e.tensor_tensor(o, a, b, op=p),
                    ins=[a_ap, b_ap], outs=[out_ap], tag=tag)

        def v_sel(out_ap, mask, t_ap, f_ap, tag=""):
            sch.add("dve", lambda e, o=out_ap, m=mask, a=t_ap, b=f_ap:
                    e.select(o, m, a, b), ins=[mask, t_ap, f_ap], outs=[out_ap],
                    tag=tag)

        def v_memset(ap_, val, tag=""):
            sch.add("dve", lambda e, o=ap_, v=val: e.memset(o, v), ins=[],
                    outs=[ap_], tag=tag)

        def a_copy(out_ap, in_ap, tag=""):
            sch.add("act", lambda e, o=out_ap, i=in_ap: e.activation(
                o, i, AF.Identity), ins=[in_ap], outs=[out_ap], tag=tag)

        def a_act(out_ap, in_ap, func, bias=None, scale=None, accum=None, tag=""):
            ins = [in_ap]
            if bias is not None and hasattr(bias, "tensor"):
                ins.append(bias)
            outs = [out_ap] + ([accum] if accum is not None else [])
            kw = {}
            if bias is not None:
                kw["bias"] = bias
            if scale is not None:
                kw["scale"] = scale

            def f(e, o=out_ap, i=in_ap, fc=func, kw=kw, ac=accum):
                if ac is not None:
                    return e.activation(o, i, fc, accum_out=ac, **kw)
                return e.activation(o, i, fc, **kw)
            sch.add("act", f, ins=ins, outs=outs, tag=tag)

        ALL_OUT = []

        stop_hit = [False]

        def stopped():
            return stop_hit[0]

        def phase_done(name):
            if DEBUG_STOP is not None and name == DEBUG_STOP:
                stop_hit[0] = True

        dma("g", ident_bf[:], w["c_ident"])
        dma("sp", ident_f[:], w["c_ident"])
        dma("g", rotA[:], w["c_rotA"])
        dma("sp", cmask[:], w["c_cmask"])
        dma("g", ones_bf[:], w["c_ones"])
        dma("sp", ones_f[:], w["c_onesf"])
        dma("g", ones2_bf[:], w["c_ones2"])
        dma("sp", ubt[:], w["l0f_ubt"])
        dma("sp", vbt[:], w["l0f_vbt"])
        dma("g", hm[:], w["c_hm"])
        dma("sp", bff1_sb[:], w["l0f_bff1"])
        v_memset(negT[:], NEG)
        v_memset(zrow[:], 0.0)
        v_memset(eps1[:], 1e-5)
        s0 = float(math.sqrt(DH) / math.log(1.0 + DH))
        dma("sp", arow[:], w["l1a_alpha"])
        a_act(lnsc[:], arow[:], AF.Exp, scale=math.log(s0))
        mm(PS[5][:, :1], ones_f[:], lnsc[:], True, True)
        v_copy(alpha_sc[:], PS[5][:, :1])

        def load_lgb(gnm, bnm):
            for j, nm in enumerate((gnm, bnm)):
                dma("g", lnrow[:], w[nm].unsqueeze(0))
                for h in range(2):
                    mm(PS[5][:, :512], ones_bf[:], lnrow[:, h * 512:(h + 1) * 512],
                       True, True)
                    v_copy(LGB[:, j * 1024 + h * 512:j * 1024 + (h + 1) * 512],
                           PS[5][:, :512])

        def ln_tile(src, dst, smbase, affine=False):
            c = smbase
            v_red(SM[:, c:c + 1], src, ALU.add)
            a_act(CN[:, 0:1024], src, AF.Square, accum=SM[:, c + 1:c + 2])
            v_ts(SM[:, c + 2:c + 3], SM[:, c:c + 1], 1.0 / D, None, ALU.mult)
            v_tt(SM[:, c + 3:c + 4], SM[:, c + 2:c + 3], SM[:, c + 2:c + 3],
                 ALU.mult)
            v_ts(SM[:, c + 4:c + 5], SM[:, c + 1:c + 2], 1.0 / D,
                 SM[:, c + 3:c + 4], ALU.mult, ALU.subtract)
            a_act(SM[:, c + 5:c + 6], SM[:, c + 4:c + 5], AF.Ln, bias=eps1[:])
            a_act(SM[:, c + 6:c + 7], SM[:, c + 5:c + 6], AF.Exp, scale=-0.5)
            v_tt(SM[:, c + 7:c + 8], SM[:, c + 2:c + 3], SM[:, c + 6:c + 7],
                 ALU.mult)
            if not affine:
                v_ts(dst, src, SM[:, c + 6:c + 7], SM[:, c + 7:c + 8],
                     ALU.mult, ALU.subtract)
            else:
                v_ts(CN[:, 1024:2048], src, SM[:, c + 6:c + 7],
                     SM[:, c + 7:c + 8], ALU.mult, ALU.subtract)
                v_tt(CN[:, 0:1024], CN[:, 1024:2048], LGB[:, 0:1024], ALU.mult)
                v_tt(dst, CN[:, 0:1024], LGB[:, 1024:2048], ALU.add)

        def t8_store(src_bf, dram_ap, rowbase, col, nch=8):
            for dc in range(nch):
                trans(PB[0][:, :128], src_bf[:, dc * 128:(dc + 1) * 128],
                      ident_bf[:])
                v_copy(PTb[:, dc * 128:(dc + 1) * 128], PB[0][:, :128])
            dst = APx(dram_ap, rowbase * S + col,
                      [[S, 128], [128 * S, nch], [1, 128]])
            srcv = PTb[:, 0:nch * 128].rearrange("p (c n) -> p c n", n=128)
            dma("act", dst, srcv)

        def ph_ln_in(b, src_ap, dst_tm, dst_T):
            for st in range(NST):
                r0 = b * S + st * 128
                p2 = st % 2
                xin = F2[:, p2 * 1024:(p2 + 1) * 1024]
                xo = F3[:, p2 * 1024:(p2 + 1) * 1024]
                dma("sp", xin, src_ap[r0:r0 + 128, :])
                ln_tile(xin, xo, p2 * 16, affine=False)
                dma("g", dst_tm[r0:r0 + 128, :], xo)
                a_act(Bq[:, 0:1024], xo, AF.Identity)
                t8_store(Bq[:, 0:1024], dst_T, b * D, st * 128)

        for b in range(BC):
            ph_ln_in(b, w["x"], sc["xn_tm"], sc["xnT"])
        phase_done("ln0")

        def quant_fp4(src, dst_bf, wd, alt=False):
            nb = wd // 16
            bmraw = CN[:, 1408:1408 + nb]
            bmc = CN[:, 1536:1536 + nb]
            rec = CN[:, 1792:1792 + nb]
            src3 = src.rearrange("p (b k) -> p b k", k=16)

            def bc16(ap2d):
                a = ap2d.ap
                return APx(ap2d, ap2d.offset, [[a[0][0], 128], [a[1][0], nb],
                                               [0, 16]])
            if alt:
                T = F1[:, 0:wd]
                SG = F3[:, 1408:1408 + wd]
            else:
                T = F2[:, 0:wd]
                SG = F1[:, 0:wd]
            SQ = CN[:, 0:wd]
            v_red(bmraw, src3, ALU.max, absv=True)
            v_ts(bmc, bmraw, 1.0 / 6.0, 1e-12, ALU.mult, ALU.max)
            v_recip(rec, bmc)
            t3 = T.rearrange("p (b k) -> p b k", k=16)
            v_tt(t3, src3, bc16(rec), ALU.mult)
            a_act(SQ, T, AF.Square)
            a_act(SG, T, AF.Sign)
            v_ts(IQ[:, 0:wd], T, 2.0, None, ALU.mult)
            v_ts(src, IQ[:, 0:wd], 0.5, None, ALU.mult)
            v_ts(U8T[:, 0:wd], SQ, 5.0625, None, ALU.is_lt)
            v_ts(U8b[:, 0:wd], SQ, 20.25, None, ALU.is_lt)
            v_ts(IQ[:, 0:wd], T, 1.0, None, ALU.mult)
            v_ts(T, SQ, 25.0, None, ALU.is_gt)
            v_stt(SQ, T, 2.0, SG, ALU.mult, ALU.mult)
            v_ts(T, SG, 4.0, None, ALU.mult)
            v_tt(SG, SQ, T, ALU.add)
            v_copy(T, IQ[:, 0:wd])
            v_sel(SQ, U8b[:, 0:wd], T, SG)
            v_sel(T, U8T[:, 0:wd], src, SQ)
            d3 = dst_bf.rearrange("p (b k) -> p b k", k=16)
            v_tt(d3, t3, bc16(bmc), ALU.mult)

        def emit_wq_tile(src_dram, n_rows, kdim, dst_dram, it):
            nh = cdiv(kdim, 1408)
            for hh in range(nh):
                c0 = hh * 1408
                cw = min(1408, kdim - c0)
                dma("sp", F3[:, c0:c0 + cw],
                    src_dram[it * 128:(it + 1) * 128, c0:c0 + cw])
                quant_fp4(F3[:, c0:c0 + cw], Bq[:, c0:c0 + cw], cw,
                          alt=(kdim == D))
            nch = kdim // 128
            for g2 in range(cdiv(nch, 8)):
                n8 = min(8, nch - g2 * 8)
                for dc in range(n8):
                    di = g2 * 8 + dc
                    trans(PB[0][:, :128], Bq[:, di * 128:(di + 1) * 128],
                          ident_bf[:])
                    v_copy(PTb[:, dc * 128:(dc + 1) * 128], PB[0][:, :128])
                dst = APx(dst_dram, (g2 * 8 * 128) * n_rows + it * 128,
                          [[n_rows, 128], [128 * n_rows, n8], [1, 128]])
                dma("act", dst,
                    PTb[:, 0:n8 * 128].rearrange("p (c n) -> p c n", n=128))

        WQ_TASKS = []
        for args_ in ((w["l1f_wgate"], IFF, D, sc["wgate_qT"]),
                      (w["l1f_wup"], IFF, D, sc["wup_qT"]),
                      (w["l1f_wdown"], D, IFF, sc["wdown_qT"])):
            for it in range(args_[1] // 128):
                WQ_TASKS.append(args_ + (it,))
        WQ_TASKS.reverse()
        phase_done("wquant")

        def load_xnT_M1(b, src):
            for dc in range(8):
                dma("sp", M1[:, dc * 2048:(dc + 1) * 2048],
                    src[b * D + dc * 128:b * D + dc * 128 + 128, :])

        def load_w_bf(arena, base, src_dram, rows, cols):
            for dc in range(rows // 128):
                dma("g", arena[:, base + dc * cols:base + (dc + 1) * cols],
                    src_dram[dc * 128:(dc + 1) * 128, :])

        TRIGC = 24576
        TRIGS = 26624

        def proj_rope(b, wbase, cosd, sind, dstd, qnpsum=None):
            for ec in range(8):
                dma("g", W[:, TRIGC:TRIGC + 2048], cosd[ec * 128:(ec + 1) * 128, :])
                dma("g", W[:, TRIGS:TRIGS + 2048], sind[ec * 128:(ec + 1) * 128, :])
                for s4 in range(4):
                    pp = s4 % 2
                    for dc in range(8):
                        mm(PS[pp][:, :512],
                           W[:, wbase + dc * 1024 + ec * 128:
                             wbase + dc * 1024 + ec * 128 + 128],
                           M1[:, dc * 2048 + s4 * 512:dc * 2048 + (s4 + 1) * 512],
                           dc == 0, dc == 7)
                    v_copy(Bq[:, s4 * 512:(s4 + 1) * 512], PS[pp][:, :512])
                    mm(PS[pp][:, :512], rotA[:], Bq[:, s4 * 512:(s4 + 1) * 512],
                       True, True)
                    v_tt(F2[:, s4 * 512:(s4 + 1) * 512], PS[pp][:, :512],
                         W[:, TRIGS + s4 * 512:TRIGS + (s4 + 1) * 512], ALU.mult)
                    v_tt(F3[:, s4 * 512:(s4 + 1) * 512],
                         Bq[:, s4 * 512:(s4 + 1) * 512],
                         W[:, TRIGC + s4 * 512:TRIGC + (s4 + 1) * 512], ALU.mult)
                    v_tt(PTb[:, s4 * 512:(s4 + 1) * 512],
                         F2[:, s4 * 512:(s4 + 1) * 512],
                         F3[:, s4 * 512:(s4 + 1) * 512], ALU.add)
                    if qnpsum is not None:
                        a_act(HB[:, s4 * 512:(s4 + 1) * 512],
                              PTb[:, s4 * 512:(s4 + 1) * 512], AF.Square)
                        mm(PS[2 + s4][0:16, :512], hm[:, ec * 16:(ec + 1) * 16],
                           HB[:, s4 * 512:(s4 + 1) * 512], ec == 0, ec == 7)
                dma("act", dstd[b * D + ec * 128:b * D + ec * 128 + 128, :], PTb[:])

        def proj_v(b, rhs_fn, dstd):
            for st in range(NST):
                p2 = st % 2
                for nh2 in range(2):
                    pp = nh2
                    for dc in range(8):
                        mm(PS[pp][:, :512],
                           M1[:, dc * 2048 + st * 128:dc * 2048 + st * 128 + 128],
                           rhs_fn(dc, nh2), dc == 0, dc == 7)
                    v_copy(HB[:, p2 * 1024 + nh2 * 512:p2 * 1024 + (nh2 + 1) * 512],
                           PS[pp][:, :512])
                dma("act", dstd[b * S + st * 128:b * S + st * 128 + 128, :],
                    HB[:, p2 * 1024:(p2 + 1) * 1024])

        def ph_l0a_proj(b):
            load_xnT_M1(b, sc["xnT"])
            load_w_bf(W, 0, w["l0a_wq_t"], D, D)
            load_w_bf(W, 8192, w["l0a_wk_t"], D, D)
            load_w_bf(W, 16384, w["l0a_wv_t"], D, D)
            proj_rope(b, 0, w["c_cos0"], w["c_sin0"], sc["l0a_qT"])
            proj_rope(b, 8192, w["c_cos0"], w["c_sin0"], sc["l0a_kT"])
            proj_v(b, lambda dc, nh2: W[:, 16384 + dc * 1024 + nh2 * 512:
                                        16384 + dc * 1024 + nh2 * 512 + 512],
                   sc["l0a_v"])

        def ph_l0a_attn(b):
            for dc in range(8):
                dma("sp", M1[:, dc * 2048:(dc + 1) * 2048],
                    sc["l0a_qT"][b * D + dc * 128:b * D + dc * 128 + 128, :])
                dma("sp", M2[:, dc * 2048:(dc + 1) * 2048],
                    sc["l0a_kT"][b * D + dc * 128:b * D + dc * 128 + 128, :])
            for st in range(NST):
                dma("sp", W[:, st * 1024:(st + 1) * 1024],
                    sc["l0a_v"][b * S + st * 128:b * S + st * 128 + 128, :])
            load_w_bf(W, 16384, w["l0a_wo_t"], D, D)

            def stage1(qt, par):
                sm = 32 * par
                for kc4 in range(4):
                    pp = kc4 % 2
                    for dc in range(8):
                        mm(PS[pp][:, :512],
                           M1[:, dc * 2048 + qt * 128:dc * 2048 + qt * 128 + 128],
                           M2[:, dc * 2048 + kc4 * 512:dc * 2048 + (kc4 + 1) * 512],
                           dc == 0, dc == 7)
                    sbuf_sc = F1 if par == 0 else F2
                    v_ts(sbuf_sc[:, kc4 * 512:(kc4 + 1) * 512],
                         PS[pp][:, :512], 1.0, -3.0e38, ALU.mult, ALU.max,
                         accum=SM[:, sm + kc4:sm + kc4 + 1])
                v_red(SM[:, sm + 4:sm + 5], SM[:, sm:sm + 4], ALU.max)
                v_ts(SM[:, sm + 5:sm + 6], SM[:, sm + 4:sm + 5], -1.0 / 32.0,
                     None, ALU.mult)

            def stage2(qt, par):
                sm = 32 * par
                prb0 = 24576 + par * 2048
                ptb0 = 28672 + par * 2048
                sbuf_sc = F1 if par == 0 else F2
                a_act(APx(W[:], prb0, [[32768, 128], [1, 2048]]),
                      sbuf_sc[:, 0:2048], AF.Exp,
                      bias=SM[:, sm + 5:sm + 6], scale=1.0 / 32.0,
                      accum=SM[:, sm + 6:sm + 7])
                v_recip(SM[:, sm + 7:sm + 8], SM[:, sm + 6:sm + 7])
                v_ts(DG[:, par * 128:(par + 1) * 128], ident_bf[:],
                     SM[:, sm + 7:sm + 8], None, ALU.mult)
                for kc in range(16):
                    pp = 4 + kc % 2
                    mm(PS[pp][:, :128],
                       APx(W[:], prb0 + kc * 128, [[32768, 128], [1, 128]]),
                       DG[:, par * 128:(par + 1) * 128], True, True)
                    a_copy(APx(W[:], ptb0 + kc * 128, [[32768, 128], [1, 128]]),
                           PS[pp][:, :128])
                for dc in range(8):
                    pp = 4 + dc % 2
                    for kc in range(16):
                        mm(PS[pp][:, :128],
                           W[:, kc * 1024 + dc * 128:kc * 1024 + dc * 128 + 128],
                           APx(W[:], ptb0 + kc * 128, [[32768, 128], [1, 128]]),
                           kc == 0, kc == 15)
                    v_copy(HB[:, par * 2048 + dc * 128:par * 2048 + (dc + 1) * 128],
                           PS[pp][:, :128])
                for nh2 in range(2):
                    pp = 2 + nh2
                    for dc in range(8):
                        mm(PS[pp][:, :512],
                           HB[:, par * 2048 + dc * 128:par * 2048 + (dc + 1) * 128],
                           W[:, 16384 + dc * 1024 + nh2 * 512:
                             16384 + dc * 1024 + nh2 * 512 + 512],
                           dc == 0, dc == 7)
                    v_copy(CN[:, par * 1024 + nh2 * 512:par * 1024 + (nh2 + 1) * 512],
                           PS[pp][:, :512])
                dma("act", sc["s_tm"][b * S + qt * 128:b * S + qt * 128 + 128, :],
                    CN[:, par * 1024:par * 1024 + 1024])

            for qt in range(NST):
                stage1(qt, qt % 2)
                if qt >= 1:
                    stage2(qt - 1, (qt - 1) % 2)
            stage2(NST - 1, (NST - 1) % 2)

        if not stopped():
            for b in range(BC):
                ph_l0a_proj(b)
        phase_done("l0a_proj")
        if not stopped():
            for b in range(BC):
                ph_l0a_attn(b)
        phase_done("l0a")

        def ph_rT():
            for dc in range(8):
                dma("g", M2[:, dc * 2048:(dc + 1) * 2048],
                    w["c_pet"][dc * 128:(dc + 1) * 128, :])
                dma("g", W[:, 24576 + dc * 1024:24576 + (dc + 1) * 1024],
                    w["l0f_wr"][dc * 128:(dc + 1) * 128, :])
            for ec in range(8):
                for s4 in range(4):
                    pp = s4 % 2
                    for dc in range(8):
                        mm(PS[pp][:, :512],
                           W[:, 24576 + dc * 1024 + ec * 128:
                             24576 + dc * 1024 + ec * 128 + 128],
                           M2[:, dc * 2048 + s4 * 512:dc * 2048 + (s4 + 1) * 512],
                           dc == 0, dc == 7)
                    v_copy(PTb[:, s4 * 512:(s4 + 1) * 512], PS[pp][:, :512])
                dma("act", sc["rT"][ec * 128:(ec + 1) * 128, :], PTb[:])

        def ph_l0f_qkv(b):
            load_xnT_M1(b, sc["xnT"])
            for dc in range(8):
                dma("g", W[:, dc * 3072:(dc + 1) * 3072],
                    w["l0f_wqkv"][dc * 128:(dc + 1) * 128, :])
            for cbase, dstd in ((0, sc["l0f_qT"]), (D, sc["l0f_kT"])):
                for ec in range(8):
                    for s4 in range(4):
                        pp = s4 % 2
                        for dc in range(8):
                            mm(PS[pp][:, :512],
                               W[:, dc * 3072 + cbase + ec * 128:
                                 dc * 3072 + cbase + ec * 128 + 128],
                               M1[:, dc * 2048 + s4 * 512:
                                  dc * 2048 + (s4 + 1) * 512],
                               dc == 0, dc == 7)
                        v_copy(PTb[:, s4 * 512:(s4 + 1) * 512], PS[pp][:, :512])
                    dma("act", dstd[b * D + ec * 128:b * D + ec * 128 + 128, :],
                        PTb[:])
            proj_v(b, lambda dc, nh2: W[:, dc * 3072 + 2 * D + nh2 * 512:
                                        dc * 3072 + 2 * D + nh2 * 512 + 512],
                   sc["l0f_v"])

        if not stopped():
            ph_rT()
            for b in range(BC):
                ph_l0f_qkv(b)
        phase_done("l0f_qkv")

        def ph_l0f_attn(b):
            for h in range(NH):
                dc = h // 2
                rb = (h % 2) * 64
                qu = APx(M2[:], 0, [[16384, 64], [1, 2048]])
                qv = APx(M2[:], 2048, [[16384, 64], [1, 2048]])
                kh = APx(M2[:], 4096, [[16384, 64], [1, 2048]])
                rh = APx(M2[:], 6144, [[16384, 64], [1, 2048]])
                vh = M2[:, 8192:9216]
                qu2 = APx(M2[:], 13312, [[16384, 64], [1, 2048]])
                dma("sp", qu, sc["l0f_qT"][b * D + h * 64:b * D + h * 64 + 64, :])
                dma("sp", kh, sc["l0f_kT"][b * D + h * 64:b * D + h * 64 + 64, :])
                dma("sp", rh, sc["rT"][h * 64:h * 64 + 64, :])
                vsrc = APx(sc["l0f_v"], (b * S) * D + h * 64,
                           [[D, 128], [128 * D, 16], [1, 64]])
                dma("sp", vh.rearrange("p (c n) -> p c n", n=64), vsrc)
                v_ts(qv, qu, vbt[:, h:h + 1], None, ALU.add)
                v_ts(qu2, qu, ubt[:, h:h + 1], None, ALU.add)

                def stage1a(qt, par):
                    kw = (qt + 1) * 128
                    nk4 = cdiv(kw, 512)
                    c0 = 2048 - kw
                    bdbuf = sc["bd0"] if (h * NST + qt) % 2 == 0 else sc["bd1"]
                    bst = 18432 + par * 2048
                    bsh = 12288 + par * 2048
                    for i4 in range(nk4):
                        cw = min(512, kw - i4 * 512)
                        cs = c0 + i4 * 512
                        pp = 2 + i4 % 2
                        mm(PS[pp][:, :cw],
                           APx(M2[:], 2048 + qt * 128, [[16384, 64], [1, 128]]),
                           APx(M2[:], 6144 + cs, [[16384, 64], [1, cw]]),
                           True, True)
                        a_copy(APx(W[:], bst + cs - c0, [[32768, 128], [1, cw]]),
                               PS[pp][:, :cw])
                    dma("act", APx(bdbuf, c0, [[RS, 128], [1, kw]]),
                        APx(W[:], bst, [[32768, 128], [1, kw]]))
                    dma("sp", APx(W[:], bsh, [[32768, 128], [1, kw]]),
                        APx(bdbuf, 2047 - qt * 128, [[RS - 1, 128], [1, kw]]))

                def stage1b(qt, par):
                    sm = 16 * par
                    sbuf_sc = (F1, F2, CN)[par]
                    kw = (qt + 1) * 128
                    nk4 = cdiv(kw, 512)
                    bsh = 12288 + par * 2048
                    for i4 in range(nk4):
                        cw = min(512, kw - i4 * 512)
                        pp = i4 % 2
                        mm(PS[pp][:, :cw],
                           APx(M2[:], 13312 + qt * 128, [[16384, 64], [1, 128]]),
                           APx(M2[:], 4096 + i4 * 512, [[16384, 64], [1, cw]]),
                           True, True)
                        v_tt(sbuf_sc[:, i4 * 512:i4 * 512 + cw], PS[pp][:, :cw],
                             APx(W[:], bsh + i4 * 512, [[32768, 128], [1, cw]]),
                             ALU.add)
                    v_copy(F2[:, 2560:2688], sbuf_sc[:, kw - 128:kw])
                    v_sel(sbuf_sc[:, kw - 128:kw], cmask[:], F2[:, 2560:2688],
                          negT[:])
                    v_red(SM[:, sm + 0:sm + 1], sbuf_sc[:, 0:kw], ALU.max)
                    v_ts(SM[:, sm + 1:sm + 2], SM[:, sm + 0:sm + 1], -0.125,
                         None, ALU.mult)

                def stage2(qt, par):
                    sm = 16 * par
                    sbuf_sc = (F1, F2, CN)[par]
                    kw = (qt + 1) * 128
                    prb0 = par * 2048
                    ptb0 = 6144 + par * 2048
                    a_act(APx(W[:], prb0, [[32768, 128], [1, kw]]),
                          sbuf_sc[:, 0:kw], AF.Exp, bias=SM[:, sm + 1:sm + 2],
                          scale=0.125, accum=SM[:, sm + 2:sm + 3])
                    v_recip(SM[:, sm + 3:sm + 4], SM[:, sm + 2:sm + 3])
                    v_ts(DG[:, par * 128:(par + 1) * 128], ident_bf[:],
                         SM[:, sm + 3:sm + 4], None, ALU.mult)
                    for kc in range(qt + 1):
                        pp = 4 + kc % 2
                        mm(PS[pp][:, :128],
                           APx(W[:], prb0 + kc * 128, [[32768, 128], [1, 128]]),
                           DG[:, par * 128:(par + 1) * 128], True, True)
                        a_copy(APx(W[:], ptb0 + kc * 128,
                                   [[32768, 128], [1, 128]]), PS[pp][:, :128])
                    for kc in range(qt + 1):
                        mm(PS[6][0:64, :128], vh[:, kc * 64:(kc + 1) * 64],
                           APx(W[:], ptb0 + kc * 128, [[32768, 128], [1, 128]]),
                           kc == 0, kc == qt)
                    v_copy(M1[rb:rb + 64,
                              dc * 2048 + qt * 128:dc * 2048 + qt * 128 + 128],
                           PS[6][0:64, :128])

                stage1a(0, 0)
                stage1a(1, 1)
                for qt in range(NST):
                    if qt + 2 < NST:
                        stage1a(qt + 2, (qt + 2) % 3)
                    stage1b(qt, qt % 3)
                    if qt >= 1:
                        stage2(qt - 1, (qt - 1) % 3)
                stage2(NST - 1, (NST - 1) % 3)

        def ph_l0f_h1(b):
            if "ctx0T" in dumps:
                for dc in range(8):
                    dma("g", sc["ctx0T"][b * D + dc * 128:b * D + dc * 128 + 128, :],
                        M1[:, dc * 2048:(dc + 1) * 2048])
            load_w_bf(W, 0, w["l0f_wo"], D, D)
            load_lgb("l0f_ln1g", "l0f_ln1b")
            for st in range(NST):
                r0 = b * S + st * 128
                for nh2 in range(2):
                    pp = nh2
                    for dc in range(8):
                        mm(PS[pp][:, :512],
                           M1[:, dc * 2048 + st * 128:dc * 2048 + st * 128 + 128],
                           W[:, dc * 1024 + nh2 * 512:dc * 1024 + nh2 * 512 + 512],
                           dc == 0, dc == 7)
                dma("sp", F2[:, 0:1024], sc["xn_tm"][r0:r0 + 128, :])
                for nh2 in range(2):
                    v_tt(F3[:, nh2 * 512:(nh2 + 1) * 512], PS[nh2][:, :512],
                         F2[:, nh2 * 512:(nh2 + 1) * 512], ALU.add)
                ln_tile(F3[:, 0:1024], F3[:, 1024:2048], 0, affine=True)
                dma("g", sc["h1_tm"][r0:r0 + 128, :], F3[:, 1024:2048])
                a_act(Bq[:, 0:1024], F3[:, 1024:2048], AF.Identity)
                t8_store(Bq[:, 0:1024], sc["h1T"], b * D, st * 128)

        def ph_l0f_ffn(b):
            for dc in range(8):
                dma("g", W[:, dc * 4096:(dc + 1) * 4096],
                    w["l0f_wff1"][dc * 128:(dc + 1) * 128, :])
            for fc in range(32):
                arena = M1 if fc < 16 else M2
                base = (fc % 16) * 1024
                dma("g", arena[:, base:base + 1024],
                    w["l0f_wff2"][fc * 128:(fc + 1) * 128, :])
            dma("g", BR[:], w["l0f_bff2"])
            load_lgb("l0f_ln2g", "l0f_ln2b")
            for st in range(NST):
                r0 = b * S + st * 128
                hsrc = APx(sc["h1T"], (b * D) * S + st * 128,
                           [[S, 128], [128 * S, 8], [1, 128]])
                dma("sp", APx(HB2[:], 0, [[2048, 128], [128, 8], [1, 128]]), hsrc)
                for fc in range(32):
                    pp = fc % 2
                    for dc in range(8):
                        mm(PS[pp][:, :128],
                           W[:, dc * 4096 + fc * 128:dc * 4096 + fc * 128 + 128],
                           HB2[:, dc * 128:(dc + 1) * 128], dc == 0, dc == 7)
                    a_act(HB[:, fc * 128:(fc + 1) * 128], PS[pp][:, :128],
                          AF.Relu, bias=bff1_sb[:, fc:fc + 1])
                for nh2 in range(2):
                    pp = 2 + nh2
                    mm(PS[pp][:, :512], ones_bf[:],
                       BR[:, nh2 * 512:(nh2 + 1) * 512], True, False)
                    for fc in range(32):
                        arena = M1 if fc < 16 else M2
                        base = (fc % 16) * 1024
                        mm(PS[pp][:, :512], HB[:, fc * 128:(fc + 1) * 128],
                           arena[:, base + nh2 * 512:base + nh2 * 512 + 512],
                           False, fc == 31)
                dma("sp", F2[:, 0:1024], sc["h1_tm"][r0:r0 + 128, :])
                for nh2 in range(2):
                    v_tt(F3[:, nh2 * 512:(nh2 + 1) * 512], PS[2 + nh2][:, :512],
                         F2[:, nh2 * 512:(nh2 + 1) * 512], ALU.add)
                ln_tile(F3[:, 0:1024], F3[:, 1024:2048], 0, affine=True)
                dma("g", sc["h2_tm"][r0:r0 + 128, :], F3[:, 1024:2048])
                a_act(Bq[:, 0:1024], F3[:, 1024:2048], AF.Identity)
                t8_store(Bq[:, 0:1024], sc["h2T"], b * D, st * 128)

        def ph_l0f_gate(b):
            for dc in range(8):
                dma("g", WGS[:, dc * 8:(dc + 1) * 8],
                    w["l0f_wg"][dc * 128:(dc + 1) * 128, :])
            for st in range(NST):
                r0 = b * S + st * 128
                hsrc = APx(sc["h2T"], (b * D) * S + st * 128,
                           [[S, 128], [128 * S, 8], [1, 128]])
                dma("sp", APx(HB2[:], 0, [[2048, 128], [128, 8], [1, 128]]), hsrc)
                for dc in range(8):
                    mm(PS[0][:, :8], HB2[:, dc * 128:(dc + 1) * 128],
                       WGS[:, dc * 8:(dc + 1) * 8], dc == 0, dc == 7)
                v_copy(SM3[:, 0:8], PS[0][:, :8])
                v_red(SM3[:, 8:9], SM3[:, 0:8], ALU.max)
                v_ts(SM3[:, 9:10], SM3[:, 8:9], -1.0, None, ALU.mult)
                a_act(SM3[:, 16:24], SM3[:, 0:8], AF.Exp, bias=SM3[:, 9:10],
                      accum=SM3[:, 10:11])
                v_recip(SM3[:, 11:12], SM3[:, 10:11])
                v_ts(SM3[:, 24:32], SM3[:, 16:24], SM3[:, 11:12], None, ALU.mult)
                v_red(SM3[:, 12:13], SM3[:, 24:32], ALU.max)
                v_ts(SM3[:, 32:40], SM3[:, 24:32], SM3[:, 12:13], None, ALU.is_ge)
                v_copy(SM3[:, 13:14], zrow[:])
                for ee in range(NE):
                    ra = 13 + (ee % 2)
                    rb2 = 13 + ((ee + 1) % 2)
                    v_ts(SM3[:, 15:16], SM3[:, ra:ra + 1], -1.0, 1.0, ALU.mult,
                         ALU.add)
                    v_tt(SM3[:, 40 + ee:41 + ee], SM3[:, 32 + ee:33 + ee],
                         SM3[:, 15:16], ALU.mult)
                    v_tt(SM3[:, rb2:rb2 + 1], SM3[:, ra:ra + 1],
                         SM3[:, 40 + ee:41 + ee], ALU.add)
                v_ts(MME[:, 0:8], SM3[:, 40:48], SM3[:, 12:13], None, ALU.mult)
                dma("g", sc["m_tm"][r0:r0 + 128, :], MME[:, 0:8])

        if not stopped():
            for b in range(BC):
                ph_l0f_attn(b)
                ph_l0f_h1(b)
        phase_done("l0f_h1")
        if not stopped():
            for b in range(BC):
                ph_l0f_ffn(b)
                ph_l0f_gate(b)
        phase_done("l0f_h2")

        def ph_moe():
            for e_ in range(NE):
                for dc in range(8):
                    dma("g", W[:, dc * 4096:(dc + 1) * 4096],
                        w["l0f_we1"][e_, dc * 128:(dc + 1) * 128, :])
                for fc in range(32):
                    arena = M1 if fc < 16 else M2
                    base = (fc % 16) * 1024
                    dma("g", arena[:, base:base + 1024],
                        w["l0f_we2"][e_, fc * 128:(fc + 1) * 128, :])
                dma("sp", be1_sb[:], w["l0f_be1"][e_])
                dma("g", BR[:], w["l0f_be2"][e_])
                for b in range(BC):
                    for st in range(NST):
                        r0 = b * S + st * 128
                        p2 = st % 2
                        hb2 = HB2[:, p2 * 1024:(p2 + 1) * 1024]
                        mme = MME[:, p2 * 8:(p2 + 1) * 8]
                        hsrc = APx(sc["h2T"], (b * D) * S + st * 128,
                                   [[S, 128], [128 * S, 8], [1, 128]])
                        dma("sp", APx(HB2[:], p2 * 1024,
                                      [[2048, 128], [128, 8], [1, 128]]), hsrc)
                        dma("sp", mme, sc["m_tm"][r0:r0 + 128, :])
                        for fc in range(32):
                            pp = fc % 2
                            for dc in range(8):
                                mm(PS[pp][:, :128],
                                   W[:, dc * 4096 + fc * 128:
                                     dc * 4096 + fc * 128 + 128],
                                   hb2[:, dc * 128:(dc + 1) * 128],
                                   dc == 0, dc == 7)
                            a_act(HB[:, fc * 128:(fc + 1) * 128], PS[pp][:, :128],
                                  AF.Relu, bias=be1_sb[:, fc:fc + 1])
                        for nh2 in range(2):
                            mm(PS[2 + nh2][:, :512], ones_bf[:],
                               BR[:, nh2 * 512:(nh2 + 1) * 512], True, False)
                        for fc in range(32):
                            arena = M1 if fc < 16 else M2
                            base = (fc % 16) * 1024
                            for nh2 in range(2):
                                mm(PS[2 + nh2][:, :512],
                                   HB[:, fc * 128:(fc + 1) * 128],
                                   arena[:, base + nh2 * 512:
                                         base + nh2 * 512 + 512],
                                   False, fc == 31)
                        for nh2 in range(2):
                            v_ts(F2[:, p2 * 1024 + nh2 * 512:
                                    p2 * 1024 + (nh2 + 1) * 512],
                                 PS[2 + nh2][:, :512], mme[:, e_:e_ + 1], None,
                                 ALU.mult)
                        if e_ == 0:
                            dma("g", sc["moe_tm"][r0:r0 + 128, :],
                                F2[:, p2 * 1024:(p2 + 1) * 1024])
                        else:
                            dma_acc(sc["moe_tm"][r0:r0 + 128, :],
                                    F2[:, p2 * 1024:(p2 + 1) * 1024])
                        if st % 4 == 3 and WQ_TASKS:
                            emit_wq_tile(*WQ_TASKS.pop())

        if not stopped():
            ph_moe()
            while WQ_TASKS:
                emit_wq_tile(*WQ_TASKS.pop())
        phase_done("moe")

        def ph_l0_final(b):
            load_w_bf(W, 0, w["l0f_wout"], D, D)
            load_lgb("l0f_ln3g", "l0f_ln3b")
            for st in range(NST):
                r0 = b * S + st * 128
                dma("sp", F2[:, 0:1024], sc["h2_tm"][r0:r0 + 128, :])
                dma("sp", F1[:, 0:1024], sc["moe_tm"][r0:r0 + 128, :])
                v_tt(F3[:, 0:1024], F2[:, 0:1024], F1[:, 0:1024], ALU.add)
                ln_tile(F3[:, 0:1024], F3[:, 1024:2048], 0, affine=True)
                a_act(Bq[:, 0:1024], F3[:, 1024:2048], AF.Identity)
                for dc in range(8):
                    trans(PB[0][:, :128], Bq[:, dc * 128:(dc + 1) * 128],
                          ident_bf[:])
                    v_copy(HB[:, dc * 128:(dc + 1) * 128], PB[0][:, :128])
                for nh2 in range(2):
                    pp = nh2
                    for dc in range(8):
                        mm(PS[pp][:, :512], HB[:, dc * 128:(dc + 1) * 128],
                           W[:, dc * 1024 + nh2 * 512:dc * 1024 + nh2 * 512 + 512],
                           dc == 0, dc == 7)
                    a_act(F2[:, 1024 + nh2 * 512:1024 + (nh2 + 1) * 512],
                          PS[pp][:, :512], AF.Silu)
                dma("sp", F1[:, 1024:2048], sc["s_tm"][r0:r0 + 128, :])
                dma("sp", CN[:, 0:1024], w["x"][r0:r0 + 128, :])
                v_tt(F2[:, 0:1024], F1[:, 1024:2048], F2[:, 1024:2048], ALU.mult)
                v_tt(F3[:, 0:1024], CN[:, 0:1024], F2[:, 0:1024], ALU.add)
                dma("g", sc["x1_tm"][r0:r0 + 128, :], F3[:, 0:1024])

        if not stopped():
            for b in range(BC):
                ph_l0_final(b)
        phase_done("l0")

        if not stopped():
            for b in range(BC):
                ph_ln_in(b, sc["x1_tm"], sc["xn1_tm"], sc["xn1T"])
        phase_done("ln1")

        def ph_l1_proj(b):
            load_xnT_M1(b, sc["xn1T"])
            load_w_bf(W, 0, w["l1a_wq"], D, D)
            load_w_bf(W, 8192, w["l1a_wk"], D, D)
            load_w_bf(W, 16384, w["l1a_wv"], D, D)
            for wbase, dstd, dstn in ((0, sc["l1_qT"], sc["l1_qn"]),
                                      (8192, sc["l1_kT"], sc["l1_kn"])):
                proj_rope(b, wbase, w["c_cos1"], w["c_sin1"], dstd, qnpsum=True)
                for s4 in range(4):
                    v_copy(F1[0:16, s4 * 512:(s4 + 1) * 512],
                           PS[2 + s4][0:16, :512])
                dma("g", dstn[b * NH:(b + 1) * NH, :], F1[0:16, 0:2048])
                if wbase == 0:
                    for qt in range(NST):
                        pp = qt % 2
                        trans(PS[pp][:, :16], F1[0:16, qt * 128:(qt + 1) * 128],
                              ident_f[0:16, 0:16])
                        v_copy(QNS[:], PS[pp][:, :16])
                        dma("g", sc["l1_qnT"][b * S + qt * 128:
                                              b * S + qt * 128 + 128, :], QNS[:])
            proj_v(b, lambda dc, nh2: W[:, 16384 + dc * 1024 + nh2 * 512:
                                        16384 + dc * 1024 + nh2 * 512 + 512],
                   sc["l1_v"])

        def ph_l1_attn(b):
            for h in range(DEBUG_L1_HEADS):
                dc = h // 2
                rb = (h % 2) * 64
                qh = APx(M2[:], 0, [[16384, 64], [1, 2048]])
                kh = APx(M2[:], 2048, [[16384, 64], [1, 2048]])
                vh = M2[:, 4096:5120]
                q2h = APx(M2[:], 5120, [[16384, 64], [1, 2048]])
                knr = APx(M2[:], 7168, [[16384, 1], [1, 2048]])
                qnr = APx(M2[:], 9216, [[16384, 1], [1, 2048]])
                dma("sp", qh, sc["l1_qT"][b * D + h * 64:b * D + h * 64 + 64, :])
                dma("sp", kh, sc["l1_kT"][b * D + h * 64:b * D + h * 64 + 64, :])
                vsrc = APx(sc["l1_v"], (b * S) * D + h * 64,
                           [[D, 128], [128 * D, 16], [1, 64]])
                dma("sp", vh.rearrange("p (c n) -> p c n", n=64), vsrc)
                v_ts(q2h, qh, -2.0, None, ALU.mult)
                dma("sp", F1[0:1, 0:2048],
                    sc["l1_kn"][b * NH + h:b * NH + h + 1, :])
                dma("sp", CN[0:1, 0:2048],
                    sc["l1_qn"][b * NH + h:b * NH + h + 1, :])
                v_ts(knr, F1[0:1, 0:2048], 1e-5, None, ALU.add)
                v_copy(qnr, CN[0:1, 0:2048])

                def stage1(qt, par):
                    sm = 32 * par
                    sbuf_sc = F1 if par == 0 else F3
                    kw = (qt + 1) * 128
                    nk4 = cdiv(kw, 512)
                    for i4 in range(nk4):
                        cw = min(512, kw - i4 * 512)
                        pp = i4 % 2
                        mm(PS[pp][:, :cw],
                           APx(M2[:], 0 + qt * 128, [[16384, 64], [1, 128]]),
                           APx(M2[:], 2048 + i4 * 512, [[16384, 64], [1, cw]]),
                           True, True)
                        mm(PS[2 + pp][:, :cw],
                           APx(M2[:], 5120 + qt * 128, [[16384, 64], [1, 128]]),
                           APx(M2[:], 2048 + i4 * 512, [[16384, 64], [1, cw]]),
                           True, False)
                        mm(PS[2 + pp][:, :cw], ones_bf[:],
                           APx(M2[:], 7168 + i4 * 512, [[16384, 1], [1, cw]]),
                           False, False)
                        mm(PS[2 + pp][:, :cw],
                           APx(M2[:], 9216 + qt * 128, [[16384, 1], [1, 128]]),
                           ones2_bf[:, 0:cw], False, True)
                        a_act(F2[:, i4 * 512:i4 * 512 + cw], PS[pp][:, :cw],
                              AF.Square)
                        if i4 % 3 == 2:
                            v_copy(F3[:, 2304:2304 + cw], PS[2 + pp][:, :cw])
                            v_recip(CN[:, i4 * 512:i4 * 512 + cw],
                                    F3[:, 2304:2304 + cw])
                        else:
                            a_act(F3[:, 2304:2304 + cw], PS[2 + pp][:, :cw],
                                  AF.Ln)
                            a_act(CN[:, i4 * 512:i4 * 512 + cw],
                                  F3[:, 2304:2304 + cw], AF.Exp, scale=-1.0)
                        v_stt(sbuf_sc[:, i4 * 512:i4 * 512 + cw],
                              F2[:, i4 * 512:i4 * 512 + cw], alpha_sc[:],
                              CN[:, i4 * 512:i4 * 512 + cw], ALU.mult,
                              ALU.mult)
                    v_copy(F2[:, 2560:2688], sbuf_sc[:, kw - 128:kw])
                    v_sel(sbuf_sc[:, kw - 128:kw], cmask[:], F2[:, 2560:2688],
                          negT[:])
                    v_red(SM[:, sm + 0:sm + 1], sbuf_sc[:, 0:kw], ALU.max)
                    v_ts(SM[:, sm + 1:sm + 2], SM[:, sm + 0:sm + 1], -1.0,
                         None, ALU.mult)

                def stage2(qt, par):
                    sm = 32 * par
                    sbuf_sc = F1 if par == 0 else F3
                    kw = (qt + 1) * 128
                    prb0 = par * 2048
                    ptb0 = 4096 + par * 2048
                    a_act(APx(W[:], prb0, [[32768, 128], [1, kw]]),
                          sbuf_sc[:, 0:kw], AF.Exp, bias=SM[:, sm + 1:sm + 2],
                          accum=SM[:, sm + 2:sm + 3])
                    v_recip(SM[:, sm + 3:sm + 4], SM[:, sm + 2:sm + 3])
                    v_ts(DG[:, par * 128:(par + 1) * 128], ident_bf[:],
                         SM[:, sm + 3:sm + 4], None, ALU.mult)
                    for kc in range(qt + 1):
                        pp = 4 + kc % 2
                        mm(PS[pp][:, :128],
                           APx(W[:], prb0 + kc * 128, [[32768, 128], [1, 128]]),
                           DG[:, par * 128:(par + 1) * 128], True, True)
                        dstp = APx(W[:], ptb0 + kc * 128,
                                   [[32768, 128], [1, 128]])
                        if kc % 2 == 0:
                            v_copy(dstp, PS[pp][:, :128])
                        else:
                            a_copy(dstp, PS[pp][:, :128])
                    for kc in range(qt + 1):
                        mm(PS[6][0:64, :128], vh[:, kc * 64:(kc + 1) * 64],
                           APx(W[:], ptb0 + kc * 128, [[32768, 128], [1, 128]]),
                           kc == 0, kc == qt)
                    a_copy(M1[rb:rb + 64,
                              dc * 2048 + qt * 128:dc * 2048 + qt * 128 + 128],
                           PS[6][0:64, :128])

                for qt in range(NST):
                    stage1(qt, qt % 2)
                    if qt >= 1:
                        stage2(qt - 1, (qt - 1) % 2)
                stage2(NST - 1, (NST - 1) % 2)

        def ph_l1_out(b):
            if "ctx1T" in dumps:
                for dc in range(8):
                    dma("g", sc["ctx1T"][b * D + dc * 128:b * D + dc * 128 + 128, :],
                        M1[:, dc * 2048:(dc + 1) * 2048])
            load_w_bf(W, 0, w["l1a_wo"], D, D)
            for st in range(NST):
                r0 = b * S + st * 128
                for nh2 in range(2):
                    pp = nh2
                    for dc in range(8):
                        mm(PS[pp][:, :512],
                           M1[:, dc * 2048 + st * 128:dc * 2048 + st * 128 + 128],
                           W[:, dc * 1024 + nh2 * 512:dc * 1024 + nh2 * 512 + 512],
                           dc == 0, dc == 7)
                    v_copy(CN[:, nh2 * 512:(nh2 + 1) * 512], PS[pp][:, :512])
                dma("act", sc["s_tm"][r0:r0 + 128, :], CN[:, 0:1024])

        if not stopped():
            for b in range(BC):
                ph_l1_proj(b)
        phase_done("l1_proj")
        if not stopped():
            for b in range(BC):
                ph_l1_attn(b)
                ph_l1_out(b)
        phase_done("l1a")

        def ph_l1f_x(b):
            for st in range(NST):
                r0 = b * S + st * 128
                dma("sp", F3[:, 1408:2432], sc["xn1_tm"][r0:r0 + 128, :])
                quant_fp4(F3[:, 1408:2432], Bq[:, 0:1024], 1024)
                t8_store(Bq[:, 0:1024], sc["xqT"], b * D, st * 128)

        def ph_l1f_h(b):
            for dcc in range(8):
                dma("sp", W[:, dcc * 2816:(dcc + 1) * 2816],
                    sc["wgate_qT"][dcc * 128:(dcc + 1) * 128, :])
                if dcc < 5:
                    dma("sp", M1[:, dcc * 2816:(dcc + 1) * 2816],
                        sc["wup_qT"][dcc * 128:(dcc + 1) * 128, :])
                else:
                    dma("sp", M2[:, (dcc - 5) * 2816:(dcc - 4) * 2816],
                        sc["wup_qT"][dcc * 128:(dcc + 1) * 128, :])

            def wup_ap(dcc, c0, cw):
                if dcc < 5:
                    return M1[:, dcc * 2816 + c0:dcc * 2816 + c0 + cw]
                return M2[:, (dcc - 5) * 2816 + c0:(dcc - 5) * 2816 + c0 + cw]

            for st in range(NST):
                r0 = b * S + st * 128
                hsrc = APx(sc["xqT"], (b * D) * S + st * 128,
                           [[S, 128], [128 * S, 8], [1, 128]])
                dma("sp", APx(HB2[:], 0, [[2048, 128], [128, 8], [1, 128]]), hsrc)
                for ncc in range(6):
                    cw = min(512, IFF - ncc * 512)
                    for dcc in range(8):
                        mm(PS[0][:, :cw], HB2[:, dcc * 128:(dcc + 1) * 128],
                           W[:, dcc * 2816 + ncc * 512:
                             dcc * 2816 + ncc * 512 + cw], dcc == 0, dcc == 7)
                    for dcc in range(8):
                        mm(PS[1][:, :cw], HB2[:, dcc * 128:(dcc + 1) * 128],
                           wup_ap(dcc, ncc * 512, cw), dcc == 0, dcc == 7)
                    a_act(CN[:, 0:cw], PS[0][:, :cw], AF.Silu)
                    v_tt(F3[:, ncc * 512:ncc * 512 + cw], CN[:, 0:cw],
                         PS[1][:, :cw], ALU.mult)
                quant_fp4(F3[:, 0:1408], Bq[:, 0:1408], 1408)
                quant_fp4(F3[:, 1408:2816], Bq[:, 1408:2816], 1408)
                for g2 in range(3):
                    n8 = min(8, 22 - g2 * 8)
                    for dc2 in range(n8):
                        di = g2 * 8 + dc2
                        trans(PB[0][:, :128], Bq[:, di * 128:(di + 1) * 128],
                              ident_bf[:])
                        v_copy(PTb[:, dc2 * 128:(dc2 + 1) * 128], PB[0][:, :128])
                    dsth = APx(sc["hqT"], (b * IFF + g2 * 8 * 128) * S + st * 128,
                               [[S, 128], [128 * S, n8], [1, 128]])
                    dma("act", dsth,
                        PTb[:, 0:n8 * 128].rearrange("p (c n) -> p c n", n=128))

        def ph_l1f_down(b):
            for dcc in range(22):
                dma("sp", W[:, dcc * 1024:(dcc + 1) * 1024],
                    sc["wdown_qT"][dcc * 128:(dcc + 1) * 128, :])
            for st in range(NST):
                r0 = b * S + st * 128
                hsrc = APx(sc["hqT"], (b * IFF) * S + st * 128,
                           [[S, 128], [128 * S, 22], [1, 128]])
                dma("sp", HB[:, 0:22 * 128].rearrange("p (c n) -> p c n", n=128),
                    hsrc)
                for nh2 in range(2):
                    pp = nh2
                    for dcc in range(22):
                        mm(PS[pp][:, :512], HB[:, dcc * 128:(dcc + 1) * 128],
                           W[:, dcc * 1024 + nh2 * 512:
                             dcc * 1024 + nh2 * 512 + 512],
                           dcc == 0, dcc == 21)
                    a_act(F2[:, 1024 + nh2 * 512:1024 + (nh2 + 1) * 512],
                          PS[pp][:, :512], AF.Silu)
                dma("sp", F1[:, 0:1024], sc["s_tm"][r0:r0 + 128, :])
                dma("sp", F3[:, 0:1024], sc["x1_tm"][r0:r0 + 128, :])
                v_tt(F2[:, 0:1024], F1[:, 0:1024], F2[:, 1024:2048], ALU.mult)
                v_tt(F3[:, 1024:2048], F3[:, 0:1024], F2[:, 0:1024], ALU.add)
                o = out[r0:r0 + 128, :]
                dma("g", o, F3[:, 1024:2048])
                ALL_OUT.append(o)

        if not stopped():
            for b in range(BC):
                ph_l1f_x(b)
        phase_done("l1f_x")
        if not stopped():
            for b in range(BC):
                ph_l1f_h(b)
                ph_l1f_down(b)
        phase_done("l1f")

        sch.add("g", lambda e: e.memset(zrow[:], 0.0), ins=ALL_OUT,
                outs=[zrow[:]], tag="fin")

        sch.finalize(DMA_SLOTS)

        @blk.tensor
        def _(t):
            sch.emit("pe", t, sems)

        @blk.vector
        def _(v):
            sch.emit("dve", v, sems)

        @blk.scalar
        def _(a):
            sch.emit("act", a, sems)

        @blk.sync
        def _(s):
            sch.emit("sp", s, sems)

        @blk.gpsimd
        def _(g):
            sch.emit("g", g, sems)

    return nc


def _host_consts():
    c = {}
    c["c_ident"] = np.eye(128, dtype=np.float32)
    A = np.zeros((128, 128), dtype=np.float32)
    for i in range(64):
        A[2 * i, 2 * i + 1] = -1.0
        A[2 * i + 1, 2 * i] = 1.0
    c["c_rotA"] = np.ascontiguousarray(A.T)
    inv0 = 1.0 / (10000.0 ** (np.arange(0, D, 2, dtype=np.float64) / D))
    t = np.arange(S, dtype=np.float64)
    ang0 = np.outer(inv0, t)
    c["c_cos0"] = np.repeat(np.cos(ang0), 2, axis=0).astype(np.float32)
    c["c_sin0"] = np.repeat(np.sin(ang0), 2, axis=0).astype(np.float32)
    inv1 = 1.0 / (10000.0 ** (np.arange(0, DH, 2, dtype=np.float64) / DH))
    ang1 = np.outer(inv1, t)
    c["c_cos1"] = np.tile(np.repeat(np.cos(ang1), 2, axis=0), (NH, 1)).astype(
        np.float32)
    c["c_sin1"] = np.tile(np.repeat(np.sin(ang1), 2, axis=0), (NH, 1)).astype(
        np.float32)
    pos = (S - 1.0) - np.arange(S, dtype=np.float64)
    angp = np.outer(inv0, pos)
    c["c_pet"] = np.concatenate([np.sin(angp), np.cos(angp)], axis=0).astype(
        np.float32)
    hmv = np.zeros((128, 8 * NH), dtype=np.float32)
    for dc in range(8):
        hmv[0:64, dc * NH + 2 * dc] = 1.0
        hmv[64:128, dc * NH + 2 * dc + 1] = 1.0
    c["c_hm"] = hmv
    c["c_cmask"] = np.tril(np.ones((128, 128), dtype=np.uint8))
    c["c_ones"] = np.ones((1, 128), dtype=np.float32)
    c["c_onesf"] = np.ones((1, 128), dtype=np.float32)
    c["c_ones2"] = np.ones((1, 512), dtype=np.float32)
    return c


def _host_weights(inp):
    f32 = np.float32
    a = lambda x: np.ascontiguousarray(np.asarray(x, dtype=f32))
    w = {}
    w["l0a_wq_t"] = a(np.asarray(inp["L0_attn_wq"]).T)
    w["l0a_wk_t"] = a(np.asarray(inp["L0_attn_wk"]).T)
    w["l0a_wv_t"] = a(np.asarray(inp["L0_attn_wv"]).T)
    w["l0a_wo_t"] = a(np.asarray(inp["L0_attn_wo"]).T)
    w["l0f_wqkv"] = a(inp["L0_ffn_Wqkv"])
    w["l0f_wo"] = a(inp["L0_ffn_Wo"])
    w["l0f_wr"] = a(inp["L0_ffn_Wr"])
    w["l0f_ubt"] = a(np.asarray(inp["L0_ffn_u_bias"]).T)
    w["l0f_vbt"] = a(np.asarray(inp["L0_ffn_v_bias"]).T)
    for nm in ("ln1", "ln2", "ln3"):
        w[f"l0f_{nm}g"] = a(inp[f"L0_ffn_{nm}_g"])
        w[f"l0f_{nm}b"] = a(inp[f"L0_ffn_{nm}_b"])
    w["l0f_wff1"] = a(inp["L0_ffn_Wff1"])
    w["l0f_bff1"] = a(np.asarray(inp["L0_ffn_bff1"]).reshape(FF // 128, 128).T)
    w["l0f_wff2"] = a(inp["L0_ffn_Wff2"])
    w["l0f_bff2"] = a(np.asarray(inp["L0_ffn_bff2"]).reshape(1, D))
    w["l0f_wg"] = a(inp["L0_ffn_Wg"])
    w["l0f_we1"] = a(inp["L0_ffn_We1"])
    w["l0f_be1"] = a(np.asarray(inp["L0_ffn_be1"]).reshape(NE, FF // 128, 128)
                     .transpose(0, 2, 1))
    w["l0f_we2"] = a(inp["L0_ffn_We2"])
    w["l0f_be2"] = a(np.asarray(inp["L0_ffn_be2"]).reshape(NE, 1, D))
    w["l0f_wout"] = a(inp["L0_ffn_Wout"])
    w["l1a_wq"] = a(inp["L1_attn_Wq"])
    w["l1a_wk"] = a(inp["L1_attn_Wk"])
    w["l1a_wv"] = a(inp["L1_attn_Wv"])
    w["l1a_wo"] = a(inp["L1_attn_Wo"])
    w["l1a_alpha"] = a(np.asarray(inp["L1_attn_alpha"]).reshape(1, 1))
    w["l1f_wgate"] = a(inp["L1_ffn_w_gate"])
    w["l1f_wup"] = a(inp["L1_ffn_w_up"])
    w["l1f_wdown"] = a(inp["L1_ffn_w_down"])
    return w


def kernel(**inputs):
    nc = build_nc()
    consts = _host_consts()
    weights = _host_weights(inputs)
    x = np.ascontiguousarray(np.asarray(inputs["L0_attn_x"], dtype=np.float32))
    in_maps = []
    for c in range(8):
        m = {"x": x[c * BC:(c + 1) * BC].reshape(BC * S, D)}
        m.update(weights)
        m.update(consts)
        in_maps.append(m)
    res = run_bass_kernel_spmd(nc, in_maps, core_ids=list(range(8)))
    outs = [np.asarray(r["out"]).reshape(BC, S, D) for r in res.results]
    return np.concatenate(outs, axis=0)


# version 09 (final)
# speedup vs baseline: 1.3568x; 1.0576x over previous
import math
import numpy as np
import ml_dtypes
import bass_rust
import concourse.bass as bass
import concourse.mybir as mybir
from concourse.bass_utils import run_bass_kernel_spmd
from contextlib import ExitStack

F32 = mybir.dt.float32
BF16 = mybir.dt.bfloat16
I16 = mybir.dt.int16
U8 = mybir.dt.uint8
AF = mybir.ActivationFunctionType
ALU = mybir.AluOpType
AX = mybir.AxisListType

BC, S, D = 2, 2048, 1024
NH, DH, FF, NE, IFF = 16, 64, 4096, 8, 2816
RS = 2176
NEG = -1.0e30
NST = S // 128

DEBUG_DUMPS = ()
DEBUG_STOP = None
DEBUG_L1_HEADS = NH
DEBUG_L1_MODE = 9
SERIAL_READS = False

SB_GRAN = 512
DR_GRAN = 1 << 18
GRAN_MAP = {"SM": 16, "SM3": 16}
_PSUM_NAMES = {f"ps{i}" for i in range(8)} | {"pb0", "pb1"}


def _keys(ap):
    t = ap.tensor
    name = t.name
    a = ap.ap
    if str(ap.space) == "MemorySpace.PSUM":
        return (name,)
    if str(ap.space) == "MemorySpace.DRAM":
        off = ap.offset
        span = 1
        for st, cnt in a:
            span += abs(st) * (cnt - 1)
        lo = off // DR_GRAN
        hi = (off + span - 1) // DR_GRAN
        return tuple(f"{name}:{i}" for i in range(lo, hi + 1))
    pitch = a[0][0] if len(a) > 0 else 1
    off = ap.offset % pitch if pitch > 0 else ap.offset
    span = 1
    for st, cnt in a[1:]:
        span += abs(st) * (cnt - 1)
    g = GRAN_MAP.get(name, SB_GRAN)
    lo = off // g
    hi = (off + span - 1) // g
    return tuple(f"{name}:{i}" for i in range(lo, hi + 1))


class _Op:
    __slots__ = ("eng", "fn", "dma", "idx", "deps", "signals", "cnt", "tag",
                 "slot")


class Sched:
    def __init__(self):
        self.ops = []
        self.res = {}

    def add(self, eng, fn, ins=(), outs=(), dma=False, tag=""):
        op = _Op()
        op.eng = eng; op.fn = fn; op.dma = dma; op.tag = tag
        op.idx = len(self.ops); op.signals = False; op.cnt = 0
        deps = {}

        def dep(p):
            if p is None or p is op:
                return
            if (not p.dma) and (not op.dma) and p.eng == "pe" and eng == "pe":
                return
            if p.dma:
                k = ("D", p.eng, p.idx)
            else:
                k = (p.eng, False)
            cur = deps.get(k)
            if cur is None or p.idx > cur.idx:
                deps[k] = p

        rkeys = []
        wkeys = []
        for ap_ in ins:
            for k in _keys(ap_):
                if SERIAL_READS or k in _PSUM_NAMES:
                    wkeys.append(k)
                else:
                    rkeys.append(k)
        for ap_ in outs:
            wkeys.extend(_keys(ap_))
        for r in rkeys:
            st = self.res.get(r)
            if st is not None:
                dep(st[0])
        for wk in wkeys:
            st = self.res.get(wk)
            if st is not None:
                dep(st[0])
                for rr in st[1]:
                    dep(rr)
        op.deps = list(deps.values())
        for p in op.deps:
            p.signals = True
        for r in rkeys:
            st = self.res.get(r)
            if st is None:
                st = self.res[r] = [None, []]
            st[1].append(op)
        for wk in wkeys:
            self.res[wk] = [op, []]
        self.ops.append(op)
        return op

    def finalize(self, dma_slots):
        cnt = {}
        for op in self.ops:
            if op.dma:
                op.signals = True
                nq = cnt.get(op.eng, 0)
                cnt[op.eng] = nq + 1
                R = dma_slots[op.eng]
                op.slot = nq % R
                op.cnt = nq // R + 1
            elif op.signals:
                k = (op.eng, False)
                cnt[k] = cnt.get(k, 0) + 1
                op.cnt = cnt[k]

    def emit(self, eng, handle, sems):
        hwm = {}
        for op in self.ops:
            if op.eng != eng:
                continue
            if op.dma and op.cnt > 1:
                k = ("D", op.eng, op.slot)
                val = (op.cnt - 1) * 16
                if hwm.get(k, 0) < val:
                    handle.wait_ge(sems[k], val)
                    hwm[k] = val
            for p in op.deps:
                if p.dma:
                    k = ("D", p.eng, p.slot)
                    val = p.cnt * 16
                else:
                    k = (p.eng, False)
                    val = p.cnt
                if hwm.get(k, 0) < val:
                    handle.wait_ge(sems[k], val)
                    hwm[k] = val
            inst = op.fn(handle)
            if op.signals:
                if op.dma:
                    inst.then_inc(sems[("D", op.eng, op.slot)], 16)
                else:
                    inst.then_inc(sems[(op.eng, False)], 1)


def APx(base_ap, offset, pattern):
    return bass_rust.AP(base_ap.tensor, offset, pattern)


def cdiv(a, b):
    return (a + b - 1) // b


def build_nc():
    nc = bass.Bass("TRN2", target_bir_lowering=False, debug=False)
    dumps = set(DEBUG_DUMPS)

    def din(name, shape, dtype=F32):
        return nc.dram_tensor(name, shape, dtype, kind="ExternalInput").ap()

    w = {}
    w["x"] = din("x", [BC * S, D])
    for item in [
        ("l0a_wq_t", [D, D]), ("l0a_wk_t", [D, D]), ("l0a_wv_t", [D, D]),
        ("l0a_wo_t", [D, D]),
        ("l0f_wqkv", [D, 3 * D]), ("l0f_wo", [D, D]), ("l0f_wr", [D, D]),
        ("l0f_ubt", [DH, NH]), ("l0f_vbt", [DH, NH]),
        ("l0f_ln1g", [D]), ("l0f_ln1b", [D]), ("l0f_ln2g", [D]), ("l0f_ln2b", [D]),
        ("l0f_ln3g", [D]), ("l0f_ln3b", [D]),
        ("l0f_wff1", [D, FF]), ("l0f_bff1", [128, FF // 128]),
        ("l0f_wff2", [FF, D]), ("l0f_bff2", [1, D]),
        ("l0f_wg", [D, NE]),
        ("l0f_we1", [NE, D, FF]), ("l0f_be1", [NE, 128, FF // 128]),
        ("l0f_we2", [NE, FF, D]), ("l0f_be2", [NE, 1, D]),
        ("l0f_wout", [D, D]),
        ("l1a_wq", [D, D]), ("l1a_wk", [D, D]), ("l1a_wv", [D, D]),
        ("l1a_wo", [D, D]), ("l1a_alpha", [1, 1]),
        ("l1f_wgate", [IFF, D]), ("l1f_wup", [IFF, D]), ("l1f_wdown", [D, IFF]),
        ("c_ident", [128, 128]), ("c_rotA", [128, 128]),
        ("c_cos0", [D, S]), ("c_sin0", [D, S]),
        ("c_cos1", [D, S]), ("c_sin1", [D, S]),
        ("c_pet", [D, S]), ("c_hm", [128, 8 * NH]),
        ("c_cmask", [128, 128], U8), ("c_ones", [1, 128]), ("c_onesf", [1, 128]), ("c_ones2", [1, 512]),
    ]:
        nm, shp = item[0], item[1]
        w[nm] = din(nm, shp, item[2] if len(item) > 2 else F32)

    out = nc.dram_tensor("out", [BC * S, D], F32, kind="ExternalOutput").ap()

    sc = {}

    def S_(name, shape, dtype=F32):
        kind = "ExternalOutput" if name in dumps else "Internal"
        sc[name] = nc.dram_tensor(name, shape, dtype, kind=kind).ap()

    S_("xn_tm", [BC * S, D]); S_("xnT", [BC * D, S], BF16)
    S_("l0a_qT", [BC * D, S], BF16); S_("l0a_kT", [BC * D, S], BF16)
    S_("l0a_v", [BC * S, D], BF16)
    S_("s_tm", [BC * S, D])
    S_("l0f_qT", [BC * D, S], BF16); S_("l0f_kT", [BC * D, S], BF16)
    S_("l0f_v", [BC * S, D], BF16); S_("rT", [D, S], BF16)
    S_("bd0", [128 * RS], BF16); S_("bd1", [128 * RS], BF16)
    S_("h1_tm", [BC * S, D]); S_("h1T", [BC * D, S], BF16)
    S_("h2_tm", [BC * S, D]); S_("h2T", [BC * D, S], BF16)
    S_("m_tm", [BC * S, NE]); S_("moe_tm", [BC * S, D])
    S_("x1_tm", [BC * S, D])
    S_("xn1_tm", [BC * S, D]); S_("xn1T", [BC * D, S], BF16)
    S_("l1_qT", [BC * D, S], BF16); S_("l1_kT", [BC * D, S], BF16)
    S_("l1_v", [BC * S, D], BF16)
    S_("l1_qn", [BC * NH, S]); S_("l1_kn", [BC * NH, S])
    S_("l1_qnT", [BC * S, NH])
    S_("wgate_qT", [D, IFF], BF16); S_("wup_qT", [D, IFF], BF16)
    S_("wdown_qT", [IFF, D], BF16)
    S_("hqT", [BC * IFF, S], BF16); S_("xqT", [BC * D, S], BF16)
    S_("ctx0T", [BC * D, S], BF16); S_("ctx1T", [BC * D, S], BF16)

    sch = Sched()

    with ExitStack() as ctx:
        def sb(name, shape, dtype=F32):
            return ctx.enter_context(nc.sbuf_tensor(name, shape, dtype))

        M1 = sb("M1", [128, 16384], BF16)
        M2 = sb("M2", [128, 16384], BF16)
        W = sb("W", [128, 32768], BF16)
        F1 = sb("F1", [128, 2048])
        F2 = sb("F2", [128, 2816])
        F3 = sb("F3", [128, 2816])
        CN = sb("CN", [128, 2048])
        Bq = sb("Bq", [128, 2816], BF16)
        PTb = sb("PTb", [128, 2048], BF16)
        HB = sb("HB", [128, 4096], BF16)
        HB2 = sb("HB2", [128, 2048], BF16)
        IQ = sb("IQ", [128, 1408], I16)
        U8T = sb("U8T", [128, 1408], U8)
        U8b = sb("U8b", [128, 1408], U8)
        LGB = sb("LGB", [128, 2048], BF16)
        SM = sb("SM", [128, 64])
        SM3 = sb("SM3", [128, 64])
        DG = sb("DG", [128, 384], BF16)
        MME = sb("MME", [128, 2 * NE])
        QNS = sb("QNS", [128, 16])
        WGS = sb("WGS", [128, 8 * NE], BF16)
        BR = sb("BR", [1, 1024], BF16)
        ident_bf = sb("ident_bf", [128, 128], BF16)
        ident_f = sb("ident_f", [128, 128], F32)
        rotA = sb("rotA", [128, 128], BF16)
        cmask = sb("cmask", [128, 128], U8)
        negT = sb("negT", [128, 128])
        ones_bf = sb("ones_bf", [1, 128], BF16)
        ones_f = sb("ones_f", [1, 128], F32)
        ones2_bf = sb("ones2_bf", [1, 512], BF16)
        ubt = sb("ubt", [DH, NH])
        vbt = sb("vbt", [DH, NH])
        hm = sb("hm", [128, 8 * NH], BF16)
        bff1_sb = sb("bff1_sb", [128, FF // 128])
        be1_sb = sb("be1_sb", [128, FF // 128])
        alpha_sc = sb("alpha_sc", [128, 1])
        lnsc = sb("lnsc", [1, 1])
        arow = sb("arow", [1, 1])
        lnrow = sb("lnrow", [1, D], BF16)
        zrow = sb("zrow", [128, 1])
        eps1 = sb("eps1", [128, 1])

        PS = [ctx.enter_context(nc.psum_tensor(f"ps{i}", [128, 512], F32))
              for i in range(7)]
        PB = [ctx.enter_context(nc.psum_tensor("pb0", [128, 1024], BF16))
              for i in range(1)]

        sems = {}
        for e in ("pe", "dve", "act", "sp", "g"):
            sems[(e, False)] = ctx.enter_context(nc.semaphore(f"sc_{e}"))
        DMA_SLOTS = {"sp": 36, "g": 36, "act": 12}
        for e, R in DMA_SLOTS.items():
            for i in range(R):
                sems[("D", e, i)] = ctx.enter_context(nc.semaphore(f"sd_{e}{i}"))

        blk = ctx.enter_context(nc.Block())

        def _outer16(ap_):
            for st_, cnt_ in ap_.ap:
                if cnt_ != 1:
                    return cnt_ % 16 == 0
            return False

        def dma(q, out_ap, in_ap, tag=""):
            if not (_outer16(out_ap) and _outer16(in_ap)):
                q = "act"
            elif q == "act":
                q = "g"
            sch.add(q, lambda e, o=out_ap, i=in_ap: e.dma_start(out=o, in_=i),
                    ins=[in_ap], outs=[out_ap], dma=True, tag=tag)

        def dma_acc(out_ap, in_ap, tag=""):
            sch.add("g", lambda e, o=out_ap, i=in_ap:
                    e.dma_start(out=o, in_=i, accum_op=ALU.add),
                    ins=[in_ap, out_ap], outs=[out_ap], dma=True, tag=tag)

        def mm(ps_ap, lhsT, rhs, start, stop, tag=""):
            sch.add("pe", lambda e, o=ps_ap, l=lhsT, r=rhs, s1=start, s2=stop:
                    e.matmul(o, l, r, start=s1, stop=s2),
                    ins=[lhsT, rhs], outs=[ps_ap], tag=tag)

        def trans(ps_ap, in_ap, id_ap, tag=""):
            sch.add("pe", lambda e, o=ps_ap, i=in_ap, d=id_ap: e.transpose(o, i, d),
                    ins=[in_ap, id_ap], outs=[ps_ap], tag=tag)

        def v_ts(out_ap, in_ap, s1, s2, op0, op1=None, accum=None, tag=""):
            ins = [in_ap] + [s for s in (s1, s2) if hasattr(s, "tensor")]
            outs = [out_ap] + ([accum] if accum is not None else [])
            if op1 is None and s2 is None and accum is None:
                sch.add("dve", lambda e, o=out_ap, i=in_ap, a=s1, p0=op0:
                        e.tensor_scalar(o, i, a, None, op0=p0),
                        ins=ins, outs=outs, tag=tag)
            elif accum is None:
                sch.add("dve", lambda e, o=out_ap, i=in_ap, a=s1, b=s2, p0=op0,
                        p1=op1: e.tensor_scalar(o, i, a, b, op0=p0, op1=p1),
                        ins=ins, outs=outs, tag=tag)
            else:
                sch.add("dve", lambda e, o=out_ap, i=in_ap, a=s1, b=s2, p0=op0,
                        p1=op1, ac=accum:
                        e.tensor_scalar(o, i, a, b, op0=p0, op1=p1, accum_out=ac),
                        ins=ins, outs=outs, tag=tag)

        def v_tt(out_ap, a_ap, b_ap, op, tag=""):
            sch.add("dve", lambda e, o=out_ap, a=a_ap, b=b_ap, p=op:
                    e.tensor_tensor(o, a, b, op=p),
                    ins=[a_ap, b_ap], outs=[out_ap], tag=tag)

        def v_stt(out_ap, in0, scal, in1, op0, op1, tag=""):
            ins = [in0, in1] + ([scal] if hasattr(scal, "tensor") else [])
            sch.add("dve", lambda e, o=out_ap, a=in0, s=scal, b=in1, p0=op0, p1=op1:
                    e.scalar_tensor_tensor(o, a, s, b, op0=p0, op1=p1),
                    ins=ins, outs=[out_ap], tag=tag)

        def v_copy(out_ap, in_ap, tag=""):
            sch.add("dve", lambda e, o=out_ap, i=in_ap: e.tensor_copy(o, i),
                    ins=[in_ap], outs=[out_ap], tag=tag)

        def v_red(out_ap, in_ap, op, absv=None, tag=""):
            sch.add("dve", lambda e, o=out_ap, i=in_ap, p=op, av=absv:
                    e.tensor_reduce(o, i, axis=AX.X, op=p, apply_absolute_value=av),
                    ins=[in_ap], outs=[out_ap], tag=tag)

        def v_recip(out_ap, in_ap, tag=""):
            sch.add("dve", lambda e, o=out_ap, i=in_ap: e.reciprocal(o, i),
                    ins=[in_ap], outs=[out_ap], tag=tag)

        def g_ts(out_ap, in_ap, s1, s2, op0, op1=None, tag=""):
            ins = [in_ap] + [x for x in (s1, s2) if hasattr(x, "tensor")]
            if op1 is None and s2 is None:
                sch.add("g", lambda e, o=out_ap, i=in_ap, a=s1, p0=op0:
                        e.tensor_scalar(o, i, a, None, op0=p0),
                        ins=ins, outs=[out_ap], tag=tag)
            else:
                sch.add("g", lambda e, o=out_ap, i=in_ap, a=s1, b=s2, p0=op0,
                        p1=op1: e.tensor_scalar(o, i, a, b, op0=p0, op1=p1),
                        ins=ins, outs=[out_ap], tag=tag)

        def g_tt(out_ap, a_ap, b_ap, op, tag=""):
            sch.add("g", lambda e, o=out_ap, a=a_ap, b=b_ap, p=op:
                    e.tensor_tensor(o, a, b, op=p),
                    ins=[a_ap, b_ap], outs=[out_ap], tag=tag)

        def v_sel(out_ap, mask, t_ap, f_ap, tag=""):
            sch.add("dve", lambda e, o=out_ap, m=mask, a=t_ap, b=f_ap:
                    e.select(o, m, a, b), ins=[mask, t_ap, f_ap], outs=[out_ap],
                    tag=tag)

        def v_memset(ap_, val, tag=""):
            sch.add("dve", lambda e, o=ap_, v=val: e.memset(o, v), ins=[],
                    outs=[ap_], tag=tag)

        def a_copy(out_ap, in_ap, tag=""):
            sch.add("act", lambda e, o=out_ap, i=in_ap: e.activation(
                o, i, AF.Identity), ins=[in_ap], outs=[out_ap], tag=tag)

        def a_act(out_ap, in_ap, func, bias=None, scale=None, accum=None, tag=""):
            ins = [in_ap]
            if bias is not None and hasattr(bias, "tensor"):
                ins.append(bias)
            outs = [out_ap] + ([accum] if accum is not None else [])
            kw = {}
            if bias is not None:
                kw["bias"] = bias
            if scale is not None:
                kw["scale"] = scale

            def f(e, o=out_ap, i=in_ap, fc=func, kw=kw, ac=accum):
                if ac is not None:
                    return e.activation(o, i, fc, accum_out=ac, **kw)
                return e.activation(o, i, fc, **kw)
            sch.add("act", f, ins=ins, outs=outs, tag=tag)

        ALL_OUT = []

        stop_hit = [False]

        def stopped():
            return stop_hit[0]

        def phase_done(name):
            if DEBUG_STOP is not None and name == DEBUG_STOP:
                stop_hit[0] = True

        dma("g", ident_bf[:], w["c_ident"])
        dma("sp", ident_f[:], w["c_ident"])
        dma("g", rotA[:], w["c_rotA"])
        dma("sp", cmask[:], w["c_cmask"])
        dma("g", ones_bf[:], w["c_ones"])
        dma("sp", ones_f[:], w["c_onesf"])
        dma("g", ones2_bf[:], w["c_ones2"])
        dma("sp", ubt[:], w["l0f_ubt"])
        dma("sp", vbt[:], w["l0f_vbt"])
        dma("g", hm[:], w["c_hm"])
        dma("sp", bff1_sb[:], w["l0f_bff1"])
        v_memset(negT[:], NEG)
        v_memset(zrow[:], 0.0)
        v_memset(eps1[:], 1e-5)
        s0 = float(math.sqrt(DH) / math.log(1.0 + DH))
        dma("sp", arow[:], w["l1a_alpha"])
        a_act(lnsc[:], arow[:], AF.Exp, scale=math.log(s0))
        mm(PS[5][:, :1], ones_f[:], lnsc[:], True, True)
        v_copy(alpha_sc[:], PS[5][:, :1])

        def load_lgb(gnm, bnm):
            for j, nm in enumerate((gnm, bnm)):
                dma("g", lnrow[:], w[nm].unsqueeze(0))
                for h in range(2):
                    mm(PS[5][:, :512], ones_bf[:], lnrow[:, h * 512:(h + 1) * 512],
                       True, True)
                    v_copy(LGB[:, j * 1024 + h * 512:j * 1024 + (h + 1) * 512],
                           PS[5][:, :512])

        def ln_tile(src, dst, smbase, affine=False):
            c = smbase
            v_red(SM[:, c:c + 1], src, ALU.add)
            a_act(CN[:, 0:1024], src, AF.Square, accum=SM[:, c + 1:c + 2])
            v_ts(SM[:, c + 2:c + 3], SM[:, c:c + 1], 1.0 / D, None, ALU.mult)
            v_tt(SM[:, c + 3:c + 4], SM[:, c + 2:c + 3], SM[:, c + 2:c + 3],
                 ALU.mult)
            v_ts(SM[:, c + 4:c + 5], SM[:, c + 1:c + 2], 1.0 / D,
                 SM[:, c + 3:c + 4], ALU.mult, ALU.subtract)
            a_act(SM[:, c + 5:c + 6], SM[:, c + 4:c + 5], AF.Ln, bias=eps1[:])
            a_act(SM[:, c + 6:c + 7], SM[:, c + 5:c + 6], AF.Exp, scale=-0.5)
            v_tt(SM[:, c + 7:c + 8], SM[:, c + 2:c + 3], SM[:, c + 6:c + 7],
                 ALU.mult)
            if not affine:
                v_ts(dst, src, SM[:, c + 6:c + 7], SM[:, c + 7:c + 8],
                     ALU.mult, ALU.subtract)
            else:
                v_ts(CN[:, 1024:2048], src, SM[:, c + 6:c + 7],
                     SM[:, c + 7:c + 8], ALU.mult, ALU.subtract)
                v_tt(CN[:, 0:1024], CN[:, 1024:2048], LGB[:, 0:1024], ALU.mult)
                v_tt(dst, CN[:, 0:1024], LGB[:, 1024:2048], ALU.add)

        def t8_store(src_bf, dram_ap, rowbase, col, nch=8):
            for dc in range(nch):
                trans(PB[0][:, :128], src_bf[:, dc * 128:(dc + 1) * 128],
                      ident_bf[:])
                v_copy(PTb[:, dc * 128:(dc + 1) * 128], PB[0][:, :128])
            dst = APx(dram_ap, rowbase * S + col,
                      [[S, 128], [128 * S, nch], [1, 128]])
            srcv = PTb[:, 0:nch * 128].rearrange("p (c n) -> p c n", n=128)
            dma("act", dst, srcv)

        def ph_ln_in(b, src_ap, dst_tm, dst_T):
            for st in range(NST):
                r0 = b * S + st * 128
                p2 = st % 2
                xin = F2[:, p2 * 1024:(p2 + 1) * 1024]
                xo = F3[:, p2 * 1024:(p2 + 1) * 1024]
                dma("sp", xin, src_ap[r0:r0 + 128, :])
                ln_tile(xin, xo, p2 * 16, affine=False)
                dma("g", dst_tm[r0:r0 + 128, :], xo)
                a_act(Bq[:, 0:1024], xo, AF.Identity)
                t8_store(Bq[:, 0:1024], dst_T, b * D, st * 128)

        for b in range(BC):
            ph_ln_in(b, w["x"], sc["xn_tm"], sc["xnT"])
        phase_done("ln0")

        def quant_fp4(src, dst_bf, wd, alt=False):
            nb = wd // 16
            bmraw = CN[:, 1408:1408 + nb]
            bmc = CN[:, 1536:1536 + nb]
            rec = CN[:, 1792:1792 + nb]
            src3 = src.rearrange("p (b k) -> p b k", k=16)

            def bc16(ap2d):
                a = ap2d.ap
                return APx(ap2d, ap2d.offset, [[a[0][0], 128], [a[1][0], nb],
                                               [0, 16]])
            if alt:
                T = F1[:, 0:wd]
                SG = F3[:, 1408:1408 + wd]
            else:
                T = F2[:, 0:wd]
                SG = F1[:, 0:wd]
            SQ = CN[:, 0:wd]
            v_red(bmraw, src3, ALU.max, absv=True)
            v_ts(bmc, bmraw, 1.0 / 6.0, 1e-12, ALU.mult, ALU.max)
            v_recip(rec, bmc)
            t3 = T.rearrange("p (b k) -> p b k", k=16)
            v_tt(t3, src3, bc16(rec), ALU.mult)
            a_act(SQ, T, AF.Square)
            a_act(SG, T, AF.Sign)
            v_ts(IQ[:, 0:wd], T, 2.0, None, ALU.mult)
            v_ts(src, IQ[:, 0:wd], 0.5, None, ALU.mult)
            v_ts(U8T[:, 0:wd], SQ, 5.0625, None, ALU.is_lt)
            v_ts(U8b[:, 0:wd], SQ, 20.25, None, ALU.is_lt)
            v_ts(IQ[:, 0:wd], T, 1.0, None, ALU.mult)
            v_ts(T, SQ, 25.0, None, ALU.is_gt)
            v_stt(SQ, T, 2.0, SG, ALU.mult, ALU.mult)
            v_ts(T, SG, 4.0, None, ALU.mult)
            v_tt(SG, SQ, T, ALU.add)
            v_copy(T, IQ[:, 0:wd])
            v_sel(SQ, U8b[:, 0:wd], T, SG)
            v_sel(T, U8T[:, 0:wd], src, SQ)
            d3 = dst_bf.rearrange("p (b k) -> p b k", k=16)
            v_tt(d3, t3, bc16(bmc), ALU.mult)

        def emit_wq_tile(src_dram, n_rows, kdim, dst_dram, it):
            nh = cdiv(kdim, 1408)
            for hh in range(nh):
                c0 = hh * 1408
                cw = min(1408, kdim - c0)
                dma("sp", F3[:, c0:c0 + cw],
                    src_dram[it * 128:(it + 1) * 128, c0:c0 + cw])
                quant_fp4(F3[:, c0:c0 + cw], Bq[:, c0:c0 + cw], cw,
                          alt=(kdim == D))
            nch = kdim // 128
            for g2 in range(cdiv(nch, 8)):
                n8 = min(8, nch - g2 * 8)
                for dc in range(n8):
                    di = g2 * 8 + dc
                    trans(PB[0][:, :128], Bq[:, di * 128:(di + 1) * 128],
                          ident_bf[:])
                    v_copy(PTb[:, dc * 128:(dc + 1) * 128], PB[0][:, :128])
                dst = APx(dst_dram, (g2 * 8 * 128) * n_rows + it * 128,
                          [[n_rows, 128], [128 * n_rows, n8], [1, 128]])
                dma("act", dst,
                    PTb[:, 0:n8 * 128].rearrange("p (c n) -> p c n", n=128))

        WQ_TASKS = []
        for args_ in ((w["l1f_wgate"], IFF, D, sc["wgate_qT"]),
                      (w["l1f_wup"], IFF, D, sc["wup_qT"]),
                      (w["l1f_wdown"], D, IFF, sc["wdown_qT"])):
            for it in range(args_[1] // 128):
                WQ_TASKS.append(args_ + (it,))
        WQ_TASKS.reverse()
        phase_done("wquant")

        def load_xnT_M1(b, src):
            for dc in range(8):
                dma("sp", M1[:, dc * 2048:(dc + 1) * 2048],
                    src[b * D + dc * 128:b * D + dc * 128 + 128, :])

        def load_w_bf(arena, base, src_dram, rows, cols):
            for dc in range(rows // 128):
                dma("g", arena[:, base + dc * cols:base + (dc + 1) * cols],
                    src_dram[dc * 128:(dc + 1) * 128, :])

        TRIGC = 24576
        TRIGS = 26624

        def proj_rope(b, wbase, cosd, sind, dstd, qnpsum=None):
            for ec in range(8):
                dma("g", W[:, TRIGC:TRIGC + 2048], cosd[ec * 128:(ec + 1) * 128, :])
                dma("g", W[:, TRIGS:TRIGS + 2048], sind[ec * 128:(ec + 1) * 128, :])
                for s4 in range(4):
                    pp = s4 % 2
                    for dc in range(8):
                        mm(PS[pp][:, :512],
                           W[:, wbase + dc * 1024 + ec * 128:
                             wbase + dc * 1024 + ec * 128 + 128],
                           M1[:, dc * 2048 + s4 * 512:dc * 2048 + (s4 + 1) * 512],
                           dc == 0, dc == 7)
                    v_copy(Bq[:, s4 * 512:(s4 + 1) * 512], PS[pp][:, :512])
                    mm(PS[pp][:, :512], rotA[:], Bq[:, s4 * 512:(s4 + 1) * 512],
                       True, True)
                    v_tt(F2[:, s4 * 512:(s4 + 1) * 512], PS[pp][:, :512],
                         W[:, TRIGS + s4 * 512:TRIGS + (s4 + 1) * 512], ALU.mult)
                    v_tt(F3[:, s4 * 512:(s4 + 1) * 512],
                         Bq[:, s4 * 512:(s4 + 1) * 512],
                         W[:, TRIGC + s4 * 512:TRIGC + (s4 + 1) * 512], ALU.mult)
                    v_tt(PTb[:, s4 * 512:(s4 + 1) * 512],
                         F2[:, s4 * 512:(s4 + 1) * 512],
                         F3[:, s4 * 512:(s4 + 1) * 512], ALU.add)
                    if qnpsum is not None:
                        a_act(HB[:, s4 * 512:(s4 + 1) * 512],
                              PTb[:, s4 * 512:(s4 + 1) * 512], AF.Square)
                        mm(PS[2 + s4][0:16, :512], hm[:, ec * 16:(ec + 1) * 16],
                           HB[:, s4 * 512:(s4 + 1) * 512], ec == 0, ec == 7)
                dma("act", dstd[b * D + ec * 128:b * D + ec * 128 + 128, :], PTb[:])

        def proj_v(b, rhs_fn, dstd):
            for st in range(NST):
                p2 = st % 2
                for nh2 in range(2):
                    pp = nh2
                    for dc in range(8):
                        mm(PS[pp][:, :512],
                           M1[:, dc * 2048 + st * 128:dc * 2048 + st * 128 + 128],
                           rhs_fn(dc, nh2), dc == 0, dc == 7)
                    v_copy(HB[:, p2 * 1024 + nh2 * 512:p2 * 1024 + (nh2 + 1) * 512],
                           PS[pp][:, :512])
                dma("act", dstd[b * S + st * 128:b * S + st * 128 + 128, :],
                    HB[:, p2 * 1024:(p2 + 1) * 1024])

        def ph_l0a_proj(b):
            load_xnT_M1(b, sc["xnT"])
            load_w_bf(W, 0, w["l0a_wq_t"], D, D)
            load_w_bf(W, 8192, w["l0a_wk_t"], D, D)
            load_w_bf(W, 16384, w["l0a_wv_t"], D, D)
            proj_rope(b, 0, w["c_cos0"], w["c_sin0"], sc["l0a_qT"])
            proj_rope(b, 8192, w["c_cos0"], w["c_sin0"], sc["l0a_kT"])
            proj_v(b, lambda dc, nh2: W[:, 16384 + dc * 1024 + nh2 * 512:
                                        16384 + dc * 1024 + nh2 * 512 + 512],
                   sc["l0a_v"])

        def ph_l0a_attn(b):
            for dc in range(8):
                dma("sp", M1[:, dc * 2048:(dc + 1) * 2048],
                    sc["l0a_qT"][b * D + dc * 128:b * D + dc * 128 + 128, :])
                dma("sp", M2[:, dc * 2048:(dc + 1) * 2048],
                    sc["l0a_kT"][b * D + dc * 128:b * D + dc * 128 + 128, :])
            for st in range(NST):
                dma("sp", W[:, st * 1024:(st + 1) * 1024],
                    sc["l0a_v"][b * S + st * 128:b * S + st * 128 + 128, :])
            load_w_bf(W, 16384, w["l0a_wo_t"], D, D)

            def stage1(qt, par):
                sm = 32 * par
                for kc4 in range(4):
                    pp = kc4 % 2
                    for dc in range(8):
                        mm(PS[pp][:, :512],
                           M1[:, dc * 2048 + qt * 128:dc * 2048 + qt * 128 + 128],
                           M2[:, dc * 2048 + kc4 * 512:dc * 2048 + (kc4 + 1) * 512],
                           dc == 0, dc == 7)
                    sbuf_sc = F1 if par == 0 else F2
                    v_ts(sbuf_sc[:, kc4 * 512:(kc4 + 1) * 512],
                         PS[pp][:, :512], 1.0, -3.0e38, ALU.mult, ALU.max,
                         accum=SM[:, sm + kc4:sm + kc4 + 1])
                v_red(SM[:, sm + 4:sm + 5], SM[:, sm:sm + 4], ALU.max)
                v_ts(SM[:, sm + 5:sm + 6], SM[:, sm + 4:sm + 5], -1.0 / 32.0,
                     None, ALU.mult)

            def stage2(qt, par):
                sm = 32 * par
                prb0 = 24576 + par * 2048
                ptb0 = 28672 + par * 2048
                sbuf_sc = F1 if par == 0 else F2
                a_act(APx(W[:], prb0, [[32768, 128], [1, 2048]]),
                      sbuf_sc[:, 0:2048], AF.Exp,
                      bias=SM[:, sm + 5:sm + 6], scale=1.0 / 32.0,
                      accum=SM[:, sm + 6:sm + 7])
                v_recip(SM[:, sm + 7:sm + 8], SM[:, sm + 6:sm + 7])
                v_ts(DG[:, par * 128:(par + 1) * 128], ident_bf[:],
                     SM[:, sm + 7:sm + 8], None, ALU.mult)
                for kc in range(16):
                    pp = 4 + (kc // 4) % 2
                    mm(PS[pp][:, (kc % 4) * 128:(kc % 4) * 128 + 128],
                       APx(W[:], prb0 + kc * 128, [[32768, 128], [1, 128]]),
                       DG[:, par * 128:(par + 1) * 128], True, True)
                    if kc % 4 == 3:
                        a_copy(APx(W[:], ptb0 + (kc - 3) * 128,
                                   [[32768, 128], [1, 512]]), PS[pp][:, :512])
                for dc in range(8):
                    pp = 4 + (dc // 4) % 2
                    for kc in range(16):
                        mm(PS[pp][:, (dc % 4) * 128:(dc % 4) * 128 + 128],
                           W[:, kc * 1024 + dc * 128:kc * 1024 + dc * 128 + 128],
                           APx(W[:], ptb0 + kc * 128, [[32768, 128], [1, 128]]),
                           kc == 0, kc == 15)
                    if dc % 4 == 3:
                        v_copy(HB[:, par * 2048 + (dc - 3) * 128:
                                  par * 2048 + (dc + 1) * 128], PS[pp][:, :512])
                for nh2 in range(2):
                    pp = 2 + nh2
                    for dc in range(8):
                        mm(PS[pp][:, :512],
                           HB[:, par * 2048 + dc * 128:par * 2048 + (dc + 1) * 128],
                           W[:, 16384 + dc * 1024 + nh2 * 512:
                             16384 + dc * 1024 + nh2 * 512 + 512],
                           dc == 0, dc == 7)
                    v_copy(CN[:, par * 1024 + nh2 * 512:par * 1024 + (nh2 + 1) * 512],
                           PS[pp][:, :512])
                dma("act", sc["s_tm"][b * S + qt * 128:b * S + qt * 128 + 128, :],
                    CN[:, par * 1024:par * 1024 + 1024])

            for qt in range(NST):
                stage1(qt, qt % 2)
                if qt >= 1:
                    stage2(qt - 1, (qt - 1) % 2)
            stage2(NST - 1, (NST - 1) % 2)

        if not stopped():
            for b in range(BC):
                ph_l0a_proj(b)
        phase_done("l0a_proj")
        if not stopped():
            for b in range(BC):
                ph_l0a_attn(b)
        phase_done("l0a")

        def ph_rT():
            for dc in range(8):
                dma("g", M2[:, dc * 2048:(dc + 1) * 2048],
                    w["c_pet"][dc * 128:(dc + 1) * 128, :])
                dma("g", W[:, 24576 + dc * 1024:24576 + (dc + 1) * 1024],
                    w["l0f_wr"][dc * 128:(dc + 1) * 128, :])
            for ec in range(8):
                for s4 in range(4):
                    pp = s4 % 2
                    for dc in range(8):
                        mm(PS[pp][:, :512],
                           W[:, 24576 + dc * 1024 + ec * 128:
                             24576 + dc * 1024 + ec * 128 + 128],
                           M2[:, dc * 2048 + s4 * 512:dc * 2048 + (s4 + 1) * 512],
                           dc == 0, dc == 7)
                    v_copy(PTb[:, s4 * 512:(s4 + 1) * 512], PS[pp][:, :512])
                dma("act", sc["rT"][ec * 128:(ec + 1) * 128, :], PTb[:])

        def ph_l0f_qkv(b):
            load_xnT_M1(b, sc["xnT"])
            for dc in range(8):
                dma("g", W[:, dc * 3072:(dc + 1) * 3072],
                    w["l0f_wqkv"][dc * 128:(dc + 1) * 128, :])
            for cbase, dstd in ((0, sc["l0f_qT"]), (D, sc["l0f_kT"])):
                for ec in range(8):
                    for s4 in range(4):
                        pp = s4 % 2
                        for dc in range(8):
                            mm(PS[pp][:, :512],
                               W[:, dc * 3072 + cbase + ec * 128:
                                 dc * 3072 + cbase + ec * 128 + 128],
                               M1[:, dc * 2048 + s4 * 512:
                                  dc * 2048 + (s4 + 1) * 512],
                               dc == 0, dc == 7)
                        v_copy(PTb[:, s4 * 512:(s4 + 1) * 512], PS[pp][:, :512])
                    dma("act", dstd[b * D + ec * 128:b * D + ec * 128 + 128, :],
                        PTb[:])
            proj_v(b, lambda dc, nh2: W[:, dc * 3072 + 2 * D + nh2 * 512:
                                        dc * 3072 + 2 * D + nh2 * 512 + 512],
                   sc["l0f_v"])

        if not stopped():
            ph_rT()
            for b in range(BC):
                ph_l0f_qkv(b)
        phase_done("l0f_qkv")

        def ph_l0f_attn(b):
            for h in range(NH):
                dc = h // 2
                rb = (h % 2) * 64
                qu = APx(M2[:], 0, [[16384, 64], [1, 2048]])
                qv = APx(M2[:], 2048, [[16384, 64], [1, 2048]])
                kh = APx(M2[:], 4096, [[16384, 64], [1, 2048]])
                rh = APx(M2[:], 6144, [[16384, 64], [1, 2048]])
                vh = M2[:, 8192:9216]
                qu2 = APx(M2[:], 13312, [[16384, 64], [1, 2048]])
                dma("sp", qu, sc["l0f_qT"][b * D + h * 64:b * D + h * 64 + 64, :])
                dma("sp", kh, sc["l0f_kT"][b * D + h * 64:b * D + h * 64 + 64, :])
                dma("sp", rh, sc["rT"][h * 64:h * 64 + 64, :])
                vsrc = APx(sc["l0f_v"], (b * S) * D + h * 64,
                           [[D, 128], [128 * D, 16], [1, 64]])
                dma("sp", vh.rearrange("p (c n) -> p c n", n=64), vsrc)
                v_ts(qv, qu, vbt[:, h:h + 1], None, ALU.add)
                v_ts(qu2, qu, ubt[:, h:h + 1], None, ALU.add)

                def stage1a(qt, par):
                    kw = (qt + 1) * 128
                    nk4 = cdiv(kw, 512)
                    c0 = 2048 - kw
                    bdbuf = sc["bd0"] if (h * NST + qt) % 2 == 0 else sc["bd1"]
                    bst = 18432 + par * 2048
                    bsh = 12288 + par * 2048
                    for i4 in range(nk4):
                        cw = min(512, kw - i4 * 512)
                        cs = c0 + i4 * 512
                        pp = 2 + i4 % 2
                        mm(PS[pp][:, :cw],
                           APx(M2[:], 2048 + qt * 128, [[16384, 64], [1, 128]]),
                           APx(M2[:], 6144 + cs, [[16384, 64], [1, cw]]),
                           True, True)
                        a_copy(APx(W[:], bst + cs - c0, [[32768, 128], [1, cw]]),
                               PS[pp][:, :cw])
                    dma("act", APx(bdbuf, c0, [[RS, 128], [1, kw]]),
                        APx(W[:], bst, [[32768, 128], [1, kw]]))
                    dma("sp", APx(W[:], bsh, [[32768, 128], [1, kw]]),
                        APx(bdbuf, 2047 - qt * 128, [[RS - 1, 128], [1, kw]]))

                def stage1b(qt, par):
                    sm = 16 * par
                    sbuf_sc = (F1, F2, CN)[par]
                    kw = (qt + 1) * 128
                    nk4 = cdiv(kw, 512)
                    bsh = 12288 + par * 2048
                    for i4 in range(nk4):
                        cw = min(512, kw - i4 * 512)
                        pp = i4 % 2
                        mm(PS[pp][:, :cw],
                           APx(M2[:], 13312 + qt * 128, [[16384, 64], [1, 128]]),
                           APx(M2[:], 4096 + i4 * 512, [[16384, 64], [1, cw]]),
                           True, True)
                        v_tt(sbuf_sc[:, i4 * 512:i4 * 512 + cw], PS[pp][:, :cw],
                             APx(W[:], bsh + i4 * 512, [[32768, 128], [1, cw]]),
                             ALU.add)
                    v_copy(F2[:, 2560:2688], sbuf_sc[:, kw - 128:kw])
                    v_sel(sbuf_sc[:, kw - 128:kw], cmask[:], F2[:, 2560:2688],
                          negT[:])
                    v_red(SM[:, sm + 0:sm + 1], sbuf_sc[:, 0:kw], ALU.max)
                    v_ts(SM[:, sm + 1:sm + 2], SM[:, sm + 0:sm + 1], -0.125,
                         None, ALU.mult)

                def stage2(qt, par):
                    sm = 16 * par
                    sbuf_sc = (F1, F2, CN)[par]
                    kw = (qt + 1) * 128
                    prb0 = par * 2048
                    ptb0 = 6144 + par * 2048
                    a_act(APx(W[:], prb0, [[32768, 128], [1, kw]]),
                          sbuf_sc[:, 0:kw], AF.Exp, bias=SM[:, sm + 1:sm + 2],
                          scale=0.125, accum=SM[:, sm + 2:sm + 3])
                    v_recip(SM[:, sm + 3:sm + 4], SM[:, sm + 2:sm + 3])
                    v_ts(DG[:, par * 128:(par + 1) * 128], ident_bf[:],
                         SM[:, sm + 3:sm + 4], None, ALU.mult)
                    nkc = qt + 1
                    for kc in range(nkc):
                        pp = 4 + (kc // 4) % 2
                        mm(PS[pp][:, (kc % 4) * 128:(kc % 4) * 128 + 128],
                           APx(W[:], prb0 + kc * 128, [[32768, 128], [1, 128]]),
                           DG[:, par * 128:(par + 1) * 128], True, True)
                        if kc % 4 == 3 or kc == nkc - 1:
                            g0 = (kc // 4) * 4
                            gn = kc - g0 + 1
                            a_copy(APx(W[:], ptb0 + g0 * 128,
                                       [[32768, 128], [1, gn * 128]]),
                                   PS[pp][:, :gn * 128])
                    for kc in range(nkc):
                        mm(PS[6][0:64, :128], vh[:, kc * 64:(kc + 1) * 64],
                           APx(W[:], ptb0 + kc * 128, [[32768, 128], [1, 128]]),
                           kc == 0, kc == qt)
                    v_copy(M1[rb:rb + 64,
                              dc * 2048 + qt * 128:dc * 2048 + qt * 128 + 128],
                           PS[6][0:64, :128])

                stage1a(0, 0)
                stage1a(1, 1)
                for qt in range(NST):
                    if qt + 2 < NST:
                        stage1a(qt + 2, (qt + 2) % 3)
                    stage1b(qt, qt % 3)
                    if qt >= 1:
                        stage2(qt - 1, (qt - 1) % 3)
                stage2(NST - 1, (NST - 1) % 3)

        def ph_l0f_h1(b):
            if "ctx0T" in dumps:
                for dc in range(8):
                    dma("g", sc["ctx0T"][b * D + dc * 128:b * D + dc * 128 + 128, :],
                        M1[:, dc * 2048:(dc + 1) * 2048])
            load_w_bf(W, 0, w["l0f_wo"], D, D)
            load_lgb("l0f_ln1g", "l0f_ln1b")
            for st in range(NST):
                r0 = b * S + st * 128
                for nh2 in range(2):
                    pp = nh2
                    for dc in range(8):
                        mm(PS[pp][:, :512],
                           M1[:, dc * 2048 + st * 128:dc * 2048 + st * 128 + 128],
                           W[:, dc * 1024 + nh2 * 512:dc * 1024 + nh2 * 512 + 512],
                           dc == 0, dc == 7)
                dma("sp", F2[:, 0:1024], sc["xn_tm"][r0:r0 + 128, :])
                for nh2 in range(2):
                    v_tt(F3[:, nh2 * 512:(nh2 + 1) * 512], PS[nh2][:, :512],
                         F2[:, nh2 * 512:(nh2 + 1) * 512], ALU.add)
                ln_tile(F3[:, 0:1024], F3[:, 1024:2048], 0, affine=True)
                dma("g", sc["h1_tm"][r0:r0 + 128, :], F3[:, 1024:2048])
                a_act(Bq[:, 0:1024], F3[:, 1024:2048], AF.Identity)
                t8_store(Bq[:, 0:1024], sc["h1T"], b * D, st * 128)

        def ph_l0f_ffn(b):
            for dc in range(8):
                dma("g", W[:, dc * 4096:(dc + 1) * 4096],
                    w["l0f_wff1"][dc * 128:(dc + 1) * 128, :])
            for fc in range(32):
                arena = M1 if fc < 16 else M2
                base = (fc % 16) * 1024
                dma("g", arena[:, base:base + 1024],
                    w["l0f_wff2"][fc * 128:(fc + 1) * 128, :])
            dma("g", BR[:], w["l0f_bff2"])
            load_lgb("l0f_ln2g", "l0f_ln2b")
            for st in range(NST):
                r0 = b * S + st * 128
                hsrc = APx(sc["h1T"], (b * D) * S + st * 128,
                           [[S, 128], [128 * S, 8], [1, 128]])
                dma("sp", APx(HB2[:], 0, [[2048, 128], [128, 8], [1, 128]]), hsrc)
                for fc in range(32):
                    pp = fc % 2
                    for dc in range(8):
                        mm(PS[pp][:, :128],
                           W[:, dc * 4096 + fc * 128:dc * 4096 + fc * 128 + 128],
                           HB2[:, dc * 128:(dc + 1) * 128], dc == 0, dc == 7)
                    a_act(HB[:, fc * 128:(fc + 1) * 128], PS[pp][:, :128],
                          AF.Relu, bias=bff1_sb[:, fc:fc + 1])
                for nh2 in range(2):
                    pp = 2 + nh2
                    mm(PS[pp][:, :512], ones_bf[:],
                       BR[:, nh2 * 512:(nh2 + 1) * 512], True, False)
                    for fc in range(32):
                        arena = M1 if fc < 16 else M2
                        base = (fc % 16) * 1024
                        mm(PS[pp][:, :512], HB[:, fc * 128:(fc + 1) * 128],
                           arena[:, base + nh2 * 512:base + nh2 * 512 + 512],
                           False, fc == 31)
                dma("sp", F2[:, 0:1024], sc["h1_tm"][r0:r0 + 128, :])
                for nh2 in range(2):
                    v_tt(F3[:, nh2 * 512:(nh2 + 1) * 512], PS[2 + nh2][:, :512],
                         F2[:, nh2 * 512:(nh2 + 1) * 512], ALU.add)
                ln_tile(F3[:, 0:1024], F3[:, 1024:2048], 0, affine=True)
                dma("g", sc["h2_tm"][r0:r0 + 128, :], F3[:, 1024:2048])
                a_act(Bq[:, 0:1024], F3[:, 1024:2048], AF.Identity)
                t8_store(Bq[:, 0:1024], sc["h2T"], b * D, st * 128)

        def ph_l0f_gate(b):
            for dc in range(8):
                dma("g", WGS[:, dc * 8:(dc + 1) * 8],
                    w["l0f_wg"][dc * 128:(dc + 1) * 128, :])
            for st in range(NST):
                r0 = b * S + st * 128
                hsrc = APx(sc["h2T"], (b * D) * S + st * 128,
                           [[S, 128], [128 * S, 8], [1, 128]])
                dma("sp", APx(HB2[:], 0, [[2048, 128], [128, 8], [1, 128]]), hsrc)
                for dc in range(8):
                    mm(PS[0][:, :8], HB2[:, dc * 128:(dc + 1) * 128],
                       WGS[:, dc * 8:(dc + 1) * 8], dc == 0, dc == 7)
                v_copy(SM3[:, 0:8], PS[0][:, :8])
                v_red(SM3[:, 8:9], SM3[:, 0:8], ALU.max)
                v_ts(SM3[:, 9:10], SM3[:, 8:9], -1.0, None, ALU.mult)
                a_act(SM3[:, 16:24], SM3[:, 0:8], AF.Exp, bias=SM3[:, 9:10],
                      accum=SM3[:, 10:11])
                v_recip(SM3[:, 11:12], SM3[:, 10:11])
                v_ts(SM3[:, 24:32], SM3[:, 16:24], SM3[:, 11:12], None, ALU.mult)
                v_red(SM3[:, 12:13], SM3[:, 24:32], ALU.max)
                v_ts(SM3[:, 32:40], SM3[:, 24:32], SM3[:, 12:13], None, ALU.is_ge)
                v_copy(SM3[:, 13:14], zrow[:])
                for ee in range(NE):
                    ra = 13 + (ee % 2)
                    rb2 = 13 + ((ee + 1) % 2)
                    v_ts(SM3[:, 15:16], SM3[:, ra:ra + 1], -1.0, 1.0, ALU.mult,
                         ALU.add)
                    v_tt(SM3[:, 40 + ee:41 + ee], SM3[:, 32 + ee:33 + ee],
                         SM3[:, 15:16], ALU.mult)
                    v_tt(SM3[:, rb2:rb2 + 1], SM3[:, ra:ra + 1],
                         SM3[:, 40 + ee:41 + ee], ALU.add)
                v_ts(MME[:, 0:8], SM3[:, 40:48], SM3[:, 12:13], None, ALU.mult)
                dma("g", sc["m_tm"][r0:r0 + 128, :], MME[:, 0:8])

        if not stopped():
            for b in range(BC):
                ph_l0f_attn(b)
                ph_l0f_h1(b)
        phase_done("l0f_h1")
        if not stopped():
            for b in range(BC):
                ph_l0f_ffn(b)
                ph_l0f_gate(b)
        phase_done("l0f_h2")

        def ph_moe():
            for e_ in range(NE):
                for dc in range(8):
                    dma("g", W[:, dc * 4096:(dc + 1) * 4096],
                        w["l0f_we1"][e_, dc * 128:(dc + 1) * 128, :])
                for fc in range(32):
                    arena = M1 if fc < 16 else M2
                    base = (fc % 16) * 1024
                    dma("g", arena[:, base:base + 1024],
                        w["l0f_we2"][e_, fc * 128:(fc + 1) * 128, :])
                dma("sp", be1_sb[:], w["l0f_be1"][e_])
                dma("g", BR[:], w["l0f_be2"][e_])
                for b in range(BC):
                    for st in range(NST):
                        r0 = b * S + st * 128
                        p2 = st % 2
                        hb2 = HB2[:, p2 * 1024:(p2 + 1) * 1024]
                        mme = MME[:, p2 * 8:(p2 + 1) * 8]
                        hsrc = APx(sc["h2T"], (b * D) * S + st * 128,
                                   [[S, 128], [128 * S, 8], [1, 128]])
                        dma("sp", APx(HB2[:], p2 * 1024,
                                      [[2048, 128], [128, 8], [1, 128]]), hsrc)
                        dma("sp", mme, sc["m_tm"][r0:r0 + 128, :])
                        for fc in range(32):
                            pp = fc % 2
                            for dc in range(8):
                                mm(PS[pp][:, :128],
                                   W[:, dc * 4096 + fc * 128:
                                     dc * 4096 + fc * 128 + 128],
                                   hb2[:, dc * 128:(dc + 1) * 128],
                                   dc == 0, dc == 7)
                            a_act(HB[:, fc * 128:(fc + 1) * 128], PS[pp][:, :128],
                                  AF.Relu, bias=be1_sb[:, fc:fc + 1])
                        for nh2 in range(2):
                            mm(PS[2 + nh2][:, :512], ones_bf[:],
                               BR[:, nh2 * 512:(nh2 + 1) * 512], True, False)
                        for fc in range(32):
                            arena = M1 if fc < 16 else M2
                            base = (fc % 16) * 1024
                            for nh2 in range(2):
                                mm(PS[2 + nh2][:, :512],
                                   HB[:, fc * 128:(fc + 1) * 128],
                                   arena[:, base + nh2 * 512:
                                         base + nh2 * 512 + 512],
                                   False, fc == 31)
                        for nh2 in range(2):
                            v_ts(F2[:, p2 * 1024 + nh2 * 512:
                                    p2 * 1024 + (nh2 + 1) * 512],
                                 PS[2 + nh2][:, :512], mme[:, e_:e_ + 1], None,
                                 ALU.mult)
                        if e_ == 0:
                            dma("g", sc["moe_tm"][r0:r0 + 128, :],
                                F2[:, p2 * 1024:(p2 + 1) * 1024])
                        else:
                            dma_acc(sc["moe_tm"][r0:r0 + 128, :],
                                    F2[:, p2 * 1024:(p2 + 1) * 1024])
                        if st % 4 == 3 and WQ_TASKS:
                            emit_wq_tile(*WQ_TASKS.pop())

        if not stopped():
            ph_moe()
            while WQ_TASKS:
                emit_wq_tile(*WQ_TASKS.pop())
        phase_done("moe")

        def ph_l0_final(b):
            load_w_bf(W, 0, w["l0f_wout"], D, D)
            load_lgb("l0f_ln3g", "l0f_ln3b")
            for st in range(NST):
                r0 = b * S + st * 128
                dma("sp", F2[:, 0:1024], sc["h2_tm"][r0:r0 + 128, :])
                dma("sp", F1[:, 0:1024], sc["moe_tm"][r0:r0 + 128, :])
                v_tt(F3[:, 0:1024], F2[:, 0:1024], F1[:, 0:1024], ALU.add)
                ln_tile(F3[:, 0:1024], F3[:, 1024:2048], 0, affine=True)
                a_act(Bq[:, 0:1024], F3[:, 1024:2048], AF.Identity)
                for dc in range(8):
                    trans(PB[0][:, :128], Bq[:, dc * 128:(dc + 1) * 128],
                          ident_bf[:])
                    v_copy(HB[:, dc * 128:(dc + 1) * 128], PB[0][:, :128])
                for nh2 in range(2):
                    pp = nh2
                    for dc in range(8):
                        mm(PS[pp][:, :512], HB[:, dc * 128:(dc + 1) * 128],
                           W[:, dc * 1024 + nh2 * 512:dc * 1024 + nh2 * 512 + 512],
                           dc == 0, dc == 7)
                    a_act(F2[:, 1024 + nh2 * 512:1024 + (nh2 + 1) * 512],
                          PS[pp][:, :512], AF.Silu)
                dma("sp", F1[:, 1024:2048], sc["s_tm"][r0:r0 + 128, :])
                dma("sp", CN[:, 0:1024], w["x"][r0:r0 + 128, :])
                v_tt(F2[:, 0:1024], F1[:, 1024:2048], F2[:, 1024:2048], ALU.mult)
                v_tt(F3[:, 0:1024], CN[:, 0:1024], F2[:, 0:1024], ALU.add)
                dma("g", sc["x1_tm"][r0:r0 + 128, :], F3[:, 0:1024])

        if not stopped():
            for b in range(BC):
                ph_l0_final(b)
        phase_done("l0")

        if not stopped():
            for b in range(BC):
                ph_ln_in(b, sc["x1_tm"], sc["xn1_tm"], sc["xn1T"])
        phase_done("ln1")

        def ph_l1_proj(b):
            load_xnT_M1(b, sc["xn1T"])
            load_w_bf(W, 0, w["l1a_wq"], D, D)
            load_w_bf(W, 8192, w["l1a_wk"], D, D)
            load_w_bf(W, 16384, w["l1a_wv"], D, D)
            for wbase, dstd, dstn in ((0, sc["l1_qT"], sc["l1_qn"]),
                                      (8192, sc["l1_kT"], sc["l1_kn"])):
                proj_rope(b, wbase, w["c_cos1"], w["c_sin1"], dstd, qnpsum=True)
                for s4 in range(4):
                    v_copy(F1[0:16, s4 * 512:(s4 + 1) * 512],
                           PS[2 + s4][0:16, :512])
                dma("g", dstn[b * NH:(b + 1) * NH, :], F1[0:16, 0:2048])
                if wbase == 0:
                    for qt in range(NST):
                        pp = qt % 2
                        trans(PS[pp][:, :16], F1[0:16, qt * 128:(qt + 1) * 128],
                              ident_f[0:16, 0:16])
                        v_copy(QNS[:], PS[pp][:, :16])
                        dma("g", sc["l1_qnT"][b * S + qt * 128:
                                              b * S + qt * 128 + 128, :], QNS[:])
            proj_v(b, lambda dc, nh2: W[:, 16384 + dc * 1024 + nh2 * 512:
                                        16384 + dc * 1024 + nh2 * 512 + 512],
                   sc["l1_v"])

        def ph_l1_attn(b):
            for h in range(DEBUG_L1_HEADS):
                dc = h // 2
                rb = (h % 2) * 64
                qh = APx(M2[:], 0, [[16384, 64], [1, 2048]])
                kh = APx(M2[:], 2048, [[16384, 64], [1, 2048]])
                vh = M2[:, 4096:5120]
                q2h = APx(M2[:], 5120, [[16384, 64], [1, 2048]])
                knr = APx(M2[:], 7168, [[16384, 1], [1, 2048]])
                qnr = APx(M2[:], 9216, [[16384, 1], [1, 2048]])
                dma("sp", qh, sc["l1_qT"][b * D + h * 64:b * D + h * 64 + 64, :])
                dma("sp", kh, sc["l1_kT"][b * D + h * 64:b * D + h * 64 + 64, :])
                vsrc = APx(sc["l1_v"], (b * S) * D + h * 64,
                           [[D, 128], [128 * D, 16], [1, 64]])
                dma("sp", vh.rearrange("p (c n) -> p c n", n=64), vsrc)
                v_ts(q2h, qh, -2.0, None, ALU.mult)
                dma("sp", F1[0:1, 0:2048],
                    sc["l1_kn"][b * NH + h:b * NH + h + 1, :])
                dma("sp", CN[0:1, 0:2048],
                    sc["l1_qn"][b * NH + h:b * NH + h + 1, :])
                v_ts(knr, F1[0:1, 0:2048], 1e-5, None, ALU.add)
                v_copy(qnr, CN[0:1, 0:2048])

                def stage1(qt, par):
                    sm = 32 * par
                    sbuf_sc = F1 if par == 0 else F3
                    kw = (qt + 1) * 128
                    nk4 = cdiv(kw, 512)
                    for i4 in range(nk4):
                        cw = min(512, kw - i4 * 512)
                        pp = i4 % 2
                        mm(PS[pp][:, :cw],
                           APx(M2[:], 0 + qt * 128, [[16384, 64], [1, 128]]),
                           APx(M2[:], 2048 + i4 * 512, [[16384, 64], [1, cw]]),
                           True, True)
                        mm(PS[2 + pp][:, :cw],
                           APx(M2[:], 5120 + qt * 128, [[16384, 64], [1, 128]]),
                           APx(M2[:], 2048 + i4 * 512, [[16384, 64], [1, cw]]),
                           True, False)
                        mm(PS[2 + pp][:, :cw], ones_bf[:],
                           APx(M2[:], 7168 + i4 * 512, [[16384, 1], [1, cw]]),
                           False, False)
                        mm(PS[2 + pp][:, :cw],
                           APx(M2[:], 9216 + qt * 128, [[16384, 1], [1, 128]]),
                           ones2_bf[:, 0:cw], False, True)
                        a_act(F2[:, i4 * 512:i4 * 512 + cw], PS[pp][:, :cw],
                              AF.Square)
                        if i4 % 3 == 2:
                            v_copy(F3[:, 2304:2304 + cw], PS[2 + pp][:, :cw])
                            v_recip(CN[:, i4 * 512:i4 * 512 + cw],
                                    F3[:, 2304:2304 + cw])
                        else:
                            a_act(F3[:, 2304:2304 + cw], PS[2 + pp][:, :cw],
                                  AF.Ln)
                            a_act(CN[:, i4 * 512:i4 * 512 + cw],
                                  F3[:, 2304:2304 + cw], AF.Exp, scale=-1.0)
                        v_stt(sbuf_sc[:, i4 * 512:i4 * 512 + cw],
                              F2[:, i4 * 512:i4 * 512 + cw], alpha_sc[:],
                              CN[:, i4 * 512:i4 * 512 + cw], ALU.mult,
                              ALU.mult)
                    v_copy(F2[:, 2560:2688], sbuf_sc[:, kw - 128:kw])
                    v_sel(sbuf_sc[:, kw - 128:kw], cmask[:], F2[:, 2560:2688],
                          negT[:])
                    v_red(SM[:, sm + 0:sm + 1], sbuf_sc[:, 0:kw], ALU.max)
                    v_ts(SM[:, sm + 1:sm + 2], SM[:, sm + 0:sm + 1], -1.0,
                         None, ALU.mult)

                def stage2(qt, par):
                    sm = 32 * par
                    sbuf_sc = F1 if par == 0 else F3
                    kw = (qt + 1) * 128
                    prb0 = par * 2048
                    ptb0 = 4096 + par * 2048
                    a_act(APx(W[:], prb0, [[32768, 128], [1, kw]]),
                          sbuf_sc[:, 0:kw], AF.Exp, bias=SM[:, sm + 1:sm + 2],
                          accum=SM[:, sm + 2:sm + 3])
                    v_recip(SM[:, sm + 3:sm + 4], SM[:, sm + 2:sm + 3])
                    v_ts(DG[:, par * 128:(par + 1) * 128], ident_bf[:],
                         SM[:, sm + 3:sm + 4], None, ALU.mult)
                    nkc = qt + 1
                    for kc in range(nkc):
                        pp = 4 + (kc // 4) % 2
                        mm(PS[pp][:, (kc % 4) * 128:(kc % 4) * 128 + 128],
                           APx(W[:], prb0 + kc * 128, [[32768, 128], [1, 128]]),
                           DG[:, par * 128:(par + 1) * 128], True, True)
                        if kc % 4 == 3 or kc == nkc - 1:
                            g0 = (kc // 4) * 4
                            gn = kc - g0 + 1
                            dstp = APx(W[:], ptb0 + g0 * 128,
                                       [[32768, 128], [1, gn * 128]])
                            if (kc // 4) % 2 == 0:
                                v_copy(dstp, PS[pp][:, :gn * 128])
                            else:
                                a_copy(dstp, PS[pp][:, :gn * 128])
                    for kc in range(qt + 1):
                        mm(PS[6][0:64, :128], vh[:, kc * 64:(kc + 1) * 64],
                           APx(W[:], ptb0 + kc * 128, [[32768, 128], [1, 128]]),
                           kc == 0, kc == qt)
                    a_copy(M1[rb:rb + 64,
                              dc * 2048 + qt * 128:dc * 2048 + qt * 128 + 128],
                           PS[6][0:64, :128])

                for qt in range(NST):
                    stage1(qt, qt % 2)
                    if qt >= 1:
                        stage2(qt - 1, (qt - 1) % 2)
                stage2(NST - 1, (NST - 1) % 2)

        def ph_l1_out(b):
            if "ctx1T" in dumps:
                for dc in range(8):
                    dma("g", sc["ctx1T"][b * D + dc * 128:b * D + dc * 128 + 128, :],
                        M1[:, dc * 2048:(dc + 1) * 2048])
            load_w_bf(W, 0, w["l1a_wo"], D, D)
            for st in range(NST):
                r0 = b * S + st * 128
                for nh2 in range(2):
                    pp = nh2
                    for dc in range(8):
                        mm(PS[pp][:, :512],
                           M1[:, dc * 2048 + st * 128:dc * 2048 + st * 128 + 128],
                           W[:, dc * 1024 + nh2 * 512:dc * 1024 + nh2 * 512 + 512],
                           dc == 0, dc == 7)
                    v_copy(CN[:, nh2 * 512:(nh2 + 1) * 512], PS[pp][:, :512])
                dma("act", sc["s_tm"][r0:r0 + 128, :], CN[:, 0:1024])

        if not stopped():
            for b in range(BC):
                ph_l1_proj(b)
        phase_done("l1_proj")
        if not stopped():
            for b in range(BC):
                ph_l1_attn(b)
                ph_l1_out(b)
        phase_done("l1a")

        def ph_l1f_x(b):
            for st in range(NST):
                r0 = b * S + st * 128
                dma("sp", F3[:, 1408:2432], sc["xn1_tm"][r0:r0 + 128, :])
                quant_fp4(F3[:, 1408:2432], Bq[:, 0:1024], 1024)
                t8_store(Bq[:, 0:1024], sc["xqT"], b * D, st * 128)

        def ph_l1f_h(b):
            for dcc in range(8):
                dma("sp", W[:, dcc * 2816:(dcc + 1) * 2816],
                    sc["wgate_qT"][dcc * 128:(dcc + 1) * 128, :])
                if dcc < 5:
                    dma("sp", M1[:, dcc * 2816:(dcc + 1) * 2816],
                        sc["wup_qT"][dcc * 128:(dcc + 1) * 128, :])
                else:
                    dma("sp", M2[:, (dcc - 5) * 2816:(dcc - 4) * 2816],
                        sc["wup_qT"][dcc * 128:(dcc + 1) * 128, :])

            def wup_ap(dcc, c0, cw):
                if dcc < 5:
                    return M1[:, dcc * 2816 + c0:dcc * 2816 + c0 + cw]
                return M2[:, (dcc - 5) * 2816 + c0:(dcc - 5) * 2816 + c0 + cw]

            for st in range(NST):
                r0 = b * S + st * 128
                hsrc = APx(sc["xqT"], (b * D) * S + st * 128,
                           [[S, 128], [128 * S, 8], [1, 128]])
                dma("sp", APx(HB2[:], 0, [[2048, 128], [128, 8], [1, 128]]), hsrc)
                for ncc in range(6):
                    cw = min(512, IFF - ncc * 512)
                    for dcc in range(8):
                        mm(PS[0][:, :cw], HB2[:, dcc * 128:(dcc + 1) * 128],
                           W[:, dcc * 2816 + ncc * 512:
                             dcc * 2816 + ncc * 512 + cw], dcc == 0, dcc == 7)
                    for dcc in range(8):
                        mm(PS[1][:, :cw], HB2[:, dcc * 128:(dcc + 1) * 128],
                           wup_ap(dcc, ncc * 512, cw), dcc == 0, dcc == 7)
                    a_act(CN[:, 0:cw], PS[0][:, :cw], AF.Silu)
                    v_tt(F3[:, ncc * 512:ncc * 512 + cw], CN[:, 0:cw],
                         PS[1][:, :cw], ALU.mult)
                quant_fp4(F3[:, 0:1408], Bq[:, 0:1408], 1408)
                quant_fp4(F3[:, 1408:2816], Bq[:, 1408:2816], 1408)
                for g2 in range(3):
                    n8 = min(8, 22 - g2 * 8)
                    for dc2 in range(n8):
                        di = g2 * 8 + dc2
                        trans(PB[0][:, :128], Bq[:, di * 128:(di + 1) * 128],
                              ident_bf[:])
                        v_copy(PTb[:, dc2 * 128:(dc2 + 1) * 128], PB[0][:, :128])
                    dsth = APx(sc["hqT"], (b * IFF + g2 * 8 * 128) * S + st * 128,
                               [[S, 128], [128 * S, n8], [1, 128]])
                    dma("act", dsth,
                        PTb[:, 0:n8 * 128].rearrange("p (c n) -> p c n", n=128))

        def ph_l1f_down(b):
            for dcc in range(22):
                dma("sp", W[:, dcc * 1024:(dcc + 1) * 1024],
                    sc["wdown_qT"][dcc * 128:(dcc + 1) * 128, :])
            for st in range(NST):
                r0 = b * S + st * 128
                hsrc = APx(sc["hqT"], (b * IFF) * S + st * 128,
                           [[S, 128], [128 * S, 22], [1, 128]])
                dma("sp", HB[:, 0:22 * 128].rearrange("p (c n) -> p c n", n=128),
                    hsrc)
                for nh2 in range(2):
                    pp = nh2
                    for dcc in range(22):
                        mm(PS[pp][:, :512], HB[:, dcc * 128:(dcc + 1) * 128],
                           W[:, dcc * 1024 + nh2 * 512:
                             dcc * 1024 + nh2 * 512 + 512],
                           dcc == 0, dcc == 21)
                    a_act(F2[:, 1024 + nh2 * 512:1024 + (nh2 + 1) * 512],
                          PS[pp][:, :512], AF.Silu)
                dma("sp", F1[:, 0:1024], sc["s_tm"][r0:r0 + 128, :])
                dma("sp", F3[:, 0:1024], sc["x1_tm"][r0:r0 + 128, :])
                v_tt(F2[:, 0:1024], F1[:, 0:1024], F2[:, 1024:2048], ALU.mult)
                v_tt(F3[:, 1024:2048], F3[:, 0:1024], F2[:, 0:1024], ALU.add)
                o = out[r0:r0 + 128, :]
                dma("g", o, F3[:, 1024:2048])
                ALL_OUT.append(o)

        if not stopped():
            for b in range(BC):
                ph_l1f_x(b)
        phase_done("l1f_x")
        if not stopped():
            for b in range(BC):
                ph_l1f_h(b)
                ph_l1f_down(b)
        phase_done("l1f")

        sch.add("g", lambda e: e.memset(zrow[:], 0.0), ins=ALL_OUT,
                outs=[zrow[:]], tag="fin")

        sch.finalize(DMA_SLOTS)

        @blk.tensor
        def _(t):
            sch.emit("pe", t, sems)

        @blk.vector
        def _(v):
            sch.emit("dve", v, sems)

        @blk.scalar
        def _(a):
            sch.emit("act", a, sems)

        @blk.sync
        def _(s):
            sch.emit("sp", s, sems)

        @blk.gpsimd
        def _(g):
            sch.emit("g", g, sems)

    return nc


def _host_consts():
    c = {}
    c["c_ident"] = np.eye(128, dtype=np.float32)
    A = np.zeros((128, 128), dtype=np.float32)
    for i in range(64):
        A[2 * i, 2 * i + 1] = -1.0
        A[2 * i + 1, 2 * i] = 1.0
    c["c_rotA"] = np.ascontiguousarray(A.T)
    inv0 = 1.0 / (10000.0 ** (np.arange(0, D, 2, dtype=np.float64) / D))
    t = np.arange(S, dtype=np.float64)
    ang0 = np.outer(inv0, t)
    c["c_cos0"] = np.repeat(np.cos(ang0), 2, axis=0).astype(np.float32)
    c["c_sin0"] = np.repeat(np.sin(ang0), 2, axis=0).astype(np.float32)
    inv1 = 1.0 / (10000.0 ** (np.arange(0, DH, 2, dtype=np.float64) / DH))
    ang1 = np.outer(inv1, t)
    c["c_cos1"] = np.tile(np.repeat(np.cos(ang1), 2, axis=0), (NH, 1)).astype(
        np.float32)
    c["c_sin1"] = np.tile(np.repeat(np.sin(ang1), 2, axis=0), (NH, 1)).astype(
        np.float32)
    pos = (S - 1.0) - np.arange(S, dtype=np.float64)
    angp = np.outer(inv0, pos)
    c["c_pet"] = np.concatenate([np.sin(angp), np.cos(angp)], axis=0).astype(
        np.float32)
    hmv = np.zeros((128, 8 * NH), dtype=np.float32)
    for dc in range(8):
        hmv[0:64, dc * NH + 2 * dc] = 1.0
        hmv[64:128, dc * NH + 2 * dc + 1] = 1.0
    c["c_hm"] = hmv
    c["c_cmask"] = np.tril(np.ones((128, 128), dtype=np.uint8))
    c["c_ones"] = np.ones((1, 128), dtype=np.float32)
    c["c_onesf"] = np.ones((1, 128), dtype=np.float32)
    c["c_ones2"] = np.ones((1, 512), dtype=np.float32)
    return c


def _host_weights(inp):
    f32 = np.float32
    a = lambda x: np.ascontiguousarray(np.asarray(x, dtype=f32))
    w = {}
    w["l0a_wq_t"] = a(np.asarray(inp["L0_attn_wq"]).T)
    w["l0a_wk_t"] = a(np.asarray(inp["L0_attn_wk"]).T)
    w["l0a_wv_t"] = a(np.asarray(inp["L0_attn_wv"]).T)
    w["l0a_wo_t"] = a(np.asarray(inp["L0_attn_wo"]).T)
    w["l0f_wqkv"] = a(inp["L0_ffn_Wqkv"])
    w["l0f_wo"] = a(inp["L0_ffn_Wo"])
    w["l0f_wr"] = a(inp["L0_ffn_Wr"])
    w["l0f_ubt"] = a(np.asarray(inp["L0_ffn_u_bias"]).T)
    w["l0f_vbt"] = a(np.asarray(inp["L0_ffn_v_bias"]).T)
    for nm in ("ln1", "ln2", "ln3"):
        w[f"l0f_{nm}g"] = a(inp[f"L0_ffn_{nm}_g"])
        w[f"l0f_{nm}b"] = a(inp[f"L0_ffn_{nm}_b"])
    w["l0f_wff1"] = a(inp["L0_ffn_Wff1"])
    w["l0f_bff1"] = a(np.asarray(inp["L0_ffn_bff1"]).reshape(FF // 128, 128).T)
    w["l0f_wff2"] = a(inp["L0_ffn_Wff2"])
    w["l0f_bff2"] = a(np.asarray(inp["L0_ffn_bff2"]).reshape(1, D))
    w["l0f_wg"] = a(inp["L0_ffn_Wg"])
    w["l0f_we1"] = a(inp["L0_ffn_We1"])
    w["l0f_be1"] = a(np.asarray(inp["L0_ffn_be1"]).reshape(NE, FF // 128, 128)
                     .transpose(0, 2, 1))
    w["l0f_we2"] = a(inp["L0_ffn_We2"])
    w["l0f_be2"] = a(np.asarray(inp["L0_ffn_be2"]).reshape(NE, 1, D))
    w["l0f_wout"] = a(inp["L0_ffn_Wout"])
    w["l1a_wq"] = a(inp["L1_attn_Wq"])
    w["l1a_wk"] = a(inp["L1_attn_Wk"])
    w["l1a_wv"] = a(inp["L1_attn_Wv"])
    w["l1a_wo"] = a(inp["L1_attn_Wo"])
    w["l1a_alpha"] = a(np.asarray(inp["L1_attn_alpha"]).reshape(1, 1))
    w["l1f_wgate"] = a(inp["L1_ffn_w_gate"])
    w["l1f_wup"] = a(inp["L1_ffn_w_up"])
    w["l1f_wdown"] = a(inp["L1_ffn_w_down"])
    return w


def kernel(**inputs):
    nc = build_nc()
    consts = _host_consts()
    weights = _host_weights(inputs)
    x = np.ascontiguousarray(np.asarray(inputs["L0_attn_x"], dtype=np.float32))
    in_maps = []
    for c in range(8):
        m = {"x": x[c * BC:(c + 1) * BC].reshape(BC * S, D)}
        m.update(weights)
        m.update(consts)
        in_maps.append(m)
    res = run_bass_kernel_spmd(nc, in_maps, core_ids=list(range(8)))
    outs = [np.asarray(r["out"]).reshape(BC, S, D) for r in res.results]
    return np.concatenate(outs, axis=0)


# version 10
# speedup vs baseline: 1.3597x; 1.0021x over previous
import math
import numpy as np
import ml_dtypes
import bass_rust
import concourse.bass as bass
import concourse.mybir as mybir
from concourse.bass_utils import run_bass_kernel_spmd
from contextlib import ExitStack

F32 = mybir.dt.float32
BF16 = mybir.dt.bfloat16
I16 = mybir.dt.int16
U8 = mybir.dt.uint8
AF = mybir.ActivationFunctionType
ALU = mybir.AluOpType
AX = mybir.AxisListType

BC, S, D = 2, 2048, 1024
NH, DH, FF, NE, IFF = 16, 64, 4096, 8, 2816
RS = 2176
NEG = -1.0e30
NST = S // 128

DEBUG_DUMPS = ()
DEBUG_STOP = None
DEBUG_L1_HEADS = NH
DEBUG_L1_MODE = 9
SERIAL_READS = False

SB_GRAN = 512
DR_GRAN = 1 << 18
GRAN_MAP = {"SM": 16, "SM3": 16}
_PSUM_NAMES = {f"ps{i}" for i in range(8)} | {"pb0", "pb1"}


def _keys(ap):
    t = ap.tensor
    name = t.name
    a = ap.ap
    if str(ap.space) == "MemorySpace.PSUM":
        return (name,)
    if str(ap.space) == "MemorySpace.DRAM":
        off = ap.offset
        span = 1
        for st, cnt in a:
            span += abs(st) * (cnt - 1)
        lo = off // DR_GRAN
        hi = (off + span - 1) // DR_GRAN
        return tuple(f"{name}:{i}" for i in range(lo, hi + 1))
    pitch = a[0][0] if len(a) > 0 else 1
    off = ap.offset % pitch if pitch > 0 else ap.offset
    span = 1
    for st, cnt in a[1:]:
        span += abs(st) * (cnt - 1)
    g = GRAN_MAP.get(name, SB_GRAN)
    lo = off // g
    hi = (off + span - 1) // g
    return tuple(f"{name}:{i}" for i in range(lo, hi + 1))


class _Op:
    __slots__ = ("eng", "fn", "dma", "idx", "deps", "signals", "cnt", "tag",
                 "slot")


class Sched:
    def __init__(self):
        self.ops = []
        self.res = {}

    def add(self, eng, fn, ins=(), outs=(), dma=False, tag=""):
        op = _Op()
        op.eng = eng; op.fn = fn; op.dma = dma; op.tag = tag
        op.idx = len(self.ops); op.signals = False; op.cnt = 0
        deps = {}

        def dep(p):
            if p is None or p is op:
                return
            if (not p.dma) and (not op.dma) and p.eng == "pe" and eng == "pe":
                return
            if p.dma:
                k = ("D", p.eng, p.idx)
            else:
                k = (p.eng, False)
            cur = deps.get(k)
            if cur is None or p.idx > cur.idx:
                deps[k] = p

        rkeys = []
        wkeys = []
        for ap_ in ins:
            for k in _keys(ap_):
                if SERIAL_READS or k in _PSUM_NAMES:
                    wkeys.append(k)
                else:
                    rkeys.append(k)
        for ap_ in outs:
            wkeys.extend(_keys(ap_))
        for r in rkeys:
            st = self.res.get(r)
            if st is not None:
                dep(st[0])
        for wk in wkeys:
            st = self.res.get(wk)
            if st is not None:
                dep(st[0])
                for rr in st[1]:
                    dep(rr)
        op.deps = list(deps.values())
        for p in op.deps:
            p.signals = True
        for r in rkeys:
            st = self.res.get(r)
            if st is None:
                st = self.res[r] = [None, []]
            st[1].append(op)
        for wk in wkeys:
            self.res[wk] = [op, []]
        self.ops.append(op)
        return op

    def finalize(self, dma_slots):
        cnt = {}
        for op in self.ops:
            if op.dma:
                op.signals = True
                nq = cnt.get(op.eng, 0)
                cnt[op.eng] = nq + 1
                R = dma_slots[op.eng]
                op.slot = nq % R
                op.cnt = nq // R + 1
            elif op.signals:
                k = (op.eng, False)
                cnt[k] = cnt.get(k, 0) + 1
                op.cnt = cnt[k]

    def emit(self, eng, handle, sems):
        hwm = {}
        for op in self.ops:
            if op.eng != eng:
                continue
            if op.dma and op.cnt > 1:
                k = ("D", op.eng, op.slot)
                val = (op.cnt - 1) * 16
                if hwm.get(k, 0) < val:
                    handle.wait_ge(sems[k], val)
                    hwm[k] = val
            for p in op.deps:
                if p.dma:
                    k = ("D", p.eng, p.slot)
                    val = p.cnt * 16
                else:
                    k = (p.eng, False)
                    val = p.cnt
                if hwm.get(k, 0) < val:
                    handle.wait_ge(sems[k], val)
                    hwm[k] = val
            inst = op.fn(handle)
            if op.signals:
                if op.dma:
                    inst.then_inc(sems[("D", op.eng, op.slot)], 16)
                else:
                    inst.then_inc(sems[(op.eng, False)], 1)


def APx(base_ap, offset, pattern):
    return bass_rust.AP(base_ap.tensor, offset, pattern)


def cdiv(a, b):
    return (a + b - 1) // b


def build_nc():
    nc = bass.Bass("TRN2", target_bir_lowering=False, debug=False)
    dumps = set(DEBUG_DUMPS)

    def din(name, shape, dtype=F32):
        return nc.dram_tensor(name, shape, dtype, kind="ExternalInput").ap()

    w = {}
    w["x"] = din("x", [BC * S, D])
    for item in [
        ("l0a_wq_t", [D, D]), ("l0a_wk_t", [D, D]), ("l0a_wv_t", [D, D]),
        ("l0a_wo_t", [D, D]),
        ("l0f_wqkv", [D, 3 * D]), ("l0f_wo", [D, D]), ("l0f_wr", [D, D]),
        ("l0f_ubt", [DH, NH]), ("l0f_vbt", [DH, NH]),
        ("l0f_ln1g", [D]), ("l0f_ln1b", [D]), ("l0f_ln2g", [D]), ("l0f_ln2b", [D]),
        ("l0f_ln3g", [D]), ("l0f_ln3b", [D]),
        ("l0f_wff1", [D, FF]), ("l0f_bff1", [128, FF // 128]),
        ("l0f_wff2", [FF, D]), ("l0f_bff2", [1, D]),
        ("l0f_wg", [D, NE]),
        ("l0f_we1", [NE, D, FF]), ("l0f_be1", [NE, 128, FF // 128]),
        ("l0f_we2", [NE, FF, D]), ("l0f_be2", [NE, 1, D]),
        ("l0f_wout", [D, D]),
        ("l1a_wq", [D, D]), ("l1a_wk", [D, D]), ("l1a_wv", [D, D]),
        ("l1a_wo", [D, D]), ("l1a_alpha", [1, 1]),
        ("l1f_wgate", [IFF, D]), ("l1f_wup", [IFF, D]), ("l1f_wdown", [D, IFF]),
        ("c_ident", [128, 128]), ("c_rotA", [128, 128]),
        ("c_cos0", [D, S]), ("c_sin0", [D, S]),
        ("c_cos1", [D, S]), ("c_sin1", [D, S]),
        ("c_pet", [D, S]), ("c_hm", [128, 8 * NH]),
        ("c_cmask", [128, 128], U8), ("c_ones", [1, 128]), ("c_onesf", [1, 128]), ("c_ones2", [1, 512]),
    ]:
        nm, shp = item[0], item[1]
        w[nm] = din(nm, shp, item[2] if len(item) > 2 else F32)

    out = nc.dram_tensor("out", [BC * S, D], F32, kind="ExternalOutput").ap()

    sc = {}

    def S_(name, shape, dtype=F32):
        kind = "ExternalOutput" if name in dumps else "Internal"
        sc[name] = nc.dram_tensor(name, shape, dtype, kind=kind).ap()

    S_("xn_tm", [BC * S, D]); S_("xnT", [BC * D, S], BF16)
    S_("l0a_qT", [BC * D, S], BF16); S_("l0a_kT", [BC * D, S], BF16)
    S_("l0a_v", [BC * S, D], BF16)
    S_("s_tm", [BC * S, D])
    S_("l0f_qT", [BC * D, S], BF16); S_("l0f_kT", [BC * D, S], BF16)
    S_("l0f_v", [BC * S, D], BF16); S_("rT", [D, S], BF16)
    S_("bd0", [128 * RS], BF16); S_("bd1", [128 * RS], BF16)
    S_("h1_tm", [BC * S, D]); S_("h1T", [BC * D, S], BF16)
    S_("h2_tm", [BC * S, D]); S_("h2T", [BC * D, S], BF16)
    S_("m_tm", [BC * S, NE]); S_("moe_tm", [BC * S, D])
    S_("x1_tm", [BC * S, D])
    S_("xn1_tm", [BC * S, D]); S_("xn1T", [BC * D, S], BF16)
    S_("l1_qT", [BC * D, S], BF16); S_("l1_kT", [BC * D, S], BF16)
    S_("l1_v", [BC * S, D], BF16)
    S_("l1_qn", [BC * NH, S]); S_("l1_kn", [BC * NH, S])
    S_("l1_qnT", [BC * S, NH])
    S_("wgate_qT", [D, IFF], BF16); S_("wup_qT", [D, IFF], BF16)
    S_("wdown_qT", [IFF, D], BF16)
    S_("hqT", [BC * IFF, S], BF16); S_("xqT", [BC * D, S], BF16)
    S_("ctx0T", [BC * D, S], BF16); S_("ctx1T", [BC * D, S], BF16)

    sch = Sched()

    with ExitStack() as ctx:
        def sb(name, shape, dtype=F32):
            return ctx.enter_context(nc.sbuf_tensor(name, shape, dtype))

        M1 = sb("M1", [128, 16384], BF16)
        M2 = sb("M2", [128, 16384], BF16)
        W = sb("W", [128, 32768], BF16)
        F1 = sb("F1", [128, 2048])
        F2 = sb("F2", [128, 2816])
        F3 = sb("F3", [128, 2816])
        CN = sb("CN", [128, 2048])
        Bq = sb("Bq", [128, 2816], BF16)
        PTb = sb("PTb", [128, 2048], BF16)
        HB = sb("HB", [128, 4096], BF16)
        HB2 = sb("HB2", [128, 2048], BF16)
        IQ = sb("IQ", [128, 1408], I16)
        U8T = sb("U8T", [128, 1408], U8)
        U8b = sb("U8b", [128, 1408], U8)
        LGB = sb("LGB", [128, 2048], BF16)
        SM = sb("SM", [128, 64])
        SM3 = sb("SM3", [128, 64])
        DG = sb("DG", [128, 384], BF16)
        MME = sb("MME", [128, 2 * NE])
        QNS = sb("QNS", [128, 16])
        WGS = sb("WGS", [128, 8 * NE], BF16)
        BR = sb("BR", [1, 1024], BF16)
        ident_bf = sb("ident_bf", [128, 128], BF16)
        ident_f = sb("ident_f", [128, 128], F32)
        rotA = sb("rotA", [128, 128], BF16)
        cmask = sb("cmask", [128, 128], U8)
        negT = sb("negT", [128, 128])
        ones_bf = sb("ones_bf", [1, 128], BF16)
        ones_f = sb("ones_f", [1, 128], F32)
        ones2_bf = sb("ones2_bf", [1, 512], BF16)
        ubt = sb("ubt", [DH, NH])
        vbt = sb("vbt", [DH, NH])
        hm = sb("hm", [128, 8 * NH], BF16)
        bff1_sb = sb("bff1_sb", [128, FF // 128])
        be1_sb = sb("be1_sb", [128, FF // 128])
        alpha_sc = sb("alpha_sc", [128, 1])
        lnsc = sb("lnsc", [1, 1])
        arow = sb("arow", [1, 1])
        lnrow = sb("lnrow", [1, D], BF16)
        zrow = sb("zrow", [128, 1])
        eps1 = sb("eps1", [128, 1])

        PS = [ctx.enter_context(nc.psum_tensor(f"ps{i}", [128, 512], F32))
              for i in range(7)]
        PB = [ctx.enter_context(nc.psum_tensor("pb0", [128, 1024], BF16))
              for i in range(1)]

        sems = {}
        for e in ("pe", "dve", "act", "sp", "g"):
            sems[(e, False)] = ctx.enter_context(nc.semaphore(f"sc_{e}"))
        DMA_SLOTS = {"sp": 36, "g": 36, "act": 12}
        for e, R in DMA_SLOTS.items():
            for i in range(R):
                sems[("D", e, i)] = ctx.enter_context(nc.semaphore(f"sd_{e}{i}"))

        blk = ctx.enter_context(nc.Block())

        def _outer16(ap_):
            for st_, cnt_ in ap_.ap:
                if cnt_ != 1:
                    return cnt_ % 16 == 0
            return False

        def dma(q, out_ap, in_ap, tag=""):
            if not (_outer16(out_ap) and _outer16(in_ap)):
                q = "act"
            elif q == "act":
                q = "g"
            sch.add(q, lambda e, o=out_ap, i=in_ap: e.dma_start(out=o, in_=i),
                    ins=[in_ap], outs=[out_ap], dma=True, tag=tag)

        def dma_acc(out_ap, in_ap, tag=""):
            sch.add("g", lambda e, o=out_ap, i=in_ap:
                    e.dma_start(out=o, in_=i, accum_op=ALU.add),
                    ins=[in_ap, out_ap], outs=[out_ap], dma=True, tag=tag)

        def mm(ps_ap, lhsT, rhs, start, stop, tag=""):
            sch.add("pe", lambda e, o=ps_ap, l=lhsT, r=rhs, s1=start, s2=stop:
                    e.matmul(o, l, r, start=s1, stop=s2),
                    ins=[lhsT, rhs], outs=[ps_ap], tag=tag)

        def trans(ps_ap, in_ap, id_ap, tag=""):
            sch.add("pe", lambda e, o=ps_ap, i=in_ap, d=id_ap: e.transpose(o, i, d),
                    ins=[in_ap, id_ap], outs=[ps_ap], tag=tag)

        def v_ts(out_ap, in_ap, s1, s2, op0, op1=None, accum=None, tag=""):
            ins = [in_ap] + [s for s in (s1, s2) if hasattr(s, "tensor")]
            outs = [out_ap] + ([accum] if accum is not None else [])
            if op1 is None and s2 is None and accum is None:
                sch.add("dve", lambda e, o=out_ap, i=in_ap, a=s1, p0=op0:
                        e.tensor_scalar(o, i, a, None, op0=p0),
                        ins=ins, outs=outs, tag=tag)
            elif accum is None:
                sch.add("dve", lambda e, o=out_ap, i=in_ap, a=s1, b=s2, p0=op0,
                        p1=op1: e.tensor_scalar(o, i, a, b, op0=p0, op1=p1),
                        ins=ins, outs=outs, tag=tag)
            else:
                sch.add("dve", lambda e, o=out_ap, i=in_ap, a=s1, b=s2, p0=op0,
                        p1=op1, ac=accum:
                        e.tensor_scalar(o, i, a, b, op0=p0, op1=p1, accum_out=ac),
                        ins=ins, outs=outs, tag=tag)

        def v_tt(out_ap, a_ap, b_ap, op, tag=""):
            sch.add("dve", lambda e, o=out_ap, a=a_ap, b=b_ap, p=op:
                    e.tensor_tensor(o, a, b, op=p),
                    ins=[a_ap, b_ap], outs=[out_ap], tag=tag)

        def v_stt(out_ap, in0, scal, in1, op0, op1, tag=""):
            ins = [in0, in1] + ([scal] if hasattr(scal, "tensor") else [])
            sch.add("dve", lambda e, o=out_ap, a=in0, s=scal, b=in1, p0=op0, p1=op1:
                    e.scalar_tensor_tensor(o, a, s, b, op0=p0, op1=p1),
                    ins=ins, outs=[out_ap], tag=tag)

        def v_copy(out_ap, in_ap, tag=""):
            sch.add("dve", lambda e, o=out_ap, i=in_ap: e.tensor_copy(o, i),
                    ins=[in_ap], outs=[out_ap], tag=tag)

        def v_red(out_ap, in_ap, op, absv=None, tag=""):
            sch.add("dve", lambda e, o=out_ap, i=in_ap, p=op, av=absv:
                    e.tensor_reduce(o, i, axis=AX.X, op=p, apply_absolute_value=av),
                    ins=[in_ap], outs=[out_ap], tag=tag)

        def v_recip(out_ap, in_ap, tag=""):
            sch.add("dve", lambda e, o=out_ap, i=in_ap: e.reciprocal(o, i),
                    ins=[in_ap], outs=[out_ap], tag=tag)

        def g_ts(out_ap, in_ap, s1, s2, op0, op1=None, tag=""):
            ins = [in_ap] + [x for x in (s1, s2) if hasattr(x, "tensor")]
            if op1 is None and s2 is None:
                sch.add("g", lambda e, o=out_ap, i=in_ap, a=s1, p0=op0:
                        e.tensor_scalar(o, i, a, None, op0=p0),
                        ins=ins, outs=[out_ap], tag=tag)
            else:
                sch.add("g", lambda e, o=out_ap, i=in_ap, a=s1, b=s2, p0=op0,
                        p1=op1: e.tensor_scalar(o, i, a, b, op0=p0, op1=p1),
                        ins=ins, outs=[out_ap], tag=tag)

        def g_tt(out_ap, a_ap, b_ap, op, tag=""):
            sch.add("g", lambda e, o=out_ap, a=a_ap, b=b_ap, p=op:
                    e.tensor_tensor(o, a, b, op=p),
                    ins=[a_ap, b_ap], outs=[out_ap], tag=tag)

        def v_sel(out_ap, mask, t_ap, f_ap, tag=""):
            sch.add("dve", lambda e, o=out_ap, m=mask, a=t_ap, b=f_ap:
                    e.select(o, m, a, b), ins=[mask, t_ap, f_ap], outs=[out_ap],
                    tag=tag)

        def v_memset(ap_, val, tag=""):
            sch.add("dve", lambda e, o=ap_, v=val: e.memset(o, v), ins=[],
                    outs=[ap_], tag=tag)

        def a_copy(out_ap, in_ap, tag=""):
            sch.add("act", lambda e, o=out_ap, i=in_ap: e.activation(
                o, i, AF.Identity), ins=[in_ap], outs=[out_ap], tag=tag)

        def a_act(out_ap, in_ap, func, bias=None, scale=None, accum=None, tag=""):
            ins = [in_ap]
            if bias is not None and hasattr(bias, "tensor"):
                ins.append(bias)
            outs = [out_ap] + ([accum] if accum is not None else [])
            kw = {}
            if bias is not None:
                kw["bias"] = bias
            if scale is not None:
                kw["scale"] = scale

            def f(e, o=out_ap, i=in_ap, fc=func, kw=kw, ac=accum):
                if ac is not None:
                    return e.activation(o, i, fc, accum_out=ac, **kw)
                return e.activation(o, i, fc, **kw)
            sch.add("act", f, ins=ins, outs=outs, tag=tag)

        ALL_OUT = []

        stop_hit = [False]

        def stopped():
            return stop_hit[0]

        def phase_done(name):
            if DEBUG_STOP is not None and name == DEBUG_STOP:
                stop_hit[0] = True

        dma("g", ident_bf[:], w["c_ident"])
        dma("sp", ident_f[:], w["c_ident"])
        dma("g", rotA[:], w["c_rotA"])
        dma("sp", cmask[:], w["c_cmask"])
        dma("g", ones_bf[:], w["c_ones"])
        dma("sp", ones_f[:], w["c_onesf"])
        dma("g", ones2_bf[:], w["c_ones2"])
        dma("sp", ubt[:], w["l0f_ubt"])
        dma("sp", vbt[:], w["l0f_vbt"])
        dma("g", hm[:], w["c_hm"])
        dma("sp", bff1_sb[:], w["l0f_bff1"])
        v_memset(negT[:], NEG)
        v_memset(zrow[:], 0.0)
        v_memset(eps1[:], 1e-5)
        s0 = float(math.sqrt(DH) / math.log(1.0 + DH))
        dma("sp", arow[:], w["l1a_alpha"])
        a_act(lnsc[:], arow[:], AF.Exp, scale=math.log(s0))
        mm(PS[5][:, :1], ones_f[:], lnsc[:], True, True)
        v_copy(alpha_sc[:], PS[5][:, :1])

        def load_lgb(gnm, bnm):
            for j, nm in enumerate((gnm, bnm)):
                dma("g", lnrow[:], w[nm].unsqueeze(0))
                for h in range(2):
                    mm(PS[5][:, :512], ones_bf[:], lnrow[:, h * 512:(h + 1) * 512],
                       True, True)
                    v_copy(LGB[:, j * 1024 + h * 512:j * 1024 + (h + 1) * 512],
                           PS[5][:, :512])

        def ln_tile(src, dst, smbase, affine=False):
            c = smbase
            v_red(SM[:, c:c + 1], src, ALU.add)
            a_act(CN[:, 0:1024], src, AF.Square, accum=SM[:, c + 1:c + 2])
            v_ts(SM[:, c + 2:c + 3], SM[:, c:c + 1], 1.0 / D, None, ALU.mult)
            v_tt(SM[:, c + 3:c + 4], SM[:, c + 2:c + 3], SM[:, c + 2:c + 3],
                 ALU.mult)
            v_ts(SM[:, c + 4:c + 5], SM[:, c + 1:c + 2], 1.0 / D,
                 SM[:, c + 3:c + 4], ALU.mult, ALU.subtract)
            a_act(SM[:, c + 5:c + 6], SM[:, c + 4:c + 5], AF.Ln, bias=eps1[:])
            a_act(SM[:, c + 6:c + 7], SM[:, c + 5:c + 6], AF.Exp, scale=-0.5)
            v_tt(SM[:, c + 7:c + 8], SM[:, c + 2:c + 3], SM[:, c + 6:c + 7],
                 ALU.mult)
            if not affine:
                v_ts(dst, src, SM[:, c + 6:c + 7], SM[:, c + 7:c + 8],
                     ALU.mult, ALU.subtract)
            else:
                v_ts(CN[:, 1024:2048], src, SM[:, c + 6:c + 7],
                     SM[:, c + 7:c + 8], ALU.mult, ALU.subtract)
                v_tt(CN[:, 0:1024], CN[:, 1024:2048], LGB[:, 0:1024], ALU.mult)
                v_tt(dst, CN[:, 0:1024], LGB[:, 1024:2048], ALU.add)

        def t8_store(src_bf, dram_ap, rowbase, col, nch=8):
            for dc in range(nch):
                trans(PB[0][:, :128], src_bf[:, dc * 128:(dc + 1) * 128],
                      ident_bf[:])
                v_copy(PTb[:, dc * 128:(dc + 1) * 128], PB[0][:, :128])
            dst = APx(dram_ap, rowbase * S + col,
                      [[S, 128], [128 * S, nch], [1, 128]])
            srcv = PTb[:, 0:nch * 128].rearrange("p (c n) -> p c n", n=128)
            dma("act", dst, srcv)

        def ph_ln_in(b, src_ap, dst_tm, dst_T):
            for st in range(NST):
                r0 = b * S + st * 128
                p2 = st % 2
                xin = F2[:, p2 * 1024:(p2 + 1) * 1024]
                xo = F3[:, p2 * 1024:(p2 + 1) * 1024]
                dma("sp", xin, src_ap[r0:r0 + 128, :])
                ln_tile(xin, xo, p2 * 16, affine=False)
                dma("g", dst_tm[r0:r0 + 128, :], xo)
                a_act(Bq[:, 0:1024], xo, AF.Identity)
                t8_store(Bq[:, 0:1024], dst_T, b * D, st * 128)

        for b in range(BC):
            ph_ln_in(b, w["x"], sc["xn_tm"], sc["xnT"])
        phase_done("ln0")

        def quant_fp4(src, dst_bf, wd, alt=False):
            nb = wd // 16
            bmraw = CN[:, 1408:1408 + nb]
            bmc = CN[:, 1536:1536 + nb]
            rec = CN[:, 1792:1792 + nb]
            src3 = src.rearrange("p (b k) -> p b k", k=16)

            def bc16(ap2d):
                a = ap2d.ap
                return APx(ap2d, ap2d.offset, [[a[0][0], 128], [a[1][0], nb],
                                               [0, 16]])
            if alt:
                T = F1[:, 0:wd]
                SG = F3[:, 1408:1408 + wd]
            else:
                T = F2[:, 0:wd]
                SG = F1[:, 0:wd]
            SQ = CN[:, 0:wd]
            v_red(bmraw, src3, ALU.max, absv=True)
            v_ts(bmc, bmraw, 1.0 / 6.0, 1e-12, ALU.mult, ALU.max)
            v_recip(rec, bmc)
            t3 = T.rearrange("p (b k) -> p b k", k=16)
            v_tt(t3, src3, bc16(rec), ALU.mult)
            a_act(SQ, T, AF.Square)
            a_act(SG, T, AF.Sign)
            v_ts(IQ[:, 0:wd], T, 2.0, None, ALU.mult)
            v_ts(src, IQ[:, 0:wd], 0.5, None, ALU.mult)
            v_ts(U8T[:, 0:wd], SQ, 5.0625, None, ALU.is_lt)
            v_ts(U8b[:, 0:wd], SQ, 20.25, None, ALU.is_lt)
            v_ts(IQ[:, 0:wd], T, 1.0, None, ALU.mult)
            v_ts(T, SQ, 25.0, None, ALU.is_gt)
            v_stt(SQ, T, 2.0, SG, ALU.mult, ALU.mult)
            v_ts(T, SG, 4.0, None, ALU.mult)
            v_tt(SG, SQ, T, ALU.add)
            v_copy(T, IQ[:, 0:wd])
            v_sel(SQ, U8b[:, 0:wd], T, SG)
            v_sel(T, U8T[:, 0:wd], src, SQ)
            d3 = dst_bf.rearrange("p (b k) -> p b k", k=16)
            v_tt(d3, t3, bc16(bmc), ALU.mult)

        def emit_wq_tile(src_dram, n_rows, kdim, dst_dram, it):
            nh = cdiv(kdim, 1408)
            for hh in range(nh):
                c0 = hh * 1408
                cw = min(1408, kdim - c0)
                dma("sp", F3[:, c0:c0 + cw],
                    src_dram[it * 128:(it + 1) * 128, c0:c0 + cw])
                quant_fp4(F3[:, c0:c0 + cw], Bq[:, c0:c0 + cw], cw,
                          alt=(kdim == D))
            nch = kdim // 128
            for g2 in range(cdiv(nch, 8)):
                n8 = min(8, nch - g2 * 8)
                for dc in range(n8):
                    di = g2 * 8 + dc
                    trans(PB[0][:, :128], Bq[:, di * 128:(di + 1) * 128],
                          ident_bf[:])
                    v_copy(PTb[:, dc * 128:(dc + 1) * 128], PB[0][:, :128])
                dst = APx(dst_dram, (g2 * 8 * 128) * n_rows + it * 128,
                          [[n_rows, 128], [128 * n_rows, n8], [1, 128]])
                dma("act", dst,
                    PTb[:, 0:n8 * 128].rearrange("p (c n) -> p c n", n=128))

        WQ_TASKS = []
        for args_ in ((w["l1f_wgate"], IFF, D, sc["wgate_qT"]),
                      (w["l1f_wup"], IFF, D, sc["wup_qT"]),
                      (w["l1f_wdown"], D, IFF, sc["wdown_qT"])):
            for it in range(args_[1] // 128):
                WQ_TASKS.append(args_ + (it,))
        WQ_TASKS.reverse()
        phase_done("wquant")

        def load_xnT_M1(b, src):
            for dc in range(8):
                dma("sp", M1[:, dc * 2048:(dc + 1) * 2048],
                    src[b * D + dc * 128:b * D + dc * 128 + 128, :])

        def load_w_bf(arena, base, src_dram, rows, cols):
            for dc in range(rows // 128):
                dma("g", arena[:, base + dc * cols:base + (dc + 1) * cols],
                    src_dram[dc * 128:(dc + 1) * 128, :])

        TRIGC = 24576
        TRIGS = 26624

        def proj_rope(b, wbase, cosd, sind, dstd, qnpsum=None):
            for ec in range(8):
                dma("g", W[:, TRIGC:TRIGC + 2048], cosd[ec * 128:(ec + 1) * 128, :])
                dma("g", W[:, TRIGS:TRIGS + 2048], sind[ec * 128:(ec + 1) * 128, :])
                for s4 in range(4):
                    pp = s4 % 2
                    for dc in range(8):
                        mm(PS[pp][:, :512],
                           W[:, wbase + dc * 1024 + ec * 128:
                             wbase + dc * 1024 + ec * 128 + 128],
                           M1[:, dc * 2048 + s4 * 512:dc * 2048 + (s4 + 1) * 512],
                           dc == 0, dc == 7)
                    v_copy(Bq[:, s4 * 512:(s4 + 1) * 512], PS[pp][:, :512])
                    mm(PS[pp][:, :512], rotA[:], Bq[:, s4 * 512:(s4 + 1) * 512],
                       True, True)
                    v_tt(F2[:, s4 * 512:(s4 + 1) * 512], PS[pp][:, :512],
                         W[:, TRIGS + s4 * 512:TRIGS + (s4 + 1) * 512], ALU.mult)
                    v_tt(F3[:, s4 * 512:(s4 + 1) * 512],
                         Bq[:, s4 * 512:(s4 + 1) * 512],
                         W[:, TRIGC + s4 * 512:TRIGC + (s4 + 1) * 512], ALU.mult)
                    v_tt(PTb[:, s4 * 512:(s4 + 1) * 512],
                         F2[:, s4 * 512:(s4 + 1) * 512],
                         F3[:, s4 * 512:(s4 + 1) * 512], ALU.add)
                    if qnpsum is not None:
                        a_act(HB[:, s4 * 512:(s4 + 1) * 512],
                              PTb[:, s4 * 512:(s4 + 1) * 512], AF.Square)
                        mm(PS[2 + s4][0:16, :512], hm[:, ec * 16:(ec + 1) * 16],
                           HB[:, s4 * 512:(s4 + 1) * 512], ec == 0, ec == 7)
                dma("act", dstd[b * D + ec * 128:b * D + ec * 128 + 128, :], PTb[:])

        def proj_v(b, rhs_fn, dstd):
            for st in range(NST):
                p2 = st % 2
                for nh2 in range(2):
                    pp = nh2
                    for dc in range(8):
                        mm(PS[pp][:, :512],
                           M1[:, dc * 2048 + st * 128:dc * 2048 + st * 128 + 128],
                           rhs_fn(dc, nh2), dc == 0, dc == 7)
                    v_copy(HB[:, p2 * 1024 + nh2 * 512:p2 * 1024 + (nh2 + 1) * 512],
                           PS[pp][:, :512])
                dma("act", dstd[b * S + st * 128:b * S + st * 128 + 128, :],
                    HB[:, p2 * 1024:(p2 + 1) * 1024])

        def ph_l0a_proj(b):
            load_xnT_M1(b, sc["xnT"])
            load_w_bf(W, 0, w["l0a_wq_t"], D, D)
            load_w_bf(W, 8192, w["l0a_wk_t"], D, D)
            load_w_bf(W, 16384, w["l0a_wv_t"], D, D)
            proj_rope(b, 0, w["c_cos0"], w["c_sin0"], sc["l0a_qT"])
            proj_rope(b, 8192, w["c_cos0"], w["c_sin0"], sc["l0a_kT"])
            proj_v(b, lambda dc, nh2: W[:, 16384 + dc * 1024 + nh2 * 512:
                                        16384 + dc * 1024 + nh2 * 512 + 512],
                   sc["l0a_v"])

        def ph_l0a_attn(b):
            for dc in range(8):
                dma("sp", M1[:, dc * 2048:(dc + 1) * 2048],
                    sc["l0a_qT"][b * D + dc * 128:b * D + dc * 128 + 128, :])
                dma("sp", M2[:, dc * 2048:(dc + 1) * 2048],
                    sc["l0a_kT"][b * D + dc * 128:b * D + dc * 128 + 128, :])
            for st in range(NST):
                dma("sp", W[:, st * 1024:(st + 1) * 1024],
                    sc["l0a_v"][b * S + st * 128:b * S + st * 128 + 128, :])
            load_w_bf(W, 16384, w["l0a_wo_t"], D, D)

            def stage1(qt, par):
                sm = 32 * par
                for kc4 in range(4):
                    pp = kc4 % 2
                    for dc in range(8):
                        mm(PS[pp][:, :512],
                           M1[:, dc * 2048 + qt * 128:dc * 2048 + qt * 128 + 128],
                           M2[:, dc * 2048 + kc4 * 512:dc * 2048 + (kc4 + 1) * 512],
                           dc == 0, dc == 7)
                    sbuf_sc = F1 if par == 0 else F2
                    v_ts(sbuf_sc[:, kc4 * 512:(kc4 + 1) * 512],
                         PS[pp][:, :512], 1.0, -3.0e38, ALU.mult, ALU.max,
                         accum=SM[:, sm + kc4:sm + kc4 + 1])
                v_red(SM[:, sm + 4:sm + 5], SM[:, sm:sm + 4], ALU.max)
                v_ts(SM[:, sm + 5:sm + 6], SM[:, sm + 4:sm + 5], -1.0 / 32.0,
                     None, ALU.mult)

            def stage2(qt, par):
                sm = 32 * par
                prb0 = 24576 + par * 2048
                ptb0 = 28672 + par * 2048
                sbuf_sc = F1 if par == 0 else F2
                a_act(APx(W[:], prb0, [[32768, 128], [1, 2048]]),
                      sbuf_sc[:, 0:2048], AF.Exp,
                      bias=SM[:, sm + 5:sm + 6], scale=1.0 / 32.0,
                      accum=SM[:, sm + 6:sm + 7])
                v_recip(SM[:, sm + 7:sm + 8], SM[:, sm + 6:sm + 7])
                v_ts(DG[:, par * 128:(par + 1) * 128], ident_bf[:],
                     SM[:, sm + 7:sm + 8], None, ALU.mult)
                for kc in range(16):
                    pp = 4 + (kc // 4) % 2
                    mm(PS[pp][:, (kc % 4) * 128:(kc % 4) * 128 + 128],
                       APx(W[:], prb0 + kc * 128, [[32768, 128], [1, 128]]),
                       DG[:, par * 128:(par + 1) * 128], True, True)
                    if kc % 4 == 3:
                        a_copy(APx(W[:], ptb0 + (kc - 3) * 128,
                                   [[32768, 128], [1, 512]]), PS[pp][:, :512])
                for dc in range(8):
                    pp = 4 + (dc // 4) % 2
                    for kc in range(16):
                        mm(PS[pp][:, (dc % 4) * 128:(dc % 4) * 128 + 128],
                           W[:, kc * 1024 + dc * 128:kc * 1024 + dc * 128 + 128],
                           APx(W[:], ptb0 + kc * 128, [[32768, 128], [1, 128]]),
                           kc == 0, kc == 15)
                    if dc % 4 == 3:
                        v_copy(HB[:, par * 2048 + (dc - 3) * 128:
                                  par * 2048 + (dc + 1) * 128], PS[pp][:, :512])
                for nh2 in range(2):
                    pp = 2 + nh2
                    for dc in range(8):
                        mm(PS[pp][:, :512],
                           HB[:, par * 2048 + dc * 128:par * 2048 + (dc + 1) * 128],
                           W[:, 16384 + dc * 1024 + nh2 * 512:
                             16384 + dc * 1024 + nh2 * 512 + 512],
                           dc == 0, dc == 7)
                    v_copy(CN[:, par * 1024 + nh2 * 512:par * 1024 + (nh2 + 1) * 512],
                           PS[pp][:, :512])
                dma("act", sc["s_tm"][b * S + qt * 128:b * S + qt * 128 + 128, :],
                    CN[:, par * 1024:par * 1024 + 1024])

            for qt in range(NST):
                stage1(qt, qt % 2)
                if qt >= 1:
                    stage2(qt - 1, (qt - 1) % 2)
            stage2(NST - 1, (NST - 1) % 2)

        if not stopped():
            for b in range(BC):
                ph_l0a_proj(b)
        phase_done("l0a_proj")
        if not stopped():
            for b in range(BC):
                ph_l0a_attn(b)
        phase_done("l0a")

        def ph_rT():
            for dc in range(8):
                dma("g", M2[:, dc * 2048:(dc + 1) * 2048],
                    w["c_pet"][dc * 128:(dc + 1) * 128, :])
                dma("g", W[:, 24576 + dc * 1024:24576 + (dc + 1) * 1024],
                    w["l0f_wr"][dc * 128:(dc + 1) * 128, :])
            for ec in range(8):
                for s4 in range(4):
                    pp = s4 % 2
                    for dc in range(8):
                        mm(PS[pp][:, :512],
                           W[:, 24576 + dc * 1024 + ec * 128:
                             24576 + dc * 1024 + ec * 128 + 128],
                           M2[:, dc * 2048 + s4 * 512:dc * 2048 + (s4 + 1) * 512],
                           dc == 0, dc == 7)
                    v_copy(PTb[:, s4 * 512:(s4 + 1) * 512], PS[pp][:, :512])
                dma("act", sc["rT"][ec * 128:(ec + 1) * 128, :], PTb[:])

        def ph_l0f_qkv(b):
            load_xnT_M1(b, sc["xnT"])
            for dc in range(8):
                dma("g", W[:, dc * 3072:(dc + 1) * 3072],
                    w["l0f_wqkv"][dc * 128:(dc + 1) * 128, :])
            for cbase, dstd in ((0, sc["l0f_qT"]), (D, sc["l0f_kT"])):
                for ec in range(8):
                    for s4 in range(4):
                        pp = s4 % 2
                        for dc in range(8):
                            mm(PS[pp][:, :512],
                               W[:, dc * 3072 + cbase + ec * 128:
                                 dc * 3072 + cbase + ec * 128 + 128],
                               M1[:, dc * 2048 + s4 * 512:
                                  dc * 2048 + (s4 + 1) * 512],
                               dc == 0, dc == 7)
                        v_copy(PTb[:, s4 * 512:(s4 + 1) * 512], PS[pp][:, :512])
                    dma("act", dstd[b * D + ec * 128:b * D + ec * 128 + 128, :],
                        PTb[:])
            proj_v(b, lambda dc, nh2: W[:, dc * 3072 + 2 * D + nh2 * 512:
                                        dc * 3072 + 2 * D + nh2 * 512 + 512],
                   sc["l0f_v"])

        if not stopped():
            ph_rT()
            for b in range(BC):
                ph_l0f_qkv(b)
        phase_done("l0f_qkv")

        def ph_l0f_attn(b):
            for h in range(NH):
                dc = h // 2
                rb = (h % 2) * 64
                qu = APx(M2[:], 0, [[16384, 64], [1, 2048]])
                qv = APx(M2[:], 2048, [[16384, 64], [1, 2048]])
                kh = APx(M2[:], 4096, [[16384, 64], [1, 2048]])
                rh = APx(M2[:], 6144, [[16384, 64], [1, 2048]])
                vh = M2[:, 8192:9216]
                qu2 = APx(M2[:], 13312, [[16384, 64], [1, 2048]])
                dma("sp", qu, sc["l0f_qT"][b * D + h * 64:b * D + h * 64 + 64, :])
                dma("sp", kh, sc["l0f_kT"][b * D + h * 64:b * D + h * 64 + 64, :])
                dma("sp", rh, sc["rT"][h * 64:h * 64 + 64, :])
                vsrc = APx(sc["l0f_v"], (b * S) * D + h * 64,
                           [[D, 128], [128 * D, 16], [1, 64]])
                dma("sp", vh.rearrange("p (c n) -> p c n", n=64), vsrc)
                v_ts(qv, qu, vbt[:, h:h + 1], None, ALU.add)
                v_ts(qu2, qu, ubt[:, h:h + 1], None, ALU.add)

                def stage1a(qt, par):
                    kw = (qt + 1) * 128
                    nk4 = cdiv(kw, 512)
                    c0 = 2048 - kw
                    bdbuf = sc["bd0"] if (h * NST + qt) % 2 == 0 else sc["bd1"]
                    bst = 18432 + par * 2048
                    bsh = 12288 + par * 2048
                    for i4 in range(nk4):
                        cw = min(512, kw - i4 * 512)
                        cs = c0 + i4 * 512
                        pp = 2 + i4 % 2
                        mm(PS[pp][:, :cw],
                           APx(M2[:], 2048 + qt * 128, [[16384, 64], [1, 128]]),
                           APx(M2[:], 6144 + cs, [[16384, 64], [1, cw]]),
                           True, True)
                        a_copy(APx(W[:], bst + cs - c0, [[32768, 128], [1, cw]]),
                               PS[pp][:, :cw])
                    dma("act", APx(bdbuf, c0, [[RS, 128], [1, kw]]),
                        APx(W[:], bst, [[32768, 128], [1, kw]]))
                    dma("sp", APx(W[:], bsh, [[32768, 128], [1, kw]]),
                        APx(bdbuf, 2047 - qt * 128, [[RS - 1, 128], [1, kw]]))

                def stage1b(qt, par):
                    sm = 16 * par
                    sbuf_sc = (F1, F2, CN)[par]
                    kw = (qt + 1) * 128
                    nk4 = cdiv(kw, 512)
                    bsh = 12288 + par * 2048
                    for i4 in range(nk4):
                        cw = min(512, kw - i4 * 512)
                        pp = i4 % 2
                        mm(PS[pp][:, :cw],
                           APx(M2[:], 13312 + qt * 128, [[16384, 64], [1, 128]]),
                           APx(M2[:], 4096 + i4 * 512, [[16384, 64], [1, cw]]),
                           True, True)
                        v_tt(sbuf_sc[:, i4 * 512:i4 * 512 + cw], PS[pp][:, :cw],
                             APx(W[:], bsh + i4 * 512, [[32768, 128], [1, cw]]),
                             ALU.add)
                    v_copy(F2[:, 2560:2688], sbuf_sc[:, kw - 128:kw])
                    v_sel(sbuf_sc[:, kw - 128:kw], cmask[:], F2[:, 2560:2688],
                          negT[:])
                    v_red(SM[:, sm + 0:sm + 1], sbuf_sc[:, 0:kw], ALU.max)
                    v_ts(SM[:, sm + 1:sm + 2], SM[:, sm + 0:sm + 1], -0.125,
                         None, ALU.mult)

                def stage2(qt, par):
                    sm = 16 * par
                    sbuf_sc = (F1, F2, CN)[par]
                    kw = (qt + 1) * 128
                    prb0 = par * 2048
                    ptb0 = 6144 + par * 2048
                    a_act(APx(W[:], prb0, [[32768, 128], [1, kw]]),
                          sbuf_sc[:, 0:kw], AF.Exp, bias=SM[:, sm + 1:sm + 2],
                          scale=0.125, accum=SM[:, sm + 2:sm + 3])
                    v_recip(SM[:, sm + 3:sm + 4], SM[:, sm + 2:sm + 3])
                    v_ts(DG[:, par * 128:(par + 1) * 128], ident_bf[:],
                         SM[:, sm + 3:sm + 4], None, ALU.mult)
                    nkc = qt + 1
                    for kc in range(nkc):
                        pp = 4 + (kc // 4) % 2
                        mm(PS[pp][:, (kc % 4) * 128:(kc % 4) * 128 + 128],
                           APx(W[:], prb0 + kc * 128, [[32768, 128], [1, 128]]),
                           DG[:, par * 128:(par + 1) * 128], True, True)
                        if kc % 4 == 3 or kc == nkc - 1:
                            g0 = (kc // 4) * 4
                            gn = kc - g0 + 1
                            a_copy(APx(W[:], ptb0 + g0 * 128,
                                       [[32768, 128], [1, gn * 128]]),
                                   PS[pp][:, :gn * 128])
                    for kc in range(nkc):
                        mm(PS[6][0:64, :128], vh[:, kc * 64:(kc + 1) * 64],
                           APx(W[:], ptb0 + kc * 128, [[32768, 128], [1, 128]]),
                           kc == 0, kc == qt)
                    v_copy(M1[rb:rb + 64,
                              dc * 2048 + qt * 128:dc * 2048 + qt * 128 + 128],
                           PS[6][0:64, :128])

                stage1a(0, 0)
                stage1a(1, 1)
                for qt in range(NST):
                    if qt + 2 < NST:
                        stage1a(qt + 2, (qt + 2) % 3)
                    stage1b(qt, qt % 3)
                    if qt >= 1:
                        stage2(qt - 1, (qt - 1) % 3)
                stage2(NST - 1, (NST - 1) % 3)

        def ph_l0f_h1(b):
            if "ctx0T" in dumps:
                for dc in range(8):
                    dma("g", sc["ctx0T"][b * D + dc * 128:b * D + dc * 128 + 128, :],
                        M1[:, dc * 2048:(dc + 1) * 2048])
            load_w_bf(W, 0, w["l0f_wo"], D, D)
            load_lgb("l0f_ln1g", "l0f_ln1b")
            for st in range(NST):
                r0 = b * S + st * 128
                for nh2 in range(2):
                    pp = nh2
                    for dc in range(8):
                        mm(PS[pp][:, :512],
                           M1[:, dc * 2048 + st * 128:dc * 2048 + st * 128 + 128],
                           W[:, dc * 1024 + nh2 * 512:dc * 1024 + nh2 * 512 + 512],
                           dc == 0, dc == 7)
                dma("sp", F2[:, 0:1024], sc["xn_tm"][r0:r0 + 128, :])
                for nh2 in range(2):
                    v_tt(F3[:, nh2 * 512:(nh2 + 1) * 512], PS[nh2][:, :512],
                         F2[:, nh2 * 512:(nh2 + 1) * 512], ALU.add)
                ln_tile(F3[:, 0:1024], F3[:, 1024:2048], 0, affine=True)
                dma("g", sc["h1_tm"][r0:r0 + 128, :], F3[:, 1024:2048])
                a_act(Bq[:, 0:1024], F3[:, 1024:2048], AF.Identity)
                t8_store(Bq[:, 0:1024], sc["h1T"], b * D, st * 128)

        def ph_l0f_ffn(b):
            for dc in range(8):
                dma("g", W[:, dc * 4096:(dc + 1) * 4096],
                    w["l0f_wff1"][dc * 128:(dc + 1) * 128, :])
            for fc in range(32):
                arena = M1 if fc < 16 else M2
                base = (fc % 16) * 1024
                dma("g", arena[:, base:base + 1024],
                    w["l0f_wff2"][fc * 128:(fc + 1) * 128, :])
            dma("g", BR[:], w["l0f_bff2"])
            load_lgb("l0f_ln2g", "l0f_ln2b")
            for st in range(NST):
                r0 = b * S + st * 128
                p2 = st % 2
                hsrc = APx(sc["h1T"], (b * D) * S + st * 128,
                           [[S, 128], [128 * S, 8], [1, 128]])
                dma("sp", APx(HB2[:], p2 * 1024,
                              [[2048, 128], [128, 8], [1, 128]]), hsrc)
                for fc in range(32):
                    pp = fc % 2
                    for dc in range(8):
                        mm(PS[pp][:, :128],
                           W[:, dc * 4096 + fc * 128:dc * 4096 + fc * 128 + 128],
                           HB2[:, p2 * 1024 + dc * 128:p2 * 1024 + (dc + 1) * 128],
                           dc == 0, dc == 7)
                    a_act(HB[:, fc * 128:(fc + 1) * 128], PS[pp][:, :128],
                          AF.Relu, bias=bff1_sb[:, fc:fc + 1])
                for nh2 in range(2):
                    pp = 2 + nh2
                    mm(PS[pp][:, :512], ones_bf[:],
                       BR[:, nh2 * 512:(nh2 + 1) * 512], True, False)
                    for fc in range(32):
                        arena = M1 if fc < 16 else M2
                        base = (fc % 16) * 1024
                        mm(PS[pp][:, :512], HB[:, fc * 128:(fc + 1) * 128],
                           arena[:, base + nh2 * 512:base + nh2 * 512 + 512],
                           False, fc == 31)
                dma("sp", F2[:, 0:1024], sc["h1_tm"][r0:r0 + 128, :])
                for nh2 in range(2):
                    v_tt(F3[:, nh2 * 512:(nh2 + 1) * 512], PS[2 + nh2][:, :512],
                         F2[:, nh2 * 512:(nh2 + 1) * 512], ALU.add)
                ln_tile(F3[:, 0:1024], F3[:, 1024:2048], 0, affine=True)
                dma("g", sc["h2_tm"][r0:r0 + 128, :], F3[:, 1024:2048])
                a_act(Bq[:, 0:1024], F3[:, 1024:2048], AF.Identity)
                t8_store(Bq[:, 0:1024], sc["h2T"], b * D, st * 128)

        def ph_l0f_gate(b):
            for dc in range(8):
                dma("g", WGS[:, dc * 8:(dc + 1) * 8],
                    w["l0f_wg"][dc * 128:(dc + 1) * 128, :])
            for st in range(NST):
                r0 = b * S + st * 128
                p2 = st % 2
                hsrc = APx(sc["h2T"], (b * D) * S + st * 128,
                           [[S, 128], [128 * S, 8], [1, 128]])
                dma("sp", APx(HB2[:], p2 * 1024,
                              [[2048, 128], [128, 8], [1, 128]]), hsrc)
                for dc in range(8):
                    mm(PS[0][:, :8],
                       HB2[:, p2 * 1024 + dc * 128:p2 * 1024 + (dc + 1) * 128],
                       WGS[:, dc * 8:(dc + 1) * 8], dc == 0, dc == 7)
                v_copy(SM3[:, 0:8], PS[0][:, :8])
                v_red(SM3[:, 8:9], SM3[:, 0:8], ALU.max)
                v_ts(SM3[:, 9:10], SM3[:, 8:9], -1.0, None, ALU.mult)
                a_act(SM3[:, 16:24], SM3[:, 0:8], AF.Exp, bias=SM3[:, 9:10],
                      accum=SM3[:, 10:11])
                v_recip(SM3[:, 11:12], SM3[:, 10:11])
                v_ts(SM3[:, 24:32], SM3[:, 16:24], SM3[:, 11:12], None, ALU.mult)
                v_red(SM3[:, 12:13], SM3[:, 24:32], ALU.max)
                v_ts(SM3[:, 32:40], SM3[:, 24:32], SM3[:, 12:13], None, ALU.is_ge)
                v_copy(SM3[:, 13:14], zrow[:])
                for ee in range(NE):
                    ra = 13 + (ee % 2)
                    rb2 = 13 + ((ee + 1) % 2)
                    v_ts(SM3[:, 15:16], SM3[:, ra:ra + 1], -1.0, 1.0, ALU.mult,
                         ALU.add)
                    v_tt(SM3[:, 40 + ee:41 + ee], SM3[:, 32 + ee:33 + ee],
                         SM3[:, 15:16], ALU.mult)
                    v_tt(SM3[:, rb2:rb2 + 1], SM3[:, ra:ra + 1],
                         SM3[:, 40 + ee:41 + ee], ALU.add)
                v_ts(MME[:, 0:8], SM3[:, 40:48], SM3[:, 12:13], None, ALU.mult)
                dma("g", sc["m_tm"][r0:r0 + 128, :], MME[:, 0:8])

        if not stopped():
            for b in range(BC):
                ph_l0f_attn(b)
                ph_l0f_h1(b)
        phase_done("l0f_h1")
        if not stopped():
            for b in range(BC):
                ph_l0f_ffn(b)
                ph_l0f_gate(b)
        phase_done("l0f_h2")

        def ph_moe():
            for e_ in range(NE):
                for dc in range(8):
                    dma("g", W[:, dc * 4096:(dc + 1) * 4096],
                        w["l0f_we1"][e_, dc * 128:(dc + 1) * 128, :])
                for fc in range(32):
                    arena = M1 if fc < 16 else M2
                    base = (fc % 16) * 1024
                    dma("g", arena[:, base:base + 1024],
                        w["l0f_we2"][e_, fc * 128:(fc + 1) * 128, :])
                dma("sp", be1_sb[:], w["l0f_be1"][e_])
                dma("g", BR[:], w["l0f_be2"][e_])
                for b in range(BC):
                    for st in range(NST):
                        r0 = b * S + st * 128
                        p2 = st % 2
                        hb2 = HB2[:, p2 * 1024:(p2 + 1) * 1024]
                        mme = MME[:, p2 * 8:(p2 + 1) * 8]
                        hsrc = APx(sc["h2T"], (b * D) * S + st * 128,
                                   [[S, 128], [128 * S, 8], [1, 128]])
                        dma("sp", APx(HB2[:], p2 * 1024,
                                      [[2048, 128], [128, 8], [1, 128]]), hsrc)
                        dma("sp", mme, sc["m_tm"][r0:r0 + 128, :])
                        for fc in range(32):
                            pp = fc % 2
                            for dc in range(8):
                                mm(PS[pp][:, :128],
                                   W[:, dc * 4096 + fc * 128:
                                     dc * 4096 + fc * 128 + 128],
                                   hb2[:, dc * 128:(dc + 1) * 128],
                                   dc == 0, dc == 7)
                            a_act(HB[:, fc * 128:(fc + 1) * 128], PS[pp][:, :128],
                                  AF.Relu, bias=be1_sb[:, fc:fc + 1])
                        for nh2 in range(2):
                            mm(PS[2 + nh2][:, :512], ones_bf[:],
                               BR[:, nh2 * 512:(nh2 + 1) * 512], True, False)
                        for fc in range(32):
                            arena = M1 if fc < 16 else M2
                            base = (fc % 16) * 1024
                            for nh2 in range(2):
                                mm(PS[2 + nh2][:, :512],
                                   HB[:, fc * 128:(fc + 1) * 128],
                                   arena[:, base + nh2 * 512:
                                         base + nh2 * 512 + 512],
                                   False, fc == 31)
                        for nh2 in range(2):
                            v_ts(F2[:, p2 * 1024 + nh2 * 512:
                                    p2 * 1024 + (nh2 + 1) * 512],
                                 PS[2 + nh2][:, :512], mme[:, e_:e_ + 1], None,
                                 ALU.mult)
                        if e_ == 0:
                            dma("g", sc["moe_tm"][r0:r0 + 128, :],
                                F2[:, p2 * 1024:(p2 + 1) * 1024])
                        else:
                            dma_acc(sc["moe_tm"][r0:r0 + 128, :],
                                    F2[:, p2 * 1024:(p2 + 1) * 1024])
                        if st % 4 == 3 and WQ_TASKS:
                            emit_wq_tile(*WQ_TASKS.pop())

        if not stopped():
            ph_moe()
            while WQ_TASKS:
                emit_wq_tile(*WQ_TASKS.pop())
        phase_done("moe")

        def ph_l0_final(b):
            load_w_bf(W, 0, w["l0f_wout"], D, D)
            load_lgb("l0f_ln3g", "l0f_ln3b")
            for st in range(NST):
                r0 = b * S + st * 128
                dma("sp", F2[:, 0:1024], sc["h2_tm"][r0:r0 + 128, :])
                dma("sp", F1[:, 0:1024], sc["moe_tm"][r0:r0 + 128, :])
                v_tt(F3[:, 0:1024], F2[:, 0:1024], F1[:, 0:1024], ALU.add)
                ln_tile(F3[:, 0:1024], F3[:, 1024:2048], 0, affine=True)
                a_act(Bq[:, 0:1024], F3[:, 1024:2048], AF.Identity)
                for dc in range(8):
                    trans(PB[0][:, :128], Bq[:, dc * 128:(dc + 1) * 128],
                          ident_bf[:])
                    v_copy(HB[:, dc * 128:(dc + 1) * 128], PB[0][:, :128])
                for nh2 in range(2):
                    pp = nh2
                    for dc in range(8):
                        mm(PS[pp][:, :512], HB[:, dc * 128:(dc + 1) * 128],
                           W[:, dc * 1024 + nh2 * 512:dc * 1024 + nh2 * 512 + 512],
                           dc == 0, dc == 7)
                    a_act(F2[:, 1024 + nh2 * 512:1024 + (nh2 + 1) * 512],
                          PS[pp][:, :512], AF.Silu)
                dma("sp", F1[:, 1024:2048], sc["s_tm"][r0:r0 + 128, :])
                dma("sp", CN[:, 0:1024], w["x"][r0:r0 + 128, :])
                v_tt(F2[:, 0:1024], F1[:, 1024:2048], F2[:, 1024:2048], ALU.mult)
                v_tt(F3[:, 0:1024], CN[:, 0:1024], F2[:, 0:1024], ALU.add)
                dma("g", sc["x1_tm"][r0:r0 + 128, :], F3[:, 0:1024])

        if not stopped():
            for b in range(BC):
                ph_l0_final(b)
        phase_done("l0")

        if not stopped():
            for b in range(BC):
                ph_ln_in(b, sc["x1_tm"], sc["xn1_tm"], sc["xn1T"])
        phase_done("ln1")

        def ph_l1_proj(b):
            load_xnT_M1(b, sc["xn1T"])
            load_w_bf(W, 0, w["l1a_wq"], D, D)
            load_w_bf(W, 8192, w["l1a_wk"], D, D)
            load_w_bf(W, 16384, w["l1a_wv"], D, D)
            for wbase, dstd, dstn in ((0, sc["l1_qT"], sc["l1_qn"]),
                                      (8192, sc["l1_kT"], sc["l1_kn"])):
                proj_rope(b, wbase, w["c_cos1"], w["c_sin1"], dstd, qnpsum=True)
                for s4 in range(4):
                    v_copy(F1[0:16, s4 * 512:(s4 + 1) * 512],
                           PS[2 + s4][0:16, :512])
                dma("g", dstn[b * NH:(b + 1) * NH, :], F1[0:16, 0:2048])
                if wbase == 0:
                    for qt in range(NST):
                        pp = qt % 2
                        trans(PS[pp][:, :16], F1[0:16, qt * 128:(qt + 1) * 128],
                              ident_f[0:16, 0:16])
                        v_copy(QNS[:], PS[pp][:, :16])
                        dma("g", sc["l1_qnT"][b * S + qt * 128:
                                              b * S + qt * 128 + 128, :], QNS[:])
            proj_v(b, lambda dc, nh2: W[:, 16384 + dc * 1024 + nh2 * 512:
                                        16384 + dc * 1024 + nh2 * 512 + 512],
                   sc["l1_v"])

        def ph_l1_attn(b):
            for h in range(DEBUG_L1_HEADS):
                dc = h // 2
                rb = (h % 2) * 64
                qh = APx(M2[:], 0, [[16384, 64], [1, 2048]])
                kh = APx(M2[:], 2048, [[16384, 64], [1, 2048]])
                vh = M2[:, 4096:5120]
                q2h = APx(M2[:], 5120, [[16384, 64], [1, 2048]])
                knr = APx(M2[:], 7168, [[16384, 1], [1, 2048]])
                qnr = APx(M2[:], 9216, [[16384, 1], [1, 2048]])
                dma("sp", qh, sc["l1_qT"][b * D + h * 64:b * D + h * 64 + 64, :])
                dma("sp", kh, sc["l1_kT"][b * D + h * 64:b * D + h * 64 + 64, :])
                vsrc = APx(sc["l1_v"], (b * S) * D + h * 64,
                           [[D, 128], [128 * D, 16], [1, 64]])
                dma("sp", vh.rearrange("p (c n) -> p c n", n=64), vsrc)
                v_ts(q2h, qh, -2.0, None, ALU.mult)
                dma("sp", F1[0:1, 0:2048],
                    sc["l1_kn"][b * NH + h:b * NH + h + 1, :])
                dma("sp", CN[0:1, 0:2048],
                    sc["l1_qn"][b * NH + h:b * NH + h + 1, :])
                v_ts(knr, F1[0:1, 0:2048], 1e-5, None, ALU.add)
                v_copy(qnr, CN[0:1, 0:2048])

                def stage1(qt, par):
                    sm = 32 * par
                    sbuf_sc = F1 if par == 0 else F3
                    kw = (qt + 1) * 128
                    nk4 = cdiv(kw, 512)
                    for i4 in range(nk4):
                        cw = min(512, kw - i4 * 512)
                        pp = i4 % 2
                        mm(PS[pp][:, :cw],
                           APx(M2[:], 0 + qt * 128, [[16384, 64], [1, 128]]),
                           APx(M2[:], 2048 + i4 * 512, [[16384, 64], [1, cw]]),
                           True, True)
                        mm(PS[2 + pp][:, :cw],
                           APx(M2[:], 5120 + qt * 128, [[16384, 64], [1, 128]]),
                           APx(M2[:], 2048 + i4 * 512, [[16384, 64], [1, cw]]),
                           True, False)
                        mm(PS[2 + pp][:, :cw], ones_bf[:],
                           APx(M2[:], 7168 + i4 * 512, [[16384, 1], [1, cw]]),
                           False, False)
                        mm(PS[2 + pp][:, :cw],
                           APx(M2[:], 9216 + qt * 128, [[16384, 1], [1, 128]]),
                           ones2_bf[:, 0:cw], False, True)
                        a_act(F2[:, i4 * 512:i4 * 512 + cw], PS[pp][:, :cw],
                              AF.Square)
                        if i4 % 3 == 2:
                            v_copy(F3[:, 2304:2304 + cw], PS[2 + pp][:, :cw])
                            v_recip(CN[:, i4 * 512:i4 * 512 + cw],
                                    F3[:, 2304:2304 + cw])
                        else:
                            a_act(F3[:, 2304:2304 + cw], PS[2 + pp][:, :cw],
                                  AF.Ln)
                            a_act(CN[:, i4 * 512:i4 * 512 + cw],
                                  F3[:, 2304:2304 + cw], AF.Exp, scale=-1.0)
                        v_stt(sbuf_sc[:, i4 * 512:i4 * 512 + cw],
                              F2[:, i4 * 512:i4 * 512 + cw], alpha_sc[:],
                              CN[:, i4 * 512:i4 * 512 + cw], ALU.mult,
                              ALU.mult)
                    v_copy(F2[:, 2560:2688], sbuf_sc[:, kw - 128:kw])
                    v_sel(sbuf_sc[:, kw - 128:kw], cmask[:], F2[:, 2560:2688],
                          negT[:])
                    v_red(SM[:, sm + 0:sm + 1], sbuf_sc[:, 0:kw], ALU.max)
                    v_ts(SM[:, sm + 1:sm + 2], SM[:, sm + 0:sm + 1], -1.0,
                         None, ALU.mult)

                def stage2(qt, par):
                    sm = 32 * par
                    sbuf_sc = F1 if par == 0 else F3
                    kw = (qt + 1) * 128
                    prb0 = par * 2048
                    ptb0 = 4096 + par * 2048
                    a_act(APx(W[:], prb0, [[32768, 128], [1, kw]]),
                          sbuf_sc[:, 0:kw], AF.Exp, bias=SM[:, sm + 1:sm + 2],
                          accum=SM[:, sm + 2:sm + 3])
                    v_recip(SM[:, sm + 3:sm + 4], SM[:, sm + 2:sm + 3])
                    v_ts(DG[:, par * 128:(par + 1) * 128], ident_bf[:],
                         SM[:, sm + 3:sm + 4], None, ALU.mult)
                    nkc = qt + 1
                    for kc in range(nkc):
                        pp = 4 + (kc // 4) % 2
                        mm(PS[pp][:, (kc % 4) * 128:(kc % 4) * 128 + 128],
                           APx(W[:], prb0 + kc * 128, [[32768, 128], [1, 128]]),
                           DG[:, par * 128:(par + 1) * 128], True, True)
                        if kc % 4 == 3 or kc == nkc - 1:
                            g0 = (kc // 4) * 4
                            gn = kc - g0 + 1
                            dstp = APx(W[:], ptb0 + g0 * 128,
                                       [[32768, 128], [1, gn * 128]])
                            if (kc // 4) % 2 == 0:
                                v_copy(dstp, PS[pp][:, :gn * 128])
                            else:
                                a_copy(dstp, PS[pp][:, :gn * 128])
                    for kc in range(qt + 1):
                        mm(PS[6][0:64, :128], vh[:, kc * 64:(kc + 1) * 64],
                           APx(W[:], ptb0 + kc * 128, [[32768, 128], [1, 128]]),
                           kc == 0, kc == qt)
                    a_copy(M1[rb:rb + 64,
                              dc * 2048 + qt * 128:dc * 2048 + qt * 128 + 128],
                           PS[6][0:64, :128])

                for qt in range(NST):
                    stage1(qt, qt % 2)
                    if qt >= 1:
                        stage2(qt - 1, (qt - 1) % 2)
                stage2(NST - 1, (NST - 1) % 2)

        def ph_l1_out(b):
            if "ctx1T" in dumps:
                for dc in range(8):
                    dma("g", sc["ctx1T"][b * D + dc * 128:b * D + dc * 128 + 128, :],
                        M1[:, dc * 2048:(dc + 1) * 2048])
            load_w_bf(W, 0, w["l1a_wo"], D, D)
            for st in range(NST):
                r0 = b * S + st * 128
                for nh2 in range(2):
                    pp = nh2
                    for dc in range(8):
                        mm(PS[pp][:, :512],
                           M1[:, dc * 2048 + st * 128:dc * 2048 + st * 128 + 128],
                           W[:, dc * 1024 + nh2 * 512:dc * 1024 + nh2 * 512 + 512],
                           dc == 0, dc == 7)
                    v_copy(CN[:, nh2 * 512:(nh2 + 1) * 512], PS[pp][:, :512])
                dma("act", sc["s_tm"][r0:r0 + 128, :], CN[:, 0:1024])

        if not stopped():
            for b in range(BC):
                ph_l1_proj(b)
        phase_done("l1_proj")
        if not stopped():
            for b in range(BC):
                ph_l1_attn(b)
                ph_l1_out(b)
        phase_done("l1a")

        def ph_l1f_x(b):
            for st in range(NST):
                r0 = b * S + st * 128
                dma("sp", F3[:, 1408:2432], sc["xn1_tm"][r0:r0 + 128, :])
                quant_fp4(F3[:, 1408:2432], Bq[:, 0:1024], 1024)
                t8_store(Bq[:, 0:1024], sc["xqT"], b * D, st * 128)

        def ph_l1f_h(b):
            for dcc in range(8):
                dma("sp", W[:, dcc * 2816:(dcc + 1) * 2816],
                    sc["wgate_qT"][dcc * 128:(dcc + 1) * 128, :])
                if dcc < 5:
                    dma("sp", M1[:, dcc * 2816:(dcc + 1) * 2816],
                        sc["wup_qT"][dcc * 128:(dcc + 1) * 128, :])
                else:
                    dma("sp", M2[:, (dcc - 5) * 2816:(dcc - 4) * 2816],
                        sc["wup_qT"][dcc * 128:(dcc + 1) * 128, :])

            def wup_ap(dcc, c0, cw):
                if dcc < 5:
                    return M1[:, dcc * 2816 + c0:dcc * 2816 + c0 + cw]
                return M2[:, (dcc - 5) * 2816 + c0:(dcc - 5) * 2816 + c0 + cw]

            for st in range(NST):
                r0 = b * S + st * 128
                p2 = st % 2
                hsrc = APx(sc["xqT"], (b * D) * S + st * 128,
                           [[S, 128], [128 * S, 8], [1, 128]])
                dma("sp", APx(HB2[:], p2 * 1024,
                              [[2048, 128], [128, 8], [1, 128]]), hsrc)
                for ncc in range(6):
                    cw = min(512, IFF - ncc * 512)
                    for dcc in range(8):
                        mm(PS[0][:, :cw],
                           HB2[:, p2 * 1024 + dcc * 128:p2 * 1024 + (dcc + 1) * 128],
                           W[:, dcc * 2816 + ncc * 512:
                             dcc * 2816 + ncc * 512 + cw], dcc == 0, dcc == 7)
                    for dcc in range(8):
                        mm(PS[1][:, :cw],
                           HB2[:, p2 * 1024 + dcc * 128:p2 * 1024 + (dcc + 1) * 128],
                           wup_ap(dcc, ncc * 512, cw), dcc == 0, dcc == 7)
                    a_act(CN[:, 0:cw], PS[0][:, :cw], AF.Silu)
                    v_tt(F3[:, ncc * 512:ncc * 512 + cw], CN[:, 0:cw],
                         PS[1][:, :cw], ALU.mult)
                quant_fp4(F3[:, 0:1408], Bq[:, 0:1408], 1408)
                quant_fp4(F3[:, 1408:2816], Bq[:, 1408:2816], 1408)
                for g2 in range(3):
                    n8 = min(8, 22 - g2 * 8)
                    for dc2 in range(n8):
                        di = g2 * 8 + dc2
                        trans(PB[0][:, :128], Bq[:, di * 128:(di + 1) * 128],
                              ident_bf[:])
                        v_copy(PTb[:, dc2 * 128:(dc2 + 1) * 128], PB[0][:, :128])
                    dsth = APx(sc["hqT"], (b * IFF + g2 * 8 * 128) * S + st * 128,
                               [[S, 128], [128 * S, n8], [1, 128]])
                    dma("act", dsth,
                        PTb[:, 0:n8 * 128].rearrange("p (c n) -> p c n", n=128))

        def ph_l1f_down(b):
            for dcc in range(22):
                dma("sp", W[:, dcc * 1024:(dcc + 1) * 1024],
                    sc["wdown_qT"][dcc * 128:(dcc + 1) * 128, :])
            for st in range(NST):
                r0 = b * S + st * 128
                hsrc = APx(sc["hqT"], (b * IFF) * S + st * 128,
                           [[S, 128], [128 * S, 22], [1, 128]])
                dma("sp", HB[:, 0:22 * 128].rearrange("p (c n) -> p c n", n=128),
                    hsrc)
                for nh2 in range(2):
                    pp = nh2
                    for dcc in range(22):
                        mm(PS[pp][:, :512], HB[:, dcc * 128:(dcc + 1) * 128],
                           W[:, dcc * 1024 + nh2 * 512:
                             dcc * 1024 + nh2 * 512 + 512],
                           dcc == 0, dcc == 21)
                    a_act(F2[:, 1024 + nh2 * 512:1024 + (nh2 + 1) * 512],
                          PS[pp][:, :512], AF.Silu)
                dma("sp", F1[:, 0:1024], sc["s_tm"][r0:r0 + 128, :])
                dma("sp", F3[:, 0:1024], sc["x1_tm"][r0:r0 + 128, :])
                v_tt(F2[:, 0:1024], F1[:, 0:1024], F2[:, 1024:2048], ALU.mult)
                v_tt(F3[:, 1024:2048], F3[:, 0:1024], F2[:, 0:1024], ALU.add)
                o = out[r0:r0 + 128, :]
                dma("g", o, F3[:, 1024:2048])
                ALL_OUT.append(o)

        if not stopped():
            for b in range(BC):
                ph_l1f_x(b)
        phase_done("l1f_x")
        if not stopped():
            for b in range(BC):
                ph_l1f_h(b)
                ph_l1f_down(b)
        phase_done("l1f")

        sch.add("g", lambda e: e.memset(zrow[:], 0.0), ins=ALL_OUT,
                outs=[zrow[:]], tag="fin")

        sch.finalize(DMA_SLOTS)

        @blk.tensor
        def _(t):
            sch.emit("pe", t, sems)

        @blk.vector
        def _(v):
            sch.emit("dve", v, sems)

        @blk.scalar
        def _(a):
            sch.emit("act", a, sems)

        @blk.sync
        def _(s):
            sch.emit("sp", s, sems)

        @blk.gpsimd
        def _(g):
            sch.emit("g", g, sems)

    return nc


def _host_consts():
    c = {}
    c["c_ident"] = np.eye(128, dtype=np.float32)
    A = np.zeros((128, 128), dtype=np.float32)
    for i in range(64):
        A[2 * i, 2 * i + 1] = -1.0
        A[2 * i + 1, 2 * i] = 1.0
    c["c_rotA"] = np.ascontiguousarray(A.T)
    inv0 = 1.0 / (10000.0 ** (np.arange(0, D, 2, dtype=np.float64) / D))
    t = np.arange(S, dtype=np.float64)
    ang0 = np.outer(inv0, t)
    c["c_cos0"] = np.repeat(np.cos(ang0), 2, axis=0).astype(np.float32)
    c["c_sin0"] = np.repeat(np.sin(ang0), 2, axis=0).astype(np.float32)
    inv1 = 1.0 / (10000.0 ** (np.arange(0, DH, 2, dtype=np.float64) / DH))
    ang1 = np.outer(inv1, t)
    c["c_cos1"] = np.tile(np.repeat(np.cos(ang1), 2, axis=0), (NH, 1)).astype(
        np.float32)
    c["c_sin1"] = np.tile(np.repeat(np.sin(ang1), 2, axis=0), (NH, 1)).astype(
        np.float32)
    pos = (S - 1.0) - np.arange(S, dtype=np.float64)
    angp = np.outer(inv0, pos)
    c["c_pet"] = np.concatenate([np.sin(angp), np.cos(angp)], axis=0).astype(
        np.float32)
    hmv = np.zeros((128, 8 * NH), dtype=np.float32)
    for dc in range(8):
        hmv[0:64, dc * NH + 2 * dc] = 1.0
        hmv[64:128, dc * NH + 2 * dc + 1] = 1.0
    c["c_hm"] = hmv
    c["c_cmask"] = np.tril(np.ones((128, 128), dtype=np.uint8))
    c["c_ones"] = np.ones((1, 128), dtype=np.float32)
    c["c_onesf"] = np.ones((1, 128), dtype=np.float32)
    c["c_ones2"] = np.ones((1, 512), dtype=np.float32)
    return c


def _host_weights(inp):
    f32 = np.float32
    a = lambda x: np.ascontiguousarray(np.asarray(x, dtype=f32))
    w = {}
    w["l0a_wq_t"] = a(np.asarray(inp["L0_attn_wq"]).T)
    w["l0a_wk_t"] = a(np.asarray(inp["L0_attn_wk"]).T)
    w["l0a_wv_t"] = a(np.asarray(inp["L0_attn_wv"]).T)
    w["l0a_wo_t"] = a(np.asarray(inp["L0_attn_wo"]).T)
    w["l0f_wqkv"] = a(inp["L0_ffn_Wqkv"])
    w["l0f_wo"] = a(inp["L0_ffn_Wo"])
    w["l0f_wr"] = a(inp["L0_ffn_Wr"])
    w["l0f_ubt"] = a(np.asarray(inp["L0_ffn_u_bias"]).T)
    w["l0f_vbt"] = a(np.asarray(inp["L0_ffn_v_bias"]).T)
    for nm in ("ln1", "ln2", "ln3"):
        w[f"l0f_{nm}g"] = a(inp[f"L0_ffn_{nm}_g"])
        w[f"l0f_{nm}b"] = a(inp[f"L0_ffn_{nm}_b"])
    w["l0f_wff1"] = a(inp["L0_ffn_Wff1"])
    w["l0f_bff1"] = a(np.asarray(inp["L0_ffn_bff1"]).reshape(FF // 128, 128).T)
    w["l0f_wff2"] = a(inp["L0_ffn_Wff2"])
    w["l0f_bff2"] = a(np.asarray(inp["L0_ffn_bff2"]).reshape(1, D))
    w["l0f_wg"] = a(inp["L0_ffn_Wg"])
    w["l0f_we1"] = a(inp["L0_ffn_We1"])
    w["l0f_be1"] = a(np.asarray(inp["L0_ffn_be1"]).reshape(NE, FF // 128, 128)
                     .transpose(0, 2, 1))
    w["l0f_we2"] = a(inp["L0_ffn_We2"])
    w["l0f_be2"] = a(np.asarray(inp["L0_ffn_be2"]).reshape(NE, 1, D))
    w["l0f_wout"] = a(inp["L0_ffn_Wout"])
    w["l1a_wq"] = a(inp["L1_attn_Wq"])
    w["l1a_wk"] = a(inp["L1_attn_Wk"])
    w["l1a_wv"] = a(inp["L1_attn_Wv"])
    w["l1a_wo"] = a(inp["L1_attn_Wo"])
    w["l1a_alpha"] = a(np.asarray(inp["L1_attn_alpha"]).reshape(1, 1))
    w["l1f_wgate"] = a(inp["L1_ffn_w_gate"])
    w["l1f_wup"] = a(inp["L1_ffn_w_up"])
    w["l1f_wdown"] = a(inp["L1_ffn_w_down"])
    return w


def kernel(**inputs):
    nc = build_nc()
    consts = _host_consts()
    weights = _host_weights(inputs)
    x = np.ascontiguousarray(np.asarray(inputs["L0_attn_x"], dtype=np.float32))
    in_maps = []
    for c in range(8):
        m = {"x": x[c * BC:(c + 1) * BC].reshape(BC * S, D)}
        m.update(weights)
        m.update(consts)
        in_maps.append(m)
    res = run_bass_kernel_spmd(nc, in_maps, core_ids=list(range(8)))
    outs = [np.asarray(r["out"]).reshape(BC, S, D) for r in res.results]
    return np.concatenate(outs, axis=0)
